# version 04
# baseline (speedup 1.0000x reference)
import numpy as np
from contextlib import ExitStack
import concourse.bass as bass
import concourse.mybir as mybir
from concourse.bass_utils import run_bass_kernel_spmd

F32 = mybir.dt.float32
BF16 = mybir.dt.bfloat16
I32 = mybir.dt.int32
U8 = mybir.dt.uint8
AF = mybir.ActivationFunctionType
ALU = mybir.AluOpType
AX = mybir.AxisListType

T = 2048
NT = 4
D = 1024
NF = D // 128

ENG_ATTR = {'te': 'tensor', 've': 'vector', 'se': 'scalar', 'po': 'gpsimd', 'sy': 'sync'}
DEBUG_DUMPS = False
FCONSTS = [1e-5, 1e-6, 1.1920929e-07, 1e-24, 1.0, 0.0, -1e30, -2.0794415416798357, 2.0]


class Prog:

    def __init__(self, nc, es):
        self.nc = nc
        self.es = es
        self.block = es.enter_context(nc.Block())
        self.sems = {}
        self.cnt = {}
        for s in ('te', 've', 'se', 'po', 'sq', 'gq', 'cq', 'w0', 'w1'):
            self.sems[s] = es.enter_context(nc.semaphore('sem_' + s))
            self.cnt[s] = 0
        self.pend = {e: [] for e in ENG_ATTR}
        self.last_wait = {e: {} for e in ENG_ATTR}
        self.psum = es.enter_context(nc.psum_tensor("psum_all", [128, 4096], F32))
        self.bank_ready = [None] * 8

    def bank(self, i):
        return self.psum.ap()[:, i * 512:(i + 1) * 512]

    def bank_bf(self, i):
        return self.psum.ap().bitcast(BF16)[:, i * 1024:(i + 1) * 1024]

    def bank_span(self, i, n):
        return self.psum.ap()[:, i * 512:(i + n) * 512]

    def op(self, eng, fn, waits=(), tick=None, tick_n=1):
        wl = []
        for w in waits:
            s, v = w if isinstance(w, tuple) else (w, self.cnt[w])
            if v > 0 and self.last_wait[eng].get(s, -1) < v:
                wl.append((s, v))
                self.last_wait[eng][s] = v
        if tick is None:
            tick = eng if eng in ('te', 've', 'se', 'po') else None
        elif tick is False:
            tick = None
        self.pend[eng].append((wl, fn, tick, tick_n))
        if tick:
            self.cnt[tick] += tick_n
            return self.cnt[tick]
        return None

    def dma(self, queue, out, in_, waits=(), sem=None):
        qsem = sem or {'sy': 'sq', 'po': 'gq', 'se': 'cq'}[queue]
        return self.op(queue, (lambda e, o=out, i=in_: e.dma_start(
            out=o, in_=i, allow_slow_non_contiguous=True)),
                       waits=waits, tick=qsem, tick_n=16)

    def wait_only(self, eng, waits):
        self.op(eng, None, waits=waits, tick=False)

    def barrier(self, prods=('te', 've', 'se', 'po'), cons=('sy', 'po', 'se', 've', 'te')):
        w = [(s, self.cnt[s]) for s in prods]
        for e in cons:
            self.op(e, None, waits=w, tick=False)

    def dump(self, name, ap, dt=None):
        if not DEBUG_DUMPS or (isinstance(DEBUG_DUMPS, (set, list)) and name not in DEBUG_DUMPS):
            return
        shape = list(ap.shape)
        dram = self.nc.dram_tensor('dbg_' + name, shape, dt or ap.dtype,
                                   kind="ExternalOutput").ap()
        self.barrier()
        sl = tuple(slice(0, s) for s in shape)
        self.dma('po', dram[sl], ap)
        gq = self.cnt['gq']
        for e in ('sy', 'po', 'se', 've', 'te'):
            self.op(e, None, waits=[('gq', gq)], tick=False)

    def flush(self):
        for eng, lst in self.pend.items():
            if not lst:
                continue
            sems = self.sems

            def body(e, lst=tuple(lst), sems=sems):
                for wl, fn, tick, tick_n in lst:
                    for (s, v) in wl:
                        e.wait_ge(sems[s], v)
                    if fn is not None:
                        ins = fn(e)
                        if tick:
                            ins.then_inc(sems[tick], tick_n)
            getattr(self.block, ENG_ATTR[eng])(body)
            self.pend[eng] = []

    def bank_waits(self, i):
        r = self.bank_ready[i]
        return [r] if r is not None else []

    def bank_release(self, i, sem, val):
        self.bank_ready[i] = (sem, val)


import os
SB_DEBUG = bool(os.environ.get('KERNEL_SB_DEBUG'))


def sb(nc, es, name, shape, dt=F32):
    h = es.enter_context(nc.sbuf_tensor(name, shape, dt))
    if SB_DEBUG:
        nbytes = shape[1] * {F32: 4, BF16: 2, U8: 1, I32: 4}.get(dt, 4)
        print(f"SB {name}: {nbytes}B/part")
    return h


def col_layout(v, pad_to=None):
    v = np.asarray(v, np.float32).reshape(-1)
    n = v.shape[0]
    k = (n + 127) // 128 if pad_to is None else pad_to
    out = np.zeros((128, k), np.float32)
    for i in range(k):
        seg = v[i * 128:(i + 1) * 128]
        out[:len(seg), i] = seg
    return out


COLS = {}


def pack_cols(c, entries):
    arrs = []
    off = 0
    COLS.clear()
    for name, a in entries:
        a = np.asarray(a, np.float32)
        assert a.shape[0] == 128
        COLS[name] = (off, a.shape[1])
        arrs.append(a)
        off += a.shape[1]
    c['c_allcols'] = np.concatenate(arrs, axis=1)


def prep_host(w):
    import ml_dtypes
    c = {}
    c['c_ident_bf'] = np.eye(128).astype(ml_dtypes.bfloat16)
    c['c_ident_f32'] = np.eye(128, dtype=np.float32)
    c['c_ones_col_bf'] = np.ones((128, 1), ml_dtypes.bfloat16)
    c['c_ones_col_f32'] = np.ones((128, 1), np.float32)
    c['c_ones_row_f32'] = np.ones((1, 128), np.float32)
    c['c_ones128'] = np.ones((128, 128), np.float32)
    ii = np.arange(128)
    c['c_mask_keep_le'] = np.where(ii[:, None] > ii[None, :], -1e30, 0.0).astype(np.float32)
    c['c_mask_causal'] = np.where(ii[None, :] > ii[:, None], -1e30, 0.0).astype(np.float32)
    c['c_mask_keep_ge'] = np.where(ii[None, :] < ii[:, None], -1e30, 0.0).astype(np.float32)
    c['c_mask_causal_bf'] = c['c_mask_causal'].astype(ml_dtypes.bfloat16)
    c['c_mask_keep_ge_bf'] = c['c_mask_keep_ge'].astype(ml_dtypes.bfloat16)
    fc = np.array(FCONSTS, np.float32)
    c['c_fconsts'] = np.tile(fc[None, :], (128, 1)).astype(np.float32)

    tp = lambda a: np.ascontiguousarray(np.asarray(a, np.float32).T)
    c['w_l0f_wqT'] = tp(w['L0_ffn_wq'])
    c['w_l0f_wkT'] = tp(w['L0_ffn_wk'])
    c['w_l0f_wvT'] = tp(w['L0_ffn_wv'])
    c['w_l0f_woT'] = tp(w['L0_ffn_wo'])
    c['w_l0f_gateT'] = tp(w['L0_ffn_gate_w'])
    c['w_l0f_e0upT'] = tp(w['L0_ffn_e0_up'])
    c['w_l0f_e0dnT'] = tp(w['L0_ffn_e0_down'])
    c['w_l0f_e1w1T'] = tp(w['L0_ffn_e1_w1'])
    c['w_l0f_e1w2T'] = tp(w['L0_ffn_e1_w2'])
    c['w_l0f_e1dnT'] = tp(w['L0_ffn_e1_down'])
    c['w_l0f_e2l1T'] = tp(w['L0_ffn_e2_l1'])
    c['w_l0f_e2l2T'] = tp(w['L0_ffn_e2_l2'])
    c['w_l0f_e2l3T'] = tp(w['L0_ffn_e2_l3'])
    c['w_l0f_e2l4T'] = tp(w['L0_ffn_e2_l4'])
    c['w_l0f_e3dnT'] = tp(w['L0_ffn_e3_down'])
    c['w_l0f_e3upT'] = tp(w['L0_ffn_e3_up'])
    c['w_l0f_e3outT'] = tp(w['L0_ffn_e3_out'])

    c['w_l1a_WaT'] = np.ascontiguousarray(w['L1_attn_Wa_w'].T)
    c['w_l1a_BT'] = np.ascontiguousarray(w['L1_attn_B_w'].T)
    c['w_l1a_CT'] = np.ascontiguousarray(w['L1_attn_C_w'].T)
    c['w_l1a_DT'] = np.ascontiguousarray(w['L1_attn_D_w'].T)
    c['w_l0a_proj'] = np.ascontiguousarray(np.asarray(w['L0_attn_in_proj_w'], np.float32))
    c['w_l0a_theta'] = np.ascontiguousarray(np.asarray(w['L0_attn_theta_w'], np.float32))
    c['w_l0a_out'] = np.ascontiguousarray(np.asarray(w['L0_attn_out_proj_w'], np.float32))
    bcm = np.eye(128, dtype=np.float32)
    bcm[0, :] = 1.0
    c['c_bcmask_lhsT'] = bcm
    mw0 = np.zeros((128, 128), np.float32)
    ii2 = np.arange(128)
    mw0[1:, :] = np.where(ii2[1:, None] > ii2[None, :], -1e30, 0.0)
    c['c_maskW0'] = mw0
    pm32 = np.zeros((128, 128), np.float32)
    for m in range(128):
        h64 = (m // 64) * 64
        d = m % 64
        pm32[h64 + (d + 32) % 64, m] = 1.0
    c['c_perm32_bf'] = pm32.astype(ml_dtypes.bfloat16)

    HIDP = 2816
    for l in range(2):
        c[f'w_l1f_qkv{l}'] = np.ascontiguousarray(np.asarray(w['L1_ffn_wqkv'][l], np.float32))
        c[f'w_l1f_wo{l}'] = np.ascontiguousarray(np.asarray(w['L1_ffn_wo'][l], np.float32))
        w1 = np.asarray(w['L1_ffn_w1'][l], np.float32)
        w1p = np.zeros((1024, 2 * HIDP), np.float32)
        w1p[:, 0:2730] = w1[:, 0:2730]
        w1p[:, HIDP:HIDP + 2730] = w1[:, 2730:5460]
        c[f'w_l1f_w1{l}'] = w1p
        w2 = np.asarray(w['L1_ffn_w2'][l], np.float32)
        w2p = np.zeros((HIDP, 1024), np.float32)
        w2p[0:2730, :] = w2
        c[f'w_l1f_w2{l}'] = w2p
    DH, WIN = 64, 512
    half_idx = np.arange(0, DH, 2, dtype=np.float32)
    inv_freq = 1.0 / (10000.0 ** (half_idx / DH))
    t_ = np.arange(T, dtype=np.float32)
    freqs = np.concatenate([t_[:, None] * inv_freq[None, :]] * 2, 1)
    base = (half_idx + 0.4 * DH) / (1.4 * DH)
    power = (t_ - T // 2) / float(WIN // 2)
    xsc = np.concatenate([base[None, :] ** power[:, None]] * 2, 1)
    sign = np.concatenate([-np.ones(32), np.ones(32)]).astype(np.float32)
    csq = (np.cos(freqs) * xsc * 8.0).T
    ssq = (np.sin(freqs) * xsc * 8.0).T * sign[:, None]
    csk = (np.cos(freqs) / xsc).T
    ssk = (np.sin(freqs) / xsc).T * sign[:, None]
    tab = np.zeros((128, 4 * T), np.float32)
    for i, tbl in enumerate((csq, ssq, csk, ssk)):
        side, j = i // 2, i % 2
        for ci in range(4):
            blk = tbl[:, ci * 512:(ci + 1) * 512]
            off = ci * 2048 + side * 1024 + j * 512
            tab[0:64, off:off + 512] = blk
            tab[64:128, off:off + 512] = blk
    c['c_l1f_tab'] = tab.astype(ml_dtypes.bfloat16)
    pm = np.zeros((128, 128), np.float32)
    for m in range(128):
        h64 = (m // 64) * 64
        d = m % 64
        k = h64 + (d + 32) % 64
        pm[k, m] = 1.0
    c['c_permrot_bf'] = pm.astype(ml_dtypes.bfloat16)

    pack_cols(c, [
        ('l0f_n1', col_layout(w['L0_ffn_n1_w'])),
        ('l0f_n2', col_layout(w['L0_ffn_n2_w'])),
        ('l0f_qn', np.concatenate([np.asarray(w['L0_ffn_qn_w'], np.float32)] * 2).reshape(128, 1)),
        ('l0f_kn', np.concatenate([np.asarray(w['L0_ffn_kn_w'], np.float32)] * 2).reshape(128, 1)),
        ('l1a_b', col_layout(w['L1_attn_Wa_b'])),
        ('l1f_qs0', np.concatenate([np.asarray(w['L1_ffn_q_scale'][0], np.float32)] * 2).reshape(128, 1)),
        ('l1f_qs1', np.concatenate([np.asarray(w['L1_ffn_q_scale'][1], np.float32)] * 2).reshape(128, 1)),
        ('l1f_ks0', np.concatenate([np.asarray(w['L1_ffn_k_scale'][0], np.float32)] * 2).reshape(128, 1)),
        ('l1f_ks1', np.concatenate([np.asarray(w['L1_ffn_k_scale'][1], np.float32)] * 2).reshape(128, 1)),
        ('l1f_lng0', col_layout(w['L1_ffn_ln_g'][0])),
        ('l1f_lnb0', col_layout(w['L1_ffn_ln_b'][0])),
        ('l1f_lng1', col_layout(w['L1_ffn_ln_g'][1])),
        ('l1f_lnb1', col_layout(w['L1_ffn_ln_b'][1])),
        ('l1f_b1a0', col_layout(np.asarray(w['L1_ffn_b1'][0], np.float32)[0:2730], pad_to=22)),
        ('l1f_b1g0', col_layout(np.asarray(w['L1_ffn_b1'][0], np.float32)[2730:5460], pad_to=22)),
        ('l1f_b1a1', col_layout(np.asarray(w['L1_ffn_b1'][1], np.float32)[0:2730], pad_to=22)),
        ('l1f_b1g1', col_layout(np.asarray(w['L1_ffn_b1'][1], np.float32)[2730:5460], pad_to=22)),
        ('l1f_b20', col_layout(w['L1_ffn_b2'][0])),
        ('l1f_b21', col_layout(w['L1_ffn_b2'][1])),
        ('l0a_lng', col_layout(w['L0_attn_norm_scale'])),
        ('l0a_lnb', col_layout(w['L0_attn_norm_bias'])),
        ('l0a_Alog', col_layout(w['L0_attn_A_log'])),
        ('l0a_dtb', col_layout(w['L0_attn_dt_bias'])),
        ('l0a_bcs', np.concatenate([np.asarray(w['L0_attn_bcB_scale'], np.float32),
                                    np.asarray(w['L0_attn_bcC_scale'], np.float32)]).reshape(128, 1)),
        ('l0a_bcb', np.concatenate([np.asarray(w['L0_attn_B_bias'], np.float32),
                                    np.asarray(w['L0_attn_C_bias'], np.float32)]).reshape(128, 1)),
        ('l0a_D', col_layout(w['L0_attn_D'])),
    ])
    return c


class Ctx:
    def col(self, name, i=0):
        off, k = COLS[name]
        assert i < k
        return self.t['allcols'][:, off + i:off + i + 1]

    def __init__(self, nc, p, aps):
        self.nc = nc
        self.p = p
        self.aps = aps
        self.t = {}
        self.wbuf = None
        self.wbufb = None
        self.wbuf_free = [None, None]
        self.wcast_free = [None, None]


def emit_load_consts(ctx, es):
    nc, p = ctx.nc, ctx.p
    tt = ctx.t
    tt['ident_bf'] = sb(nc, es, 'ident_bf', [128, 128], BF16)
    tt['ident_f32'] = sb(nc, es, 'ident_f32', [128, 128], F32)
    tt['ones_col_bf'] = sb(nc, es, 'ones_col_bf', [128, 1], BF16)
    tt['ones_col_f32'] = sb(nc, es, 'ones_col_f32', [128, 1], F32)
    tt['ones_row_f32'] = sb(nc, es, 'ones_row_f32', [1, 128], F32)
    tt['ones128'] = sb(nc, es, 'ones128', [128, 128], F32)
    tt['mask_causal_bf'] = sb(nc, es, 'mask_causal_bf', [128, 128], BF16)
    tt['mask_keep_ge_bf'] = sb(nc, es, 'mask_keep_ge_bf', [128, 128], BF16)
    tt['permrot_bf'] = sb(nc, es, 'permrot_bf', [128, 128], BF16)
    for name in ('ident_bf', 'ident_f32', 'ones_col_bf', 'ones_col_f32',
                 'ones_row_f32', 'ones128',
                 'mask_causal_bf', 'mask_keep_ge_bf', 'permrot_bf'):
        p.dma('sy', tt[name][:], ctx.aps['c_' + name][:, :])
    tt['fconsts'] = sb(nc, es, 'fconsts', [128, len(FCONSTS)], F32)
    p.dma('sy', tt['fconsts'][:], ctx.aps['c_fconsts'][:, :])
    ncols_total = ctx.aps['c_allcols'].shape[1]
    tt['allcols'] = sb(nc, es, 'allcols', [128, ncols_total], F32)
    p.dma('sy', tt['allcols'][:], ctx.aps['c_allcols'][:, :])
    for i, v in enumerate(FCONSTS):
        nc.const_aps.register(F32, float(v), tt['fconsts'][:, i:i + 1])
    cld = p.cnt['sq']
    for e in ('te', 've', 'se', 'po'):
        p.wait_only(e, [('sq', cld)])
    ctx.wbuf = [sb(nc, es, f'wpool{i}', [128, 1024], F32) for i in range(2)]
    ctx.wbufb = [sb(nc, es, f'wpoolb{i}', [128, 2048], BF16) for i in range(2)]
    for i in range(2):
        p.op('po', (lambda e, i=i: e.memset(ctx.wbuf[i][:], 0.0)))
        ctx.wbuf_free[i] = ('po', p.cnt['po'])


WSLAB_COLS = 128


class Slabs:

    def __init__(self, ctx, lhsT_ap, K, k0=0):
        self.ctx = ctx
        self.ap = lhsT_ap
        self.K = K
        self.k0 = k0
        self.KC = K // 128
        assert K % 128 == 0 and self.KC * WSLAB_COLS <= 2048
        self.i = 0

    def get(self, m0, mw):
        ctx, p = self.ctx, self.ctx.p
        bi = self.i % 2
        self.i += 1
        wsem = 'w%d' % bi
        halves = [(0, self.KC)] if self.KC * WSLAB_COLS <= 1024 else \
            [(0, (self.KC + 1) // 2), ((self.KC + 1) // 2, self.KC - (self.KC + 1) // 2)]
        first = True
        for (kc0, kcn) in halves:
            waits = []
            if first and ctx.wbuf_free[bi]:
                waits.append(ctx.wbuf_free[bi])
            if not first:
                waits.append(('po', p.cnt['po']))
            p.dma('sy', ctx.wbuf[bi][:, 0:kcn * WSLAB_COLS]
                  .rearrange('p (kc m) -> p kc m', m=WSLAB_COLS)[:, :, 0:mw],
                  self.ap[self.k0 + kc0 * 128:self.k0 + (kc0 + kcn) * 128, m0:m0 + mw]
                  .rearrange('(kc p) m -> p kc m', p=128),
                  waits=waits, sem=wsem)
            ld = p.cnt[wsem]
            cw = [(wsem, ld)]
            if first and ctx.wcast_free[bi]:
                cw.append(ctx.wcast_free[bi])
            p.op('po', (lambda e, b=bi, o=kc0 * WSLAB_COLS, n=kcn * WSLAB_COLS:
                        e.tensor_copy(ctx.wbufb[b][:, o:o + n], ctx.wbuf[b][:, 0:n])),
                 waits=cw)
            first = False
        cast_done = p.cnt['po']
        ctx.wbuf_free[bi] = ('po', cast_done)

        def tile(k, b=bi, mw=mw):
            return ctx.wbufb[b][:, k * WSLAB_COLS:k * WSLAB_COLS + mw]
        return bi, tile, cast_done

    def done(self, bi, te_cnt):
        self.ctx.wcast_free[bi] = ('te', te_cnt)


def emit_linear(ctx, es, name, lhsT_ap, K, M, rhs_tiles, out_cb,
                rhs_col0=0, n_cols=T, nchunk=512, m_groups=None,
                wait_rhs=()):
    nc, p = ctx.nc, ctx.p
    KC = K // 128
    assert K % 128 == 0 and KC * WSLAB_COLS <= 2048
    if m_groups is None:
        m_groups = [(i * 128, min(128, M - i * 128)) for i in range((M + 127) // 128)]
    slabs = Slabs(ctx, lhsT_ap, K)
    nci = [(i * nchunk, min(nchunk, n_cols - i * nchunk))
           for i in range((n_cols + nchunk - 1) // nchunk)]
    for gi, (m0, mw) in enumerate(m_groups):
        bi, tile_fn, cast_done = slabs.get(m0, mw)
        for ni, (c0, cw) in enumerate(nci):
            bidx = (gi * len(nci) + ni) % 6
            bank = p.bank(bidx)
            te_waits = [('po', cast_done)] + p.bank_waits(bidx) + list(wait_rhs)
            for k in range(KC):
                p.op('te', (lambda e, k=k, bank=bank, c0=c0, cw=cw, mw=mw,
                            tile_fn=tile_fn, rt=rhs_tiles[k], rc=rhs_col0:
                            e.matmul(bank[0:mw, 0:cw], tile_fn(k),
                                     rt[:, rc + c0:rc + c0 + cw],
                                     start=(k == 0), stop=(k == KC - 1))),
                     waits=te_waits if k == 0 else ())
            te_cnt = p.cnt['te']
            slabs.done(bi, te_cnt)
            out_cb(gi, m0, mw, ni, c0, cw, bidx, bank, te_cnt)
    return


def emit_postln(ctx, es, name, src, dst, eps=1e-5, dst_bf=None, gamma=None, beta=None,
                rms=False, wait_src=(), end_scope=True, src2=None, src_scale=1.0,
                src_builder=None):
    nc, p = ctx.nc, ctx.p
    tt = ctx.t
    nf = len(src)
    dinv = 1.0 / (nf * 128)
    les = es.enter_context(ExitStack())
    stat_t = sb(nc, les, name + '_stat', [1, 2 * T], F32)
    sums = stat_t[:, 0:T]; sumsq = stat_t[:, T:2 * T]
    mu = sums; var = sumsq; rstd = sumsq
    sq_bf = sb(nc, les, name + '_sq', [128, T], BF16)
    src_bf = sb(nc, les, name + '_srcbf', [128, T], BF16)
    rstd_b = sb(nc, les, name + '_rstdb', [128, T], F32)
    m2_b = None if rms else sb(nc, les, name + '_m2b', [128, T], F32)
    need_comb = (src2 is not None) or (src_scale != 1.0) or (src_builder is not None)
    rot = [sb(nc, les, name + f'_rot{i}', [128, T], F32) for i in range(2)] \
        if need_comb else None
    rot_gate = [0, 0]

    def comb(f, phase, waits):
        s2 = src2[f] if src2 is not None else None
        if not need_comb or (src_builder is None and s2 is None and src_scale == 1.0):
            return src[f][:], 0
        rt = rot[f % 2]
        w = list(waits) + ([('ve', p.cnt['ve'])] if True else [])
        if rot_gate[f % 2]:
            w.append(rot_gate[f % 2])
        if src_builder is not None:
            cve = src_builder(f, rt, w)
            return rt[:], cve
        if s2 is None:
            p.op('ve', (lambda e, f=f, rt=rt: e.tensor_single_scalar(
                rt[:], src[f][:], float(src_scale), ALU.mult)), waits=w)
        elif src_scale == 1.0:
            p.op('ve', (lambda e, f=f, rt=rt, s2=s2: e.tensor_tensor(
                rt[:], src[f][:], s2[:], ALU.add)), waits=w)
        else:
            p.op('ve', (lambda e, f=f, rt=rt, s2=s2: e.scalar_tensor_tensor(
                rt[:], src[f][:], float(src_scale), s2[:], ALU.mult, ALU.add)), waits=w)
        return rt[:], p.cnt['ve']
    bq = 6; bx = 7
    te_prev = 0
    for f in range(nf):
        w0 = list(wait_src) if f == 0 else [('te', te_prev)]
        cap, cve = comb(f, 'stats', w0)
        wc = w0 + ([('ve', cve)] if cve else [])
        p.op('se', (lambda e, cap=cap: e.activation(sq_bf[:], cap, AF.Square)),
             waits=wc)
        se_c = p.cnt['se']
        p.op('po', (lambda e, cap=cap: e.tensor_copy(src_bf[:], cap)), waits=wc)
        po_c = p.cnt['po']
        rot_gate[f % 2] = ('se', se_c)
        for ci in range(NT):
            bs = p.bank(ci); bqq = p.bank(4 + ci)
            p.op('te', (lambda e, ci=ci, f=f, bs=bs:
                        e.matmul(bs[0:1, 0:512], tt['ones_col_bf'][:],
                                 src_bf[:, ci * 512:(ci + 1) * 512],
                                 start=(f == 0), stop=(f == nf - 1))),
                 waits=[('po', po_c)] + (p.bank_waits(ci) if f == 0 else []))
            p.op('te', (lambda e, ci=ci, f=f, bq2=bqq:
                        e.matmul(bq2[0:1, 0:512], tt['ones_col_bf'][:],
                                 sq_bf[:, ci * 512:(ci + 1) * 512],
                                 start=(f == 0), stop=(f == nf - 1))),
                 waits=[('se', se_c)] + (p.bank_waits(4 + ci) if f == 0 else []))
        te_prev = p.cnt['te']
    te_stats = p.cnt['te']
    for ci in range(NT):
        p.op('ve', (lambda e, ci=ci: e.tensor_copy(sums[:, ci * 512:(ci + 1) * 512],
                                                   p.bank(ci)[0:1, 0:512])),
             waits=[('te', te_stats)])
        p.op('ve', (lambda e, ci=ci: e.tensor_copy(sumsq[:, ci * 512:(ci + 1) * 512],
                                                   p.bank(4 + ci)[0:1, 0:512])))
    for ci in range(NT):
        p.bank_release(ci, 've', p.cnt['ve'])
        p.bank_release(4 + ci, 've', p.cnt['ve'])
    musq = rstd_b[0:1, 0:T]
    vstep = lambda fn: p.op('ve', fn, waits=[('ve', p.cnt['ve'])])
    if rms:
        vstep(lambda e: e.tensor_single_scalar(var[:], sumsq[:], dinv, ALU.mult))
    else:
        vstep(lambda e: e.tensor_single_scalar(mu[:], sums[:], dinv, ALU.mult))
        vstep(lambda e: e.tensor_single_scalar(var[:], sumsq[:], dinv, ALU.mult))
        vstep(lambda e: e.tensor_tensor(musq[:], mu[:], mu[:], ALU.mult))
        vstep(lambda e: e.tensor_tensor(var[:], var[:], musq[:], ALU.subtract))
    ve_var = p.cnt['ve']
    p.op('se', (lambda e: e.activation(rstd[:], var[:], AF.Ln, bias=float(eps))),
         waits=[('ve', ve_var)])
    p.op('se', (lambda e: e.activation(rstd[:], rstd[:], AF.Exp, scale=-0.5)),
         waits=[('se', p.cnt['se'])])
    se_rstd = p.cnt['se']
    ve_m2 = p.cnt['ve']
    for ci in range(NT):
        bidx = ci % 2
        p.op('te', (lambda e, ci=ci, b=p.bank(bidx):
                    e.matmul(b[:, 0:512], tt['ones_row_f32'][:],
                             rstd[:, ci * 512:(ci + 1) * 512], start=True, stop=True)),
             waits=[('se', se_rstd)] + p.bank_waits(bidx))
        p.op('ve', (lambda e, ci=ci, b=p.bank(bidx):
                    e.tensor_copy(rstd_b[:, ci * 512:(ci + 1) * 512], b[:, 0:512])),
             waits=[('te', p.cnt['te'])])
        p.bank_release(bidx, 've', p.cnt['ve'])
        if not rms:
            bidx2 = 2 + ci % 2
            p.op('te', (lambda e, ci=ci, b=p.bank(bidx2):
                        e.matmul(b[:, 0:512], tt['ones_row_f32'][:],
                                 mu[:, ci * 512:(ci + 1) * 512], start=True, stop=True)),
                 waits=[('ve', ve_var)] + p.bank_waits(bidx2))
            p.op('ve', (lambda e, ci=ci, b=p.bank(bidx2):
                        e.tensor_copy(m2_b[:, ci * 512:(ci + 1) * 512], b[:, 0:512])),
                 waits=[('te', p.cnt['te'])])
            p.bank_release(bidx2, 've', p.cnt['ve'])
    ve_bcast = p.cnt['ve']
    scr = None
    if dst is None and not need_comb:
        scr = [sb(nc, les, name + f'_scr{i}', [128, T], F32) for i in range(2)]
        dstl = lambda f: scr[f % 2]
        scr_gate = [0, 0]
    elif dst is None:
        dstl = lambda f: rot[f % 2]
        scr_gate = [0, 0]
        scr = 'rot'
    else:
        dstl = lambda f: dst[f]
    for f in range(nf):
        dtile = dstl(f)
        w0 = [('ve', ve_bcast)]
        g = scr_gate[f % 2] if scr is not None else 0
        if g:
            w0.append(('po', g[1]) if isinstance(g, tuple) else ('se', g))
        if scr == 'rot' and isinstance(rot_gate[f % 2], tuple):
            w0.append(rot_gate[f % 2])
        cap, cve = comb(f, 'apply', w0)
        if cve:
            w0 = w0 + [('ve', cve)]
        if not rms:
            p.op('ve', (lambda e, d=dtile, cap=cap: e.tensor_tensor(d[:], cap, m2_b[:], ALU.subtract)),
                 waits=w0)
            p.op('ve', (lambda e, d=dtile: e.tensor_tensor(d[:], d[:], rstd_b[:], ALU.mult)),
                 waits=[('ve', p.cnt['ve'])])
        else:
            p.op('ve', (lambda e, d=dtile, cap=cap: e.tensor_tensor(d[:], cap, rstd_b[:], ALU.mult)),
                 waits=w0)
        if gamma is not None:
            p.op('se', (lambda e, d=dtile, f=f: e.activation(
                (dst_bf[f][:] if dst_bf is not None else d[:]), d[:], AF.Identity,
                bias=beta[f], scale=gamma[f])), waits=[('ve', p.cnt['ve'])])
            if scr is not None:
                scr_gate[f % 2] = p.cnt['se']
        elif dst_bf is not None:
            p.op('po', (lambda e, d=dtile, f=f: e.tensor_copy(dst_bf[f][:], d[:])),
                 waits=[('ve', p.cnt['ve'])])
            if scr is not None:
                scr_gate[f % 2] = ('po-gate', p.cnt['po'])
        if need_comb:
            rot_gate[f % 2] = ('ve', p.cnt['ve'])
    if end_scope:
        p.barrier()
        les.close()


def emit_l1_attn(ctx, es, x, x_bf):
    nc, p = ctx.nc, ctx.p
    tt = ctx.t
    h_bf = [sb(nc, es, f'l1a_hb{i}', [128, T], BF16) for i in range(2)]
    bias_ld = 0
    with ExitStack() as ies:
        a_t = sb(nc, ies, 'l1a_a', [128, T], F32)
        b_t = sb(nc, ies, 'l1a_b', [128, T], F32)
        h_t = sb(nc, ies, 'l1a_h', [128, T], F32)
        for i in range(2):
            ascan = p.cnt['ve']

            def evac_a(gi, m0, mw, ni, c0, cw, bidx, pap, te_cnt, i=i, ascan=ascan):
                p.op('se', (lambda e, c0=c0, cw=cw, pap=pap, i=i:
                            e.activation(a_t[:, c0:c0 + cw], pap[0:128, 0:cw],
                                         AF.Sigmoid, bias=ctx.col('l1a_b', i))),
                     waits=[('te', te_cnt), ('ve', ascan)])
                p.bank_release(bidx, 'se', p.cnt['se'])
            emit_linear(ctx, ies, 'l1a_wa', ctx.aps['w_l1a_WaT'], D, 128, x_bf, evac_a,
                        m_groups=[(i * 128, 128)])

            def evac_b(gi, m0, mw, ni, c0, cw, bidx, pap, te_cnt, ascan=ascan):
                p.op('ve', (lambda e, c0=c0, cw=cw, pap=pap:
                            e.tensor_copy(b_t[:, c0:c0 + cw], pap[0:128, 0:cw])),
                     waits=[('te', te_cnt), ('ve', ascan)])
                p.bank_release(bidx, 've', p.cnt['ve'])
            emit_linear(ctx, ies, 'l1a_bm', ctx.aps['w_l1a_BT'], D, 128, x_bf, evac_b,
                        m_groups=[(i * 128, 128)])
            p.op('ve', (lambda e: e.tensor_tensor_scan(h_t[:], a_t[:], b_t[:],
                                                       0.0, ALU.mult, ALU.add)),
                 waits=[('se', p.cnt['se']), ('ve', p.cnt['ve'])])
            p.op('po', (lambda e, i=i: e.tensor_copy(h_bf[i][:], h_t[:])),
                 waits=[('ve', p.cnt['ve'])])
    scan_done_po = p.cnt['po']

    KC_C, KC_D = 2, 8
    s_dt = Slabs(ctx, ctx.aps['w_l1a_DT'], D)
    wc = sb(nc, es, 'l1a_wc', [128, 256], F32)
    wcb = sb(nc, es, 'l1a_wcb', [128, 256], BF16)
    wc_gate = None
    for mf in range(NF):
        waits = [wc_gate] if wc_gate else []
        p.dma('po', wc[:].rearrange('p (kc m) -> p kc m', m=128),
              ctx.aps['w_l1a_CT'][:, mf * 128:(mf + 1) * 128]
              .rearrange('(kc p) m -> p kc m', p=128), waits=waits)
        gl = p.cnt['gq']
        p.op('po', (lambda e: e.tensor_copy(wcb[:], wc[:])), waits=[('gq', gl)])
        wc_cast = p.cnt['po']
        bi, tile, cast = s_dt.get(mf * 128, 128)
        for ni in range(NT):
            bidx = (mf * NT + ni) % 6
            bank = p.bank(bidx)
            c0 = ni * 512
            for k in range(KC_C):
                p.op('te', (lambda e, k=k, bank=bank, c0=c0:
                            e.matmul(bank[:, 0:512], wcb[:, k * 128:(k + 1) * 128],
                                     h_bf[k][:, c0:c0 + 512], start=(k == 0), stop=False)),
                     waits=([('po', wc_cast)] + p.bank_waits(bidx)) if k == 0 else ())
            for k in range(KC_D):
                p.op('te', (lambda e, k=k, bank=bank, c0=c0, tile=tile:
                            e.matmul(bank[:, 0:512], tile(k),
                                     x_bf[k][:, c0:c0 + 512], start=False,
                                     stop=(k == KC_D - 1))),
                     waits=[('po', cast)] if k == 0 else ())
            te_cnt = p.cnt['te']
            p.op('ve', (lambda e, mf=mf, c0=c0, bank=bank:
                        e.tensor_tensor(x[mf][:, c0:c0 + 512], x[mf][:, c0:c0 + 512],
                                        bank[:, 0:512], ALU.add)),
                 waits=[('te', te_cnt)])
            p.bank_release(bidx, 've', p.cnt['ve'])
        s_dt.done(bi, p.cnt['te'])
        wc_gate = ('te', p.cnt['te'])
    emit_postln(ctx, es, 'l1a_ln', x, x, eps=1e-5, dst_bf=x_bf,
                wait_src=[('ve', p.cnt['ve'])])


def emit_l0_ffn(ctx, es, x, x_bf):
    nc, p = ctx.nc, ctx.p
    tt = ctx.t
    aps = ctx.aps
    EPS = 1.1920929e-07

    zerocol = tt['fconsts'][:, 5:6]

    emit_postln(ctx, es, 'l0f_n1', x, None, eps=EPS, dst_bf=x_bf, rms=True,
                gamma=[ctx.col('l0f_n1', f) for f in range(NF)], beta=[zerocol] * NF,
                wait_src=[('sq', p.cnt['sq'])])

    p.dump('n1_xn0', x_bf[0][:])
    delta = [sb(nc, es, f'l0f_delta{i}', [128, T], BF16) for i in range(6)]
    oes = es.enter_context(ExitStack())
    o_all = [sb(nc, oes, f'l0f_o{i}', [128, T], BF16) for i in range(NF)]
    sq_wq = Slabs(ctx, aps['w_l0f_wqT'], D)
    sq_wk = Slabs(ctx, aps['w_l0f_wkT'], D)
    sq_wv = Slabs(ctx, aps['w_l0f_wvT'], D)

    with ExitStack() as pes:
        q_bf = sb(nc, pes, 'l0f_q', [128, T], BF16)
        k_bf = sb(nc, pes, 'l0f_k', [128, T], BF16)
        vtm = sb(nc, pes, 'l0f_vtm', [128, T], BF16)
        P_s = sb(nc, pes, 'l0f_P', [128, T], BF16)
        PT_s = sb(nc, pes, 'l0f_PT', [128, T], BF16)
        vfm = PT_s
        rstd_scr = sb(nc, pes, 'l0f_rstd', [128, 1536], F32)
        bsc = sb(nc, pes, 'l0f_bsc', [128, 512], F32)
        cols = sb(nc, pes, 'l0f_cols', [128, 8], F32)
        otok = sb(nc, pes, 'l0f_otok', [128, 64], BF16)
        SLOTS = [(4, 0), (4, 32), (4, 64), (5, 0), (5, 32), (5, 64), (6, 0), (6, 32)]

        def proj_pair(slab, hp, wait_rhs):
            bi, tile, cast = slab.get(hp * 128, 128)
            for ci in range(NT):
                w0 = [('po', cast)] + p.bank_waits(ci) + wait_rhs
                for k in range(8):
                    p.op('te', (lambda e, k=k, ci=ci, tile=tile:
                                e.matmul(p.bank(ci)[0:128, 0:512], tile(k),
                                         x_bf[k][:, ci * 512:(ci + 1) * 512],
                                         start=(k == 0), stop=(k == 7))),
                         waits=w0 if k == 0 else ())
            slab.done(bi, p.cnt['te'])
            return p.cnt['te']

        def qk_rms(out_bf, gcol, fold8, te_proj):
            p.op('se', (lambda e: e.activation(P_s[:], p.bank_span(0, 4), AF.Square)),
                 waits=[('te', te_proj)] + p.bank_waits(4) + p.bank_waits(5) + p.bank_waits(6))
            sq_done = p.cnt['se']
            for si, (h2, ci) in enumerate([(a, b) for a in range(2) for b in range(4)]):
                b, r = SLOTS[si]
                p.op('te', (lambda e, h2=h2, ci=ci, b=b, r=r:
                            e.matmul(p.bank(b)[r:r + 1, 0:512],
                                     tt['ones_col_bf'][h2 * 64:(h2 + 1) * 64, 0:1],
                                     P_s[h2 * 64:(h2 + 1) * 64, ci * 512:(ci + 1) * 512],
                                     start=True, stop=True)),
                     waits=[('se', sq_done)] if si == 0 else ())
            te_norm = p.cnt['te']
            for si in range(8):
                b, r = SLOTS[si]
                p.op('se', (lambda e, b=b, r=r: e.activation(
                    p.bank(b)[r:r + 1, 0:512], p.bank(b)[r:r + 1, 0:512], AF.Ln,
                    scale=1.0 / 64.0, bias=float(EPS))),
                     waits=([('te', te_norm)] if si == 0 else [('se', p.cnt['se'])]))
                p.op('se', (lambda e, b=b, r=r: e.activation(
                    p.bank(b)[r:r + 1, 0:512], p.bank(b)[r:r + 1, 0:512], AF.Exp,
                    scale=-0.5, bias=(-2.0794415416798357 if fold8 else 0.0))),
                     waits=[('se', p.cnt['se'])])
            se_rows = p.cnt['se']
            for si in range(8):
                b, r = SLOTS[si]
                j = (4, 5, 6).index(b)
                p.op('ve', (lambda e, b=b, r=r, j=j: e.tensor_copy(
                    rstd_scr[r:r + 1, j * 512:(j + 1) * 512],
                    p.bank(b)[r:r + 1, 0:512])),
                     waits=[('se', se_rows)] if si == 0 else ())
            ve_rows = p.cnt['ve']
            for b in (4, 5):
                p.bank_release(b, 've', ve_rows)
            for si, (h2, ci) in enumerate([(a, b) for a in range(2) for b in range(4)]):
                b, r = SLOTS[si]
                j = (4, 5, 6).index(b)
                p.op('te', (lambda e, h2=h2, ci=ci, r=r, j=j:
                            e.matmul(p.bank(4 + ci)[h2 * 64:h2 * 64 + 64, 0:512],
                                     tt['ones128'][r:r + 1, 0:64],
                                     rstd_scr[r:r + 1, j * 512:(j + 1) * 512],
                                     start=True, stop=True)),
                     waits=([('ve', ve_rows)] + p.bank_waits(4) + p.bank_waits(5)
                            + p.bank_waits(6) + p.bank_waits(7)) if si == 0 else ())
            te_b = p.cnt['te']
            for ci in range(NT):
                p.op('ve', (lambda e, ci=ci: e.tensor_copy(bsc[:], p.bank(4 + ci)[:, 0:512])),
                     waits=([('te', te_b)] if ci == 0 else []) + [('se', p.cnt['se']),
                                                                  ('ve', p.cnt['ve'])])
                p.bank_release(4 + ci, 've', p.cnt['ve'])
                p.op('ve', (lambda e, ci=ci: e.tensor_tensor(
                    bsc[:], p.bank(ci)[:, 0:512], bsc[:], ALU.mult)),
                     waits=[('ve', p.cnt['ve'])])
                p.bank_release(ci, 've', p.cnt['ve'])
                p.op('se', (lambda e, ci=ci, out_bf=out_bf, gcol=gcol: e.activation(
                    out_bf[:, ci * 512:(ci + 1) * 512], bsc[:], AF.Identity,
                    bias=zerocol, scale=gcol)), waits=[('ve', p.cnt['ve'])])
            return p.cnt['se']

        dma_w = p.cnt['sq']
        for hp in range(NF):
            wq_rhs = [('se', p.cnt['se']), ('sq', dma_w)]
            tq = proj_pair(sq_wq, hp, wq_rhs)
            qk_done = qk_rms(q_bf, ctx.col('l0f_qn'), True, tq)
            tk = proj_pair(sq_wk, hp, [])
            qk_done = qk_rms(k_bf, ctx.col('l0f_kn'), False, tk)
            bi, tile, cast = sq_wv.get(hp * 128, 128)
            for ci in range(NT):
                w0 = [('po', cast)] + p.bank_waits(ci)
                for k in range(8):
                    p.op('te', (lambda e, k=k, ci=ci, tile=tile:
                                e.matmul(p.bank(ci)[0:128, 0:512], tile(k),
                                         x_bf[k][:, ci * 512:(ci + 1) * 512],
                                         start=(k == 0), stop=(k == 7))),
                         waits=w0 if k == 0 else ())
                p.op('ve', (lambda e, ci=ci: e.tensor_copy(
                    vfm[:, ci * 512:(ci + 1) * 512], p.bank(ci)[:, 0:512])),
                     waits=[('te', p.cnt['te'])])
                p.bank_release(ci, 've', p.cnt['ve'])
            sq_wv.done(bi, p.cnt['te'])
            ve_vfm = p.cnt['ve']
            for m in range(16):
                b = 4 + m // 8
                p.op('te', (lambda e, m=m, b=b: e.transpose(
                    p.bank_bf(b)[:, (m % 8) * 128:(m % 8) * 128 + 128],
                    vfm[:, m * 128:(m + 1) * 128], tt['ident_bf'][:])),
                     waits=([('ve', ve_vfm)] + p.bank_waits(4) + p.bank_waits(5))
                     if m == 0 else ())
                if m % 8 == 7:
                    p.op('ve', (lambda e, m=m, b=b: e.tensor_copy(
                        vtm[:, (m - 7) * 128:(m + 1) * 128], p.bank_bf(b)[:, 0:1024])),
                         waits=[('te', p.cnt['te'])])
                    p.bank_release(b, 've', p.cnt['ve'])
            if hp == 0:
                p.dump('q0', q_bf[:])
                p.dump('k0', k_bf[:])
                p.dump('vtm0', vtm[:])
            negmax = cols[:, 0:1]; sums = cols[:, 1:2]; rsum = cols[:, 2:3]
            rsum_read = 0
            otok_read = 0
            pt_read = 0
            for h2 in range(2):
                for qt in range(16):
                    kw = (qt + 1) * 128
                    nk = qt + 1
                    nkc = (kw + 511) // 512
                    w0 = [('se', p.cnt['se']), ('ve', p.cnt['ve'])]
                    for b2 in range(nkc):
                        w0 += p.bank_waits(b2)
                    for c in range(nkc):
                        c0, c1 = c * 512, min(kw, (c + 1) * 512)
                        p.op('te', (lambda e, h2=h2, qt=qt, c0=c0, c1=c1:
                                    e.matmul(p.psum.ap()[0:128, c0:c1],
                                             q_bf[h2 * 64:h2 * 64 + 64, qt * 128:(qt + 1) * 128],
                                             k_bf[h2 * 64:h2 * 64 + 64, c0:c1],
                                             start=True, stop=True)),
                             waits=w0 if c == 0 else ())
                    p.op('te', (lambda e, kw=kw: e.matmul(
                        p.psum.ap()[0:128, kw - 128:kw], tt['ident_bf'][:],
                        tt['mask_causal_bf'][:], start=False, stop=True,
                        skip_group_check=True)))
                    te_s = p.cnt['te']
                    p.op('ve', (lambda e, kw=kw: e.tensor_reduce(
                        negmax[:], p.psum.ap()[0:128, 0:kw], AX.X, ALU.max, negate=True)),
                         waits=[('te', te_s)])
                    ve_max = p.cnt['ve']
                    p.op('se', (lambda e, kw=kw: e.activation(
                        P_s[:, 0:kw], p.psum.ap()[0:128, 0:kw], AF.Exp,
                        bias=negmax[:, 0:1], accum_out=sums[:, 0:1])),
                         waits=[('ve', ve_max)]
                         + ([('ve', rsum_read)] if rsum_read else [])
                         + ([('te', pt_read)] if pt_read else []))
                    se_exp = p.cnt['se']
                    for b2 in range(nkc):
                        p.bank_release(b2, 'se', se_exp)
                    p.op('ve', (lambda e: e.reciprocal(rsum[:], sums[:, 0:1])),
                         waits=[('se', se_exp)])
                    rsum_done = p.cnt['ve']
                    rsum_read = rsum_done
                    for ki in range(nk):
                        b = 4 + ki // 8
                        p.op('te', (lambda e, ki=ki, b=b: e.transpose(
                            p.bank_bf(b)[:, (ki % 8) * 128:(ki % 8) * 128 + 128],
                            P_s[:, ki * 128:(ki + 1) * 128], tt['ident_bf'][:])),
                             waits=([('se', se_exp)] + p.bank_waits(4) + p.bank_waits(5))
                             if ki == 0 else ())
                    te_pt = p.cnt['te']
                    pt_read = te_pt
                    nspan = min(nk, 8)
                    p.op('ve', (lambda e, nspan=nspan: e.tensor_copy(
                        PT_s[:, 0:nspan * 128], p.bank_bf(4)[:, 0:nspan * 128])),
                         waits=[('te', te_pt)])
                    if nk > 8:
                        p.op('ve', (lambda e, nk=nk: e.tensor_copy(
                            PT_s[:, 1024:nk * 128], p.bank_bf(5)[:, 0:(nk - 8) * 128])))
                    ve_pt = p.cnt['ve']
                    p.bank_release(4, 've', ve_pt)
                    p.bank_release(5, 've', ve_pt)
                    for ki in range(nk):
                        p.op('te', (lambda e, ki=ki, h2=h2, last=(ki == nk - 1):
                                    e.matmul(p.bank(6)[0:128, 0:64],
                                             PT_s[:, ki * 128:(ki + 1) * 128],
                                             vtm[:, ki * 128 + h2 * 64:ki * 128 + h2 * 64 + 64],
                                             start=(ki == 0), stop=last)),
                             waits=([('ve', ve_pt)] + p.bank_waits(6)) if ki == 0 else ())
                    te_pv = p.cnt['te']
                    p.op('ve', (lambda e: e.tensor_scalar(
                        otok[:], p.bank(6)[0:128, 0:64], rsum[:, 0:1], None, ALU.mult)),
                         waits=[('te', te_pv), ('ve', rsum_done)]
                         + ([('te', otok_read)] if otok_read else []))
                    ve_o = p.cnt['ve']
                    p.bank_release(6, 've', ve_o)
                    p.op('te', (lambda e, qt=qt: e.transpose(
                        p.bank_bf(7)[0:64, (qt % 8) * 128:(qt % 8) * 128 + 128],
                        otok[:], tt['ident_bf'][:])),
                         waits=[('ve', ve_o)] + (p.bank_waits(7) if qt % 8 == 0 else []))
                    otok_read = p.cnt['te']
                    if qt % 8 == 7:
                        p.op('ve', (lambda e, h2=h2, hp=hp, qt=qt: e.tensor_copy(
                            o_all[hp][h2 * 64:h2 * 64 + 64, (qt - 7) * 128:(qt + 1) * 128],
                            p.bank_bf(7)[0:64, 0:1024])),
                             waits=[('te', otok_read)])
                        p.bank_release(7, 've', p.cnt['ve'])
    p.dump('o_all0', o_all[0][:])
    p.dump('o_all7', o_all[7][:])
    def evac_wo(gi, m0, mw, ni, c0, cw, bidx, pap, te_cnt):
        p.op('ve', (lambda e, gi=gi, c0=c0, cw=cw, pap=pap: e.tensor_copy(
            delta[gi][:, c0:c0 + cw], pap[0:128, 0:cw])),
             waits=[('te', te_cnt)])
        p.bank_release(bidx, 've', p.cnt['ve'])
    emit_linear(ctx, oes, 'l0f_wo', aps['w_l0f_woT'], D, D, o_all, evac_wo,
                m_groups=[(256 + i * 128, 128) for i in range(6)],
                wait_rhs=[('ve', p.cnt['ve'])])
    p.barrier()
    oes.close()

    p.dump('postwo_x2', x[2][:])
    emit_postln(ctx, es, 'l0f_n2', x, None, eps=EPS, dst_bf=x_bf, rms=True,
                gamma=[ctx.col('l0f_n2', f) for f in range(NF)], beta=[zerocol] * NF,
                wait_src=[('ve', p.cnt['ve'])],
                src2=[None, None] + [delta[i] for i in range(6)])

    wts_tm = sb(nc, es, 'l0f_wtstm', [128, 64], F32)
    with ExitStack() as ges:
        logits = sb(nc, ges, 'l0f_logits', [4, T], F32)
        pscr = sb(nc, ges, 'l0f_pscr', [128, 4], F32)
        pscr2 = sb(nc, ges, 'l0f_pscr2', [128, 4], F32)
        keepm = sb(nc, ges, 'l0f_keepm', [128, 4], F32)
        cmp8 = sb(nc, ges, 'l0f_cmp8', [128, 4], U8)
        gc = sb(nc, ges, 'l0f_gcols', [128, 8], F32)

        def evac_gate(gi, m0, mw, ni, c0, cw, bidx, pap, te_cnt):
            p.op('ve', (lambda e, c0=c0, cw=cw, pap=pap: e.tensor_copy(
                logits[:, c0:c0 + cw], pap[0:4, 0:cw])), waits=[('te', te_cnt)])
            p.bank_release(bidx, 've', p.cnt['ve'])
        emit_linear(ctx, ges, 'l0f_gate', aps['w_l0f_gateT'], D, 4, x_bf, evac_gate,
                    m_groups=[(0, 4)])
        ve_lg = p.cnt['ve']
        for ci in range(16):
            p.op('te', (lambda e, ci=ci: e.transpose(
                p.bank(4)[0:128, ci * 4:ci * 4 + 4],
                logits[0:4, ci * 128:(ci + 1) * 128], tt['ident_f32'][0:4, 0:4])),
                 waits=([('ve', ve_lg)] + p.bank_waits(4)) if ci == 0 else ())
        te_lt = p.cnt['te']
        nm = gc[:, 0:1]; ssum = gc[:, 1:2]; rs = gc[:, 2:3]; pm1 = gc[:, 3:4]
        pm2 = gc[:, 4:5]; dcol = gc[:, 5:6]; dc2 = gc[:, 6:7]; rdc = gc[:, 7:8]
        eqf = sb(nc, ges, 'l0f_eqf', [128, 4], F32)
        t1s = sb(nc, ges, 'l0f_t1', [128, 4], F32)
        t2s = sb(nc, ges, 'l0f_t2', [128, 4], F32)
        kfs = sb(nc, ges, 'l0f_kf', [128, 4], F32)
        cmp8b = sb(nc, ges, 'l0f_cmp8b', [128, 4], U8)

        def vstep(fn, extra=()):
            p.op('ve', fn, waits=[('ve', p.cnt['ve'])] + list(extra))
        for ci in range(16):
            sl = p.bank(4)[0:128, ci * 4:ci * 4 + 4]
            vstep((lambda e, sl=sl: e.tensor_reduce(nm[:], sl, AX.X, ALU.max,
                                                    negate=True)),
                  extra=([('te', te_lt)] if ci == 0 else [('se', p.cnt['se'])]))
            p.op('se', (lambda e, sl=sl: e.activation(pscr[:], sl, AF.Exp,
                                                      bias=nm[:, 0:1],
                                                      accum_out=ssum[:, 0:1])),
                 waits=[('ve', p.cnt['ve'])])
            vstep(lambda e: e.reciprocal(rs[:], ssum[:, 0:1]),
                  extra=[('se', p.cnt['se'])])
            vstep(lambda e: e.tensor_scalar(pscr2[:], pscr[:], rs[:, 0:1], None, ALU.mult))
            vstep(lambda e: e.tensor_reduce(pm1[:], pscr2[:], AX.X, ALU.max))
            vstep(lambda e: e.tensor_scalar(cmp8[:], pscr2[:], pm1[:, 0:1], None, ALU.is_ge))
            vstep(lambda e: e.tensor_copy(eqf[:], cmp8[:]))
            vstep(lambda e: e.tensor_tensor(t1s[:], pscr2[:], eqf[:], ALU.mult))
            vstep(lambda e: e.tensor_tensor(t2s[:], pscr2[:], t1s[:], ALU.subtract))
            vstep(lambda e: e.tensor_reduce(pm2[:], t2s[:], AX.X, ALU.max))
            vstep(lambda e: e.tensor_scalar(cmp8b[:], pscr2[:], pm2[:, 0:1], None, ALU.is_ge))
            vstep(lambda e: e.tensor_copy(keepm[:], cmp8b[:]))
            vstep(lambda e: e.tensor_tensor(dcol[:], pm1[:], pm2[:], ALU.add))
            vstep(lambda e: e.tensor_single_scalar(dc2[:], dcol[:], 1e-8, ALU.add))
            vstep(lambda e: e.reciprocal(rdc[:], dc2[:]))
            vstep(lambda e: e.tensor_tensor(kfs[:], keepm[:], pscr2[:], ALU.mult))
            vstep((lambda e, ci=ci: e.tensor_scalar(wts_tm[:, ci * 4:ci * 4 + 4],
                                                    kfs[:], rdc[:, 0:1], None, ALU.mult)))
        p.bank_release(4, 've', p.cnt['ve'])
        p.dump('xf0', x_bf[0][:])
        p.dump('logits', logits[:])
        p.dump('wts_tm', wts_tm[:])
        p.barrier()

    slabs_e = {nm2: Slabs(ctx, aps['w_l0f_' + nm2], kk) for nm2, kk in [
        ('e0upT', 1024), ('e0dnT', 2048), ('e1w1T', 1024), ('e1w2T', 1024),
        ('e1dnT', 2048), ('e2l1T', 1024), ('e2l2T', 2048), ('e2l3T', 2048),
        ('e2l4T', 2048), ('e3dnT', 1024), ('e3upT', 1024), ('e3outT', 2048)]}
    with ExitStack() as ees:
        hidA = sb(nc, ees, 'l0f_hidA', [128, 8192], BF16)
        hidB = sb(nc, ees, 'l0f_hidB', [128, 8192], BF16)
        wts_b = sb(nc, ees, 'l0f_wtsb', [128, T], F32)
        wrow = sb(nc, ees, 'l0f_wrow', [1, T], F32)
        hid_gate = {id(hidA): 0, id(hidB): 0}

        def hid_layer(slab, src_fn, KC, nm_t, act, out_t, tq, wait_extra=(), src_t=None):
            for m in range(nm_t):
                bi, tile, cast = slab.get(m * 128, 128)
                b = 6 + m % 2
                w0 = [('po', cast)] + p.bank_waits(b) + list(wait_extra)
                if m == 0 and hid_gate[id(out_t)]:
                    w0.append(('te', hid_gate[id(out_t)]))
                for k in range(KC):
                    p.op('te', (lambda e, k=k, b=b, tile=tile, src_fn=src_fn:
                                e.matmul(p.bank(b)[0:128, 0:512], tile(k), src_fn(k),
                                         start=(k == 0), stop=(k == KC - 1))),
                         waits=w0 if k == 0 else ())
                slab.done(bi, p.cnt['te'])
                hg = hid_gate[id(out_t)]
                p.op('se', (lambda e, m=m, b=b, out_t=out_t, act=act: e.activation(
                    out_t[:, m * 512:(m + 1) * 512], p.bank(b)[0:128, 0:512], act)),
                     waits=[('te', p.cnt['te'])] + ([('te', hg)] if hg else []))
                p.bank_release(b, 'se', p.cnt['se'])
            if src_t is not None:
                hid_gate[id(src_t)] = p.cnt['te']

        def weight_hid(out_t, nm_t, ew):
            for m in range(nm_t):
                p.op('ve', (lambda e, m=m, out_t=out_t, ew=ew: e.tensor_tensor(
                    out_t[:, m * 512:(m + 1) * 512], out_t[:, m * 512:(m + 1) * 512],
                    wts_b[:, ew * 512:(ew + 1) * 512], ALU.mult)),
                     waits=[('se', p.cnt['se']), ('ve', p.cnt['ve'])])

        def down_proj(slab, hid_t, KC, first, last):
            for fi in range(6):
                bi, tile, cast = slab.get(256 + fi * 128, 128)
                w0 = [('po', cast), ('ve', p.cnt['ve'])] + (p.bank_waits(fi) if first else [])
                for k in range(KC):
                    p.op('te', (lambda e, k=k, fi=fi, tile=tile, hid_t=hid_t:
                                e.matmul(p.bank(fi)[0:128, 0:512], tile(k),
                                         hid_t[:, k * 512:(k + 1) * 512],
                                         start=(first and k == 0),
                                         stop=(last and k == KC - 1),
                                         skip_group_check=True)),
                         waits=w0 if k == 0 else ())
                slab.done(bi, p.cnt['te'])
            hid_gate[id(hid_t)] = p.cnt['te']

        for tq in range(NT):
            tc0 = tq * 512
            xfq = lambda k, tc0=tc0: x_bf[k][:, tc0:tc0 + 512]
            vw = p.cnt['ve']
            for ew in range(4):
                for c4 in range(4):
                    ti = tq * 4 + c4
                    p.op('te', (lambda e, ew=ew, c4=c4, ti=ti: e.transpose(
                        p.bank(4 + ew)[0:1, c4 * 128:(c4 + 1) * 128],
                        wts_tm[0:128, ti * 4 + ew:ti * 4 + ew + 1], tt['ident_f32'][:])),
                         waits=([('ve', vw)] + p.bank_waits(4 + ew)) if c4 == 0 else ())
            te_w = p.cnt['te']
            for ew in range(4):
                p.op('ve', (lambda e, ew=ew: e.tensor_copy(
                    wrow[0:1, ew * 512:(ew + 1) * 512], p.bank(4 + ew)[0:1, 0:512])),
                     waits=[('te', te_w)])
                p.bank_release(4 + ew, 've', p.cnt['ve'])
            ve_wr = p.cnt['ve']
            for ew in range(4):
                p.op('te', (lambda e, ew=ew: e.matmul(
                    p.bank(4 + ew)[0:128, 0:512], tt['ones_row_f32'][:],
                    wrow[0:1, ew * 512:(ew + 1) * 512], start=True, stop=True)),
                     waits=[('ve', ve_wr)] + p.bank_waits(4 + ew))
            te_wb = p.cnt['te']
            for ew in range(4):
                p.op('ve', (lambda e, ew=ew: e.tensor_copy(
                    wts_b[:, ew * 512:(ew + 1) * 512], p.bank(4 + ew)[0:128, 0:512])),
                     waits=[('te', te_wb)])
                p.bank_release(4 + ew, 've', p.cnt['ve'])

            hid_layer(slabs_e['e0upT'], xfq, 8, 16, AF.Gelu, hidA, tq)
            if tq == 0:
                p.dump('hid0', hidA[:])
                p.dump('wtsb0', wts_b[:])
            weight_hid(hidA, 16, 0)
            down_proj(slabs_e['e0dnT'], hidA, 16, True, False)
            hid_layer(slabs_e['e1w1T'], xfq, 8, 16, AF.Silu, hidB, tq)
            for m in range(16):
                bi, tile, cast = slabs_e['e1w2T'].get(m * 128, 128)
                b = 6 + m % 2
                w0 = [('po', cast)] + p.bank_waits(b)
                for k in range(8):
                    p.op('te', (lambda e, k=k, b=b, tile=tile, xfq=xfq:
                                e.matmul(p.bank(b)[0:128, 0:512], tile(k), xfq(k),
                                         start=(k == 0), stop=(k == 7))),
                         waits=w0 if k == 0 else ())
                slabs_e['e1w2T'].done(bi, p.cnt['te'])
                hg = hid_gate[id(hidB)]
                p.op('ve', (lambda e, m=m, b=b: e.tensor_tensor(
                    hidB[:, m * 512:(m + 1) * 512], hidB[:, m * 512:(m + 1) * 512],
                    p.bank(b)[0:128, 0:512], ALU.mult)),
                     waits=[('te', p.cnt['te']), ('se', p.cnt['se'])])
                p.bank_release(b, 've', p.cnt['ve'])
            weight_hid(hidB, 16, 1)
            down_proj(slabs_e['e1dnT'], hidB, 16, False, False)
            hid_layer(slabs_e['e2l1T'], xfq, 8, 16, AF.Gelu, hidA, tq)
            hid_layer(slabs_e['e2l2T'],
                      (lambda k: hidA[:, k * 512:(k + 1) * 512]), 16, 16, AF.Gelu, hidB, tq,
                      wait_extra=[('se', p.cnt['se'])], src_t=hidA)
            hid_layer(slabs_e['e2l3T'],
                      (lambda k: hidB[:, k * 512:(k + 1) * 512]), 16, 16, AF.Gelu, hidA, tq,
                      wait_extra=[('se', p.cnt['se'])], src_t=hidB)
            weight_hid(hidA, 16, 2)
            down_proj(slabs_e['e2l4T'], hidA, 16, False, False)
            hid_layer(slabs_e['e3dnT'], xfq, 8, 8, AF.Gelu, hidB, tq)
            hid_layer(slabs_e['e3upT'],
                      (lambda k: hidB[:, k * 512:(k + 1) * 512]), 8, 16, AF.Gelu, hidA, tq,
                      wait_extra=[('se', p.cnt['se'])], src_t=hidB)
            weight_hid(hidA, 16, 3)
            down_proj(slabs_e['e3outT'], hidA, 16, False, True)
            for fi in range(6):
                p.op('ve', (lambda e, fi=fi, tc0=tc0: e.tensor_tensor(
                    delta[fi][:, tc0:tc0 + 512], delta[fi][:, tc0:tc0 + 512],
                    p.bank(fi)[0:128, 0:512], ALU.add)),
                     waits=[('te', p.cnt['te'])])
                p.bank_release(fi, 've', p.cnt['ve'])
            p.flush()
        p.barrier()

    for ff in range(NF):
        p.dump(f'premoe_x{ff}', x[ff][:])
    emit_postln(ctx, es, 'l0f_ln', x, x, eps=1e-5, dst_bf=x_bf,
                wait_src=[('ve', p.cnt['ve'])],
                src2=[None, None] + [delta[i] for i in range(6)], src_scale=2.0)


def emit_l1_ffn(ctx, es, x, x_bf):
    nc, p = ctx.nc, ctx.p
    tt = ctx.t
    aps = ctx.aps
    zerocol = tt['fconsts'][:, 5:6]
    delta = [sb(nc, es, f'l1f_delta{i}', [128, T], BF16) for i in range(NF)]
    SLOTS = [(4, 0), (4, 32), (4, 64), (5, 0), (5, 32), (5, 64), (6, 0), (6, 32)]
    sh_gate = [0, 0]

    def build_sh(l, dcur):
        for f in range(NF):
            w = [('ve', p.cnt['ve']), ('te', p.cnt['te']), ('se', p.cnt['se']),
                 ('po', p.cnt['po'])]
            if f < 4:
                if dcur is None:
                    continue
                p.op('ve', (lambda e, f=f, dcur=dcur: e.tensor_tensor(
                    x_bf[f][:], x[f][:], dcur[f][:], ALU.add)), waits=w)
            else:
                p.op('ve', (lambda e, f=f: e.memset(x_bf[f][:, 0:1], 0.0)), waits=w)
                if dcur is None:
                    p.op('po', (lambda e, f=f: e.tensor_copy(
                        x_bf[f][:, 1:T], x[f][:, 0:T - 1])), waits=[('ve', p.cnt['ve'])])
                else:
                    p.op('ve', (lambda e, f=f, dcur=dcur: e.tensor_tensor(
                        x_bf[f][:, 1:T], x[f][:, 0:T - 1], dcur[f][:, 0:T - 1],
                        ALU.add)))

    def qk_l2rms(out_t, gcol, te_proj, vstage, rstd_scr, bsc):
        p.op('se', (lambda e: e.activation(vstage[:], p.bank_span(0, 4), AF.Square)),
             waits=[('te', te_proj)] + p.bank_waits(4) + p.bank_waits(5) + p.bank_waits(6))
        sq_done = p.cnt['se']
        for si, (h2, ci) in enumerate([(a, b) for a in range(2) for b in range(4)]):
            b, r = SLOTS[si]
            p.op('te', (lambda e, h2=h2, ci=ci, b=b, r=r:
                        e.matmul(p.bank(b)[r:r + 1, 0:512],
                                 tt['ones_col_bf'][h2 * 64:(h2 + 1) * 64, 0:1],
                                 vstage[h2 * 64:(h2 + 1) * 64, ci * 512:(ci + 1) * 512],
                                 start=True, stop=True)),
                 waits=[('se', sq_done)] if si == 0 else ())
        te_norm = p.cnt['te']
        for si in range(8):
            b, r = SLOTS[si]
            p.op('se', (lambda e, b=b, r=r: e.activation(
                p.bank(b)[r:r + 1, 0:512], p.bank(b)[r:r + 1, 0:512], AF.Ln,
                bias=1e-24)),
                 waits=([('te', te_norm)] if si == 0 else [('se', p.cnt['se'])]))
            p.op('se', (lambda e, b=b, r=r: e.activation(
                p.bank(b)[r:r + 1, 0:512], p.bank(b)[r:r + 1, 0:512], AF.Exp,
                scale=-0.5)), waits=[('se', p.cnt['se'])])
        se_rows = p.cnt['se']

        def slot_row(b, r):
            if b == 6:
                return bsc[r:r + 1, 0:512]
            j = (4, 5).index(b)
            return rstd_scr[r:r + 1, j * 512:(j + 1) * 512]
        for si in range(8):
            b, r = SLOTS[si]
            p.op('ve', (lambda e, b=b, r=r: e.tensor_copy(
                slot_row(b, r), p.bank(b)[r:r + 1, 0:512])),
                 waits=[('se', se_rows)] if si == 0 else ())
        ve_rows = p.cnt['ve']
        for b in (4, 5):
            p.bank_release(b, 've', ve_rows)
        for si, (h2, ci) in enumerate([(a, b) for a in range(2) for b in range(4)]):
            b, r = SLOTS[si]
            p.op('te', (lambda e, h2=h2, ci=ci, b=b, r=r:
                        e.matmul(p.bank(4 + ci)[h2 * 64:h2 * 64 + 64, 0:512],
                                 tt['ones128'][r:r + 1, 0:64],
                                 slot_row(b, r),
                                 start=True, stop=True)),
                 waits=([('ve', ve_rows)] + p.bank_waits(4) + p.bank_waits(5)
                        + p.bank_waits(6) + p.bank_waits(7)) if si == 0 else ())
        te_b = p.cnt['te']
        for ci in range(NT):
            p.op('ve', (lambda e, ci=ci: e.tensor_copy(bsc[:], p.bank(4 + ci)[:, 0:512])),
                 waits=([('te', te_b)] if ci == 0 else []) + [('se', p.cnt['se']),
                                                              ('ve', p.cnt['ve'])])
            p.bank_release(4 + ci, 've', p.cnt['ve'])
            p.op('ve', (lambda e, ci=ci: e.tensor_tensor(
                bsc[:], p.bank(ci)[:, 0:512], bsc[:], ALU.mult)),
                 waits=[('ve', p.cnt['ve'])])
            p.bank_release(ci, 've', p.cnt['ve'])
            p.op('se', (lambda e, ci=ci, out_t=out_t, gcol=gcol: e.activation(
                out_t[:, ci * 512:(ci + 1) * 512], bsc[:], AF.Identity,
                bias=zerocol, scale=gcol)), waits=[('ve', p.cnt['ve'])])
        return p.cnt['se']

    for l in range(2):
        les1 = ExitStack()
        q_t = [sb(nc, les1, f'l1f_q{i}_{l}', [128, T], BF16) for i in range(4)]
        k_t = [sb(nc, les1, f'l1f_k{i}_{l}', [128, T], BF16) for i in range(4)]
        vtm = sb(nc, les1, f'l1f_vtm_{l}', [128, 16 * 512], BF16)
        vstage = sb(nc, les1, f'l1f_vstage_{l}', [128, T], BF16)
        tabb = sb(nc, les1, f'l1f_tab_{l}', [128, 1024], BF16)
        rstd_scr = sb(nc, les1, f'l1f_rstd_{l}', [128, 1024], F32)
        bsc = sb(nc, les1, f'l1f_bsc_{l}', [128, 512], F32)
        cols = sb(nc, les1, f'l1f_cols_{l}', [128, 8], F32)
        otok = sb(nc, les1, f'l1f_otok_{l}', [128, 64], BF16)
        sq_qkv = Slabs(ctx, aps[f'w_l1f_qkv{l}'], D)
        sq_wo = Slabs(ctx, aps[f'w_l1f_wo{l}'], 512)
        sq_w1 = Slabs(ctx, aps[f'w_l1f_w1{l}'], D)
        sq_w2a = Slabs(ctx, aps[f'w_l1f_w2{l}'], 1408)
        sq_w2b = Slabs(ctx, aps[f'w_l1f_w2{l}'], 1408, k0=1408)
        build_sh(l, None if l == 0 else delta)
        sh_done = [('po', p.cnt['po']), ('ve', p.cnt['ve'])]
        for gi in range(12):
            bi, tile, cast = sq_qkv.get(gi * 128, 128)
            for ci in range(NT):
                w0 = [('po', cast)] + p.bank_waits(ci) + (sh_done if gi == 0 else [])
                for k in range(8):
                    p.op('te', (lambda e, k=k, ci=ci, tile=tile:
                                e.matmul(p.bank(ci)[0:128, 0:512], tile(k),
                                         x_bf[k][:, ci * 512:(ci + 1) * 512],
                                         start=(k == 0), stop=(k == 7))),
                         waits=w0 if k == 0 else ())
            sq_qkv.done(bi, p.cnt['te'])
            if gi < 4:
                qk_l2rms(q_t[gi], ctx.col(f'l1f_qs{l}'), p.cnt['te'], vstage, rstd_scr, bsc)
            elif gi < 8:
                qk_l2rms(k_t[gi - 4], ctx.col(f'l1f_ks{l}'), p.cnt['te'], vstage, rstd_scr, bsc)
            else:
                vt = gi - 8
                for ci in range(NT):
                    p.op('ve', (lambda e, ci=ci: e.tensor_copy(
                        vstage[:, ci * 512:(ci + 1) * 512], p.bank(ci)[:, 0:512])),
                         waits=[('te', p.cnt['te'])] if ci == 0 else ())
                    p.bank_release(ci, 've', p.cnt['ve'])
                ve_v = p.cnt['ve']
                for m in range(16):
                    b = 4 + m // 8
                    p.op('te', (lambda e, m=m, b=b: e.transpose(
                        p.bank_bf(b)[:, (m % 8) * 128:(m % 8) * 128 + 128],
                        vstage[:, m * 128:(m + 1) * 128], tt['ident_bf'][:])),
                         waits=([('ve', ve_v)] + p.bank_waits(4) + p.bank_waits(5))
                         if m == 0 else ())
                    if m % 8 == 7:
                        p.op('ve', (lambda e, m=m, b=b, vt=vt: e.tensor_copy(
                            vtm[:].rearrange('p (tt c) -> p tt c', c=512)
                            [:, m - 7:m + 1, vt * 128:(vt + 1) * 128],
                            p.bank_bf(b)[:, 0:1024]
                            .rearrange('p (tt c) -> p tt c', c=128))),
                             waits=[('te', p.cnt['te'])])
                        p.bank_release(b, 've', p.cnt['ve'])
        for ci in range(NT):
            for side in range(2):
                p.dma('po', tabb[:],
                      aps['c_l1f_tab'][:, ci * 2048 + side * 1024:
                                       ci * 2048 + side * 1024 + 1024],
                      waits=[('te', p.cnt['te']), ('ve', p.cnt['ve'])])
                tl = p.cnt['gq']
                for f in range(4):
                    qt_ = q_t[f] if side == 0 else k_t[f]
                    p.op('te', (lambda e, qt_=qt_, ci=ci: e.matmul(
                        p.bank(6)[:, 0:512],
                        tt['permrot_bf'][:], qt_[:, ci * 512:(ci + 1) * 512],
                        start=True, stop=True)),
                         waits=[('gq', tl), ('se', p.cnt['se']),
                                ('ve', p.cnt['ve'])] + p.bank_waits(6))
                    te_pm = p.cnt['te']
                    p.op('ve', (lambda e, qt_=qt_, ci=ci: e.tensor_tensor(
                        bsc[:], qt_[:, ci * 512:(ci + 1) * 512], tabb[:, 0:512],
                        ALU.mult)),
                         waits=[('gq', tl), ('ve', p.cnt['ve']), ('se', p.cnt['se'])])
                    p.op('ve', (lambda e, qt_=qt_, ci=ci: e.tensor_tensor(
                        qt_[:, ci * 512:(ci + 1) * 512], p.bank(6)[:, 0:512],
                        tabb[:, 512:1024], ALU.mult)),
                         waits=[('te', te_pm), ('ve', p.cnt['ve'])])
                    p.op('ve', (lambda e, qt_=qt_, ci=ci: e.tensor_tensor(
                        qt_[:, ci * 512:(ci + 1) * 512],
                        qt_[:, ci * 512:(ci + 1) * 512], bsc[:], ALU.add)),
                         waits=[('ve', p.cnt['ve'])])
                    p.bank_release(6, 've', p.cnt['ve'])
        negmax = cols[:, 0:1]; sums = cols[:, 1:2]; rsum = cols[:, 2:3]
        rsum_read = 0
        otok_read = 0
        pt_read = 0
        for h2x in range(8):
            ft, h2 = h2x // 2, h2x % 2
            qtile = q_t[ft]; ktile = k_t[ft]
            for qt in range(16):
                nk = min(qt + 1, 5)
                kc0 = qt + 1 - nk
                kw = nk * 128
                w0 = [('se', p.cnt['se']), ('ve', p.cnt['ve'])]
                w0 += p.bank_waits(0) + p.bank_waits(1)
                nkc = (kw + 511) // 512
                for c in range(nkc):
                    c0, c1 = c * 512, min(kw, (c + 1) * 512)
                    p.op('te', (lambda e, h2=h2, qt=qt, c0=c0, c1=c1, kc0=kc0,
                                qtile=qtile, ktile=ktile:
                                e.matmul(p.psum.ap()[0:128, c0:c1],
                                         qtile[h2 * 64:h2 * 64 + 64, qt * 128:(qt + 1) * 128],
                                         ktile[h2 * 64:h2 * 64 + 64,
                                               kc0 * 128 + c0:kc0 * 128 + c1],
                                         start=True, stop=True)),
                         waits=w0 if c == 0 else ())
                p.op('te', (lambda e, kw=kw: e.matmul(
                    p.psum.ap()[0:128, kw - 128:kw], tt['ident_bf'][:],
                    tt['mask_causal_bf'][:], start=False, stop=True,
                    skip_group_check=True)))
                if nk == 5:
                    p.op('te', (lambda e: e.matmul(
                        p.psum.ap()[0:128, 0:128], tt['ident_bf'][:],
                        tt['mask_keep_ge_bf'][:], start=False, stop=True,
                        skip_group_check=True)))
                te_s = p.cnt['te']
                p.op('ve', (lambda e, kw=kw: e.tensor_reduce(
                    negmax[:], p.psum.ap()[0:128, 0:kw], AX.X, ALU.max, negate=True)),
                     waits=[('te', te_s)])
                ve_max = p.cnt['ve']
                p.op('se', (lambda e, kw=kw: e.activation(
                    vstage[:, 0:kw], p.psum.ap()[0:128, 0:kw], AF.Exp,
                    bias=negmax[:, 0:1], accum_out=sums[:, 0:1])),
                     waits=[('ve', ve_max)]
                     + ([('ve', rsum_read)] if rsum_read else [])
                     + ([('te', pt_read)] if pt_read else []))
                se_exp = p.cnt['se']
                for b2 in range(nkc):
                    p.bank_release(b2, 'se', se_exp)
                p.op('ve', (lambda e: e.reciprocal(rsum[:], sums[:, 0:1])),
                     waits=[('se', se_exp)])
                rsum_done = p.cnt['ve']
                rsum_read = rsum_done
                for ki in range(nk):
                    p.op('te', (lambda e, ki=ki: e.transpose(
                        p.bank_bf(4)[:, ki * 128:ki * 128 + 128],
                        vstage[:, ki * 128:(ki + 1) * 128], tt['ident_bf'][:])),
                         waits=([('se', se_exp)] + p.bank_waits(4)) if ki == 0 else ())
                te_pt = p.cnt['te']
                pt_read = te_pt
                p.op('ve', (lambda e, kw=kw: e.tensor_copy(
                    vstage[:, 1024:1024 + kw], p.bank_bf(4)[:, 0:kw])), waits=[('te', te_pt)])
                ve_pt = p.cnt['ve']
                p.bank_release(4, 've', ve_pt)
                for ki in range(nk):
                    p.op('te', (lambda e, ki=ki, kc0=kc0, h2x=h2x, last=(ki == nk - 1):
                                e.matmul(p.bank(6)[0:128, 0:64],
                                         vstage[:, 1024 + ki * 128:1024 + (ki + 1) * 128],
                                         vtm[:, (kc0 + ki) * 512 + h2x * 64:
                                             (kc0 + ki) * 512 + h2x * 64 + 64],
                                         start=(ki == 0), stop=last)),
                         waits=([('ve', ve_pt)] + p.bank_waits(6)) if ki == 0 else ())
                te_pv = p.cnt['te']
                p.op('ve', (lambda e: e.tensor_scalar(
                    otok[:], p.bank(6)[0:128, 0:64], rsum[:, 0:1], None, ALU.mult)),
                     waits=[('te', te_pv), ('ve', rsum_done)]
                     + ([('te', otok_read)] if otok_read else []))
                ve_o = p.cnt['ve']
                p.bank_release(6, 've', ve_o)
                p.op('te', (lambda e, qt=qt: e.transpose(
                    p.bank_bf(7)[0:64, (qt % 8) * 128:(qt % 8) * 128 + 128],
                    otok[:], tt['ident_bf'][:])),
                     waits=[('ve', ve_o)] + (p.bank_waits(7) if qt % 8 == 0 else []))
                otok_read = p.cnt['te']
                if qt % 8 == 7:
                    p.op('ve', (lambda e, h2=h2, ft=ft, qt=qt: e.tensor_copy(
                        x_bf[ft][h2 * 64:h2 * 64 + 64, (qt - 7) * 128:(qt + 1) * 128],
                        p.bank_bf(7)[0:64, 0:1024])),
                         waits=[('te', otok_read)])
                    p.bank_release(7, 've', p.cnt['ve'])
        def evac_wo(gi, m0, mw, ni, c0, cw, bidx, pap, te_cnt, l=l):
            if l == 0:
                p.op('ve', (lambda e, gi=gi, c0=c0, cw=cw, pap=pap: e.tensor_copy(
                    delta[gi][:, c0:c0 + cw], pap[0:128, 0:cw])),
                     waits=[('te', te_cnt)])
            else:
                p.op('ve', (lambda e, gi=gi, c0=c0, cw=cw, pap=pap: e.tensor_tensor(
                    delta[gi][:, c0:c0 + cw], delta[gi][:, c0:c0 + cw],
                    pap[0:128, 0:cw], ALU.add)),
                     waits=[('te', te_cnt)])
            p.bank_release(bidx, 've', p.cnt['ve'])
        emit_linear(ctx, es, f'l1f_wo{l}', aps[f'w_l1f_wo{l}'], 512, D,
                    x_bf[0:4], evac_wo, wait_rhs=[('ve', p.cnt['ve'])])
        p.barrier()
        les1.close()

        def shift_builder(f, rt, waits, l=l):
            sc = delta[f]
            if f < 4:
                p.op('ve', (lambda e, f=f, rt=rt: e.tensor_tensor(
                    rt[:], x[f][:], delta[f][:], ALU.add)), waits=waits)
            else:
                p.op('ve', (lambda e, f=f, rt=rt: e.tensor_tensor(
                    rt[:, 1:T], x[f][:, 0:T - 1], delta[f][:, 0:T - 1], ALU.add)),
                     waits=waits)
                p.op('ve', (lambda e, rt=rt: e.memset(rt[:, 0:1], 0.0)),
                     waits=[('ve', p.cnt['ve'])])
            return p.cnt['ve']
        emit_postln(ctx, es, f'l1f_ln{l}', x, None, eps=1e-5, dst_bf=x_bf,
                    gamma=[ctx.col(f'l1f_lng{l}', f) for f in range(NF)],
                    beta=[ctx.col(f'l1f_lnb{l}', f) for f in range(NF)],
                    wait_src=[('ve', p.cnt['ve']), ('se', p.cnt['se'])],
                    src_builder=shift_builder)

        with ExitStack() as fes:
            agA = sb(nc, fes, f'l1f_agA{l}', [128, 22 * 512], BF16)
            gg = sb(nc, fes, f'l1f_gg{l}', [128, 22 * 512], BF16)
            hid_gate = {id(agA): None, id(gg): None}
            for tq in range(NT):
                tc0 = tq * 512
                for part, out_t, act, bcol in ((0, agA, AF.Identity, f'l1f_b1a{l}'),
                                               (1, gg, AF.Gelu, f'l1f_b1g{l}')):
                    for m in range(22):
                        bi, tile, cast = sq_w1.get(part * 2816 + m * 128, 128)
                        b = 6 + m % 2
                        w0 = [('po', cast)] + p.bank_waits(b)
                        if m == 0 and hid_gate[id(out_t)]:
                            w0.append(hid_gate[id(out_t)])
                        for k in range(8):
                            p.op('te', (lambda e, k=k, b=b, tile=tile, tc0=tc0:
                                        e.matmul(p.bank(b)[0:128, 0:512], tile(k),
                                                 x_bf[k][:, tc0:tc0 + 512],
                                                 start=(k == 0), stop=(k == 7))),
                                 waits=w0 if k == 0 else ())
                        sq_w1.done(bi, p.cnt['te'])
                        p.op('se', (lambda e, m=m, b=b, out_t=out_t, act=act, bcol=bcol:
                                    e.activation(out_t[:, m * 512:(m + 1) * 512],
                                                 p.bank(b)[0:128, 0:512], act,
                                                 bias=ctx.col(bcol, m))),
                             waits=[('te', p.cnt['te'])])
                        p.bank_release(b, 'se', p.cnt['se'])
                for m in range(22):
                    p.op('ve', (lambda e, m=m: e.tensor_tensor(
                        agA[:, m * 512:(m + 1) * 512], agA[:, m * 512:(m + 1) * 512],
                        gg[:, m * 512:(m + 1) * 512], ALU.mult)),
                         waits=[('se', p.cnt['se'])] if m == 0 else ())
                ve_h2 = p.cnt['ve']
                for fi in range(NF):
                    bi1, tile1, cast1 = sq_w2a.get(fi * 128, 128)
                    b = fi % 6
                    w0 = [('po', cast1), ('ve', ve_h2)] + p.bank_waits(b)
                    for k in range(11):
                        p.op('te', (lambda e, k=k, b=b, tile=tile1:
                                    e.matmul(p.bank(b)[0:128, 0:512], tile(k),
                                             agA[:, k * 512:(k + 1) * 512],
                                             start=(k == 0), stop=False)),
                             waits=w0 if k == 0 else ())
                    sq_w2a.done(bi1, p.cnt['te'])
                    bi2, tile2, cast2 = sq_w2b.get(fi * 128, 128)
                    for k in range(11):
                        p.op('te', (lambda e, k=k, b=b, tile=tile2:
                                    e.matmul(p.bank(b)[0:128, 0:512], tile(k),
                                             agA[:, (11 + k) * 512:(12 + k) * 512],
                                             start=False, stop=(k == 10))),
                             waits=[('po', cast2)] if k == 0 else ())
                    sq_w2b.done(bi2, p.cnt['te'])
                    p.op('ve', (lambda e, fi=fi, b=b, tc0=tc0, l=l:
                                e.scalar_tensor_tensor(
                                    delta[fi][:, tc0:tc0 + 512], p.bank(b)[0:128, 0:512],
                                    ctx.col(f'l1f_b2{l}', fi), delta[fi][:, tc0:tc0 + 512],
                                    ALU.add, ALU.add)),
                         waits=[('te', p.cnt['te'])])
                    p.bank_release(b, 've', p.cnt['ve'])
                hid_gate[id(agA)] = ('te', p.cnt['te'])
                hid_gate[id(gg)] = ('ve', ve_h2)
                p.flush()
        p.barrier()
    emit_postln(ctx, es, 'l1f_lnout', x, x, eps=1e-5, dst_bf=x_bf,
                wait_src=[('ve', p.cnt['ve'])],
                src2=[delta[i] for i in range(NF)], src_scale=2.0)


def emit_l0_attn(ctx, es, x, x_bf):
    nc, p = ctx.nc, ctx.p
    tt = ctx.t
    aps = ctx.aps
    TWO_PI = 6.283185307179586

    SCAL = sb(nc, es, 'l0a_scal', [128, T], F32)
    B_bf = sb(nc, es, 'l0a_Bbf', [64, T], BF16)
    C_bf = sb(nc, es, 'l0a_Cbf', [64, T], BF16)
    theta = sb(nc, es, 'l0a_theta', [32, T], F32)

    def evac_theta(gi, m0, mw, ni, c0, cw, bidx, pap, te_cnt):
        p.op('ve', (lambda e, c0=c0, cw=cw, pap=pap: e.tensor_copy(
            theta[:, c0:c0 + cw], pap[0:32, 0:cw])), waits=[('te', te_cnt)])
        p.bank_release(bidx, 've', p.cnt['ve'])
    emit_linear(ctx, es, 'l0a_th', aps['w_l0a_theta'], D, 32, x_bf, evac_theta,
                m_groups=[(0, 32)], wait_rhs=[('po', p.cnt['po'])])

    emit_postln(ctx, es, 'l0a_ln', x, None, eps=1e-6, dst_bf=x_bf,
                gamma=[ctx.col('l0a_lng', f) for f in range(NF)],
                beta=[ctx.col('l0a_lnb', f) for f in range(NF)],
                wait_src=[('ve', p.cnt['ve']), ('te', p.cnt['te'])])

    sq_proj = Slabs(ctx, aps['w_l0a_proj'], D)
    sq_out = Slabs(ctx, aps['w_l0a_out'], 2048)
    SLOTS = [(4, 0), (4, 32), (4, 64), (5, 0), (5, 32), (5, 64), (6, 0), (6, 32)]
    with ExitStack() as tes:
        BC_raw = sb(nc, tes, 'l0a_bcraw', [128, T], F32)
        TSCAL = sb(nc, tes, 'l0a_tscal', [16, T], F32)
        LAM16 = sb(nc, tes, 'l0a_lam16', [16, T], F32)
        ONES32 = sb(nc, tes, 'l0a_ones32', [32, T], F32)
        phib = sb(nc, tes, 'l0a_phi', [32, T], F32)
        sinb = sb(nc, tes, 'l0a_sin', [32, T], F32)
        ni32 = sb(nc, tes, 'l0a_ni32', [32, T], I32)
        rr32 = ONES32
        cs2 = sb(nc, tes, 'l0a_cs2', [128, T], BF16)
        ss2 = sb(nc, tes, 'l0a_ss2', [128, T], BF16)
        colscr = sb(nc, tes, 'l0a_colscr', [128, 8], F32)
        rowscr = phib[0:1, 0:T]
        ni32f = ni32[:].bitcast(F32)
        bcsc = sb(nc, tes, 'l0a_bcsc', [128, 512], F32)
        bcnbf = sb(nc, tes, 'l0a_bcnbf', [128, 512], BF16)
        perm32 = sb(nc, tes, 'l0a_perm32', [128, 128], BF16)
        p.dma('po', perm32[:], aps['c_perm32_bf'][:, :])
        perm_gq = p.cnt['gq']
        p.op('ve', lambda e: e.memset(ONES32[:], 1.0))

        def evac_bc(gi, m0, mw, ni, c0, cw, bidx, pap, te_cnt):
            p.op('ve', (lambda e, c0=c0, cw=cw, pap=pap: e.tensor_copy(
                BC_raw[:, c0:c0 + cw], pap[0:128, 0:cw])), waits=[('te', te_cnt)])
            p.bank_release(bidx, 've', p.cnt['ve'])
        emit_linear(ctx, tes, 'l0a_bc', aps['w_l0a_proj'], D, 128, x_bf, evac_bc,
                    m_groups=[(4096, 128)], wait_rhs=[('se', p.cnt['se'])])

        def evac_dt(gi, m0, mw, ni, c0, cw, bidx, pap, te_cnt):
            p.op('ve', (lambda e, c0=c0, cw=cw, pap=pap: e.tensor_copy(
                TSCAL[0:16, c0:c0 + cw], pap[0:16, 0:cw])), waits=[('te', te_cnt)])
            p.bank_release(bidx, 've', p.cnt['ve'])
        emit_linear(ctx, tes, 'l0a_dtm', aps['w_l0a_proj'], D, 16, x_bf, evac_dt,
                    m_groups=[(4224, 16)])

        def evac_lam(gi, m0, mw, ni, c0, cw, bidx, pap, te_cnt):
            p.op('se', (lambda e, c0=c0, cw=cw, pap=pap: e.activation(
                LAM16[0:16, c0:c0 + cw], pap[0:16, 0:cw], AF.Sigmoid)),
                 waits=[('te', te_cnt)])
            p.bank_release(bidx, 'se', p.cnt['se'])
        emit_linear(ctx, tes, 'l0a_lamm', aps['w_l0a_proj'], D, 16, x_bf, evac_lam,
                    m_groups=[(4240, 16)])

        dt = TSCAL[0:16, 0:T]
        lam = LAM16[0:16, 0:T]
        scr16 = ni32[0:16, 0:T].bitcast(F32)

        def vst(fn, w=()):
            p.op('ve', fn, waits=[('ve', p.cnt['ve'])] + list(w))
        vst(lambda e: e.tensor_scalar(dt, dt, ctx.col('l0a_dtb')[0:16, 0:1], None, ALU.add),
            w=[('se', p.cnt['se'])])
        vst(lambda e: e.tensor_single_scalar(scr16, dt, -1.0, ALU.mult))
        vst(lambda e: e.tensor_tensor(scr16, dt, scr16, ALU.min))
        p.op('se', (lambda e: e.activation(scr16, scr16, AF.Exp)),
             waits=[('ve', p.cnt['ve'])])
        p.op('se', (lambda e: e.activation(scr16, scr16, AF.Ln, bias=1.0)),
             waits=[('se', p.cnt['se'])])
        vst(lambda e: e.tensor_single_scalar(dt, dt, 0.0, ALU.max),
            w=[('se', p.cnt['se'])])
        vst(lambda e: e.tensor_tensor(dt, dt, scr16, ALU.add))
        dt_done = p.cnt['ve']
        p.op('se', (lambda e: e.activation(colscr[:, 0:1], ctx.col('l0a_Alog'),
                                           AF.Exp)), waits=[('ve', dt_done)])
        vst(lambda e: e.tensor_single_scalar(colscr[:, 0:1], colscr[:, 0:1],
                                             -1.0, ALU.mult), w=[('se', p.cnt['se'])])
        vst(lambda e: e.tensor_scalar(scr16, dt, colscr[0:16, 0:1], None, ALU.mult))
        vst(lambda e: e.tensor_tensor_scan(SCAL[0:16, 0:T], ONES32[0:16, 0:T], scr16,
                                           0.0, ALU.mult, ALU.add))
        for ci in range(NT):
            p.op('te', (lambda e, ci=ci: e.matmul(
                p.bank(0)[0:1, 0:512], tt['ones_col_f32'][0:16, 0:1],
                dt[:, ci * 512:(ci + 1) * 512], start=True, stop=True)),
                 waits=[('ve', p.cnt['ve'])] + p.bank_waits(0))
            p.op('ve', (lambda e, ci=ci: e.tensor_single_scalar(
                rowscr[:, ci * 512:(ci + 1) * 512], p.bank(0)[0:1, 0:512],
                1.0 / 16.0, ALU.mult)), waits=[('te', p.cnt['te'])])
            p.bank_release(0, 've', p.cnt['ve'])
        for ci in range(NT):
            p.op('te', (lambda e, ci=ci: e.matmul(
                p.bank(1)[0:32, 0:512], tt['ones_row_f32'][:, 0:32],
                rowscr[0:1, ci * 512:(ci + 1) * 512], start=True, stop=True)),
                 waits=[('ve', p.cnt['ve'])] + p.bank_waits(1))
            p.op('ve', (lambda e, ci=ci: e.tensor_tensor(
                theta[:, ci * 512:(ci + 1) * 512], theta[:, ci * 512:(ci + 1) * 512],
                p.bank(1)[0:32, 0:512], ALU.mult)),
                 waits=[('te', p.cnt['te']), ('ve', p.cnt['ve'])])
            p.bank_release(1, 've', p.cnt['ve'])
        vst(lambda e: e.tensor_tensor_scan(phib[:], ONES32[:], theta[:],
                                           0.0, ALU.mult, ALU.add))
        for dst, shift in ((sinb, 0.0), (theta, 0.25)):
            vst(lambda e, dst=dst, shift=shift: e.tensor_scalar(
                dst[:], phib[:], 1.0 / TWO_PI, shift + 0.5 + 16384.0,
                ALU.mult, ALU.add))
            vst(lambda e, dst=dst: e.tensor_copy(ni32[:], dst[:]))
            vst(lambda e: e.tensor_copy(rr32[:], ni32[:]))
            vst(lambda e, dst=dst: e.tensor_tensor(dst[:], dst[:], rr32[:],
                                                   ALU.subtract))
            vst(lambda e, dst=dst: e.tensor_single_scalar(ni32[:], dst[:], 0.0,
                                                          ALU.is_lt))
            vst(lambda e: e.tensor_copy(rr32[:], ni32[:]))
            vst(lambda e, dst=dst: e.tensor_tensor(dst[:], dst[:], rr32[:], ALU.add))
            vst(lambda e, dst=dst: e.tensor_scalar(dst[:], dst[:], TWO_PI,
                                                   -3.141592653589793,
                                                   ALU.mult, ALU.add))
            p.op('se', (lambda e, dst=dst: e.activation(dst[:], dst[:], AF.Sin)),
                 waits=[('ve', p.cnt['ve'])])
        trig_se = p.cnt['se']
        p.op('ve', (lambda e: e.tensor_copy(cs2[0:32, 0:T], theta[:])),
             waits=[('se', trig_se), ('ve', p.cnt['ve'])])
        p.op('ve', (lambda e: e.tensor_copy(cs2[32:64, 0:T], cs2[0:32, 0:T])),
             waits=[('ve', p.cnt['ve'])])
        p.op('ve', (lambda e: e.tensor_copy(cs2[64:96, 0:T], cs2[0:32, 0:T])))
        p.op('ve', (lambda e: e.tensor_single_scalar(ss2[0:32, 0:T], sinb[:], -1.0,
                                                     ALU.mult)))
        p.op('ve', (lambda e: e.tensor_copy(ss2[32:64, 0:T], sinb[:])))
        p.op('ve', (lambda e: e.tensor_single_scalar(ss2[64:96, 0:T], sinb[:], -1.0,
                                                     ALU.mult)))
        vrep = p.cnt['ve']
        p.dma('po', cs2[96:128, 0:T], cs2[0:32, 0:T], waits=[('ve', vrep)])
        p.dma('po', ss2[96:128, 0:T], ss2[32:64, 0:T])
        rep_gq = p.cnt['gq']

        for ci in range(NT):
            p.op('se', (lambda e, ci=ci: e.activation(
                bcnbf[:], BC_raw[:, ci * 512:(ci + 1) * 512], AF.Square)),
                 waits=[('ve', p.cnt['ve']), ('te', p.cnt['te']),
                        ('se', p.cnt['se'])])
            sq_se = p.cnt['se']
            for h2 in range(2):
                b, r = SLOTS[h2 * 4 + ci]
                p.op('te', (lambda e, h2=h2, b=b, r=r: e.matmul(
                    p.bank(b)[r:r + 1, 0:512],
                    tt['ones_col_bf'][h2 * 64:(h2 + 1) * 64, 0:1],
                    bcnbf[h2 * 64:(h2 + 1) * 64, 0:512], start=True, stop=True)),
                     waits=[('se', sq_se)] + (p.bank_waits(b) if ci == 0 else []))
        te_norm = p.cnt['te']
        for si in range(8):
            b, r = SLOTS[si]
            p.op('se', (lambda e, b=b, r=r: e.activation(
                p.bank(b)[r:r + 1, 0:512], p.bank(b)[r:r + 1, 0:512], AF.Ln,
                scale=1.0 / 64.0, bias=1e-06)),
                 waits=([('te', te_norm)] if si == 0 else [('se', p.cnt['se'])]))
            p.op('se', (lambda e, b=b, r=r: e.activation(
                p.bank(b)[r:r + 1, 0:512], p.bank(b)[r:r + 1, 0:512], AF.Exp,
                scale=-0.5)), waits=[('se', p.cnt['se'])])
        se_rows = p.cnt['se']

        def slot_row(b, r):
            i = SLOTS.index((b, r))
            if i < 4:
                return phib[0:1, i * 512:(i + 1) * 512]
            return ni32f[0:1, (i - 4) * 512:(i - 3) * 512]
        for si in range(8):
            b, r = SLOTS[si]
            p.op('ve', (lambda e, b=b, r=r: e.tensor_copy(
                slot_row(b, r), p.bank(b)[r:r + 1, 0:512])),
                 waits=[('se', se_rows)] if si == 0 else ())
        ve_rows = p.cnt['ve']
        for b in (4, 5, 6):
            p.bank_release(b, 've', ve_rows)
        for si in range(8):
            h2, ci = si // 4, si % 4
            b, r = SLOTS[si]
            p.op('te', (lambda e, h2=h2, ci=ci, b=b, r=r: e.matmul(
                p.bank(4 + ci)[h2 * 64:h2 * 64 + 64, 0:512],
                tt['ones128'][0:1, 0:64], slot_row(b, r), start=True, stop=True)),
                 waits=([('ve', ve_rows)] + p.bank_waits(4) + p.bank_waits(5)
                        + p.bank_waits(6) + p.bank_waits(7)) if si == 0 else ())
        te_b = p.cnt['te']
        for ci in range(NT):
            p.op('ve', (lambda e, ci=ci: e.tensor_tensor(
                bcsc[:], BC_raw[:, ci * 512:(ci + 1) * 512], p.bank(4 + ci)[:, 0:512],
                ALU.mult)),
                 waits=([('te', te_b)] if ci == 0 else []) + [('se', p.cnt['se']),
                                                              ('ve', p.cnt['ve'])])
            p.bank_release(4 + ci, 've', p.cnt['ve'])
            p.op('se', (lambda e, ci=ci: e.activation(
                BC_raw[:, ci * 512:(ci + 1) * 512], bcsc[:], AF.Identity,
                bias=ctx.col('l0a_bcb'), scale=ctx.col('l0a_bcs'))),
                 waits=[('ve', p.cnt['ve'])])
        bcn_se = p.cnt['se']

        for ci in range(NT):
            c0 = ci * 512
            p.op('ve', (lambda e, c0=c0: e.tensor_copy(
                bcnbf[:], BC_raw[:, c0:c0 + 512])),
                 waits=[('se', bcn_se), ('ve', p.cnt['ve'])])
            cast_ve = p.cnt['ve']
            p.op('te', (lambda e: e.matmul(p.bank(1)[:, 0:512], perm32[:],
                                           bcnbf[:], start=True, stop=True)),
                 waits=[('ve', cast_ve), ('gq', max(rep_gq, perm_gq))]
                 + p.bank_waits(1))
            te_pm = p.cnt['te']
            for half, out_t in ((0, B_bf), (1, C_bf)):
                hb = half * 64
                p.op('ve', (lambda e, c0=c0, hb=hb, out_t=out_t: e.tensor_tensor(
                    out_t[0:64, c0:c0 + 512], BC_raw[hb:hb + 64, c0:c0 + 512],
                    cs2[hb:hb + 64, c0:c0 + 512], ALU.mult)),
                     waits=[('ve', p.cnt['ve'])])
                p.op('ve', (lambda e, c0=c0, hb=hb: e.tensor_tensor(
                    bcsc[0:64, 0:512], p.bank(1)[hb:hb + 64, 0:512],
                    ss2[hb:hb + 64, c0:c0 + 512], ALU.mult)),
                     waits=[('te', te_pm), ('ve', p.cnt['ve'])])
                p.op('ve', (lambda e, c0=c0, out_t=out_t: e.tensor_tensor(
                    out_t[0:64, c0:c0 + 512], out_t[0:64, c0:c0 + 512],
                    bcsc[0:64, 0:512], ALU.add)), waits=[('ve', p.cnt['ve'])])
            p.bank_release(1, 've', p.cnt['ve'])

        vst(lambda e: e.tensor_tensor(scr16, dt, lam, ALU.mult),
            w=[('se', p.cnt['se']), ('te', p.cnt['te'])])
        vst(lambda e: e.tensor_tensor(scr16, dt, scr16, ALU.subtract))
        vst(lambda e: e.tensor_tensor(dt, dt, lam, ALU.mult))
        dtlam = dt
        vst(lambda e: e.tensor_copy(lam, dtlam))
        woff = lam
        vst(lambda e: e.tensor_tensor(LAM16[0:16, 0:T - 1],
                                      LAM16[0:16, 0:T - 1],
                                      scr16[:, 1:T], ALU.add))
        vst(lambda e: e.tensor_tensor(scr16, dtlam, woff, ALU.subtract))
        p.op('se', (lambda e: e.activation(LAM16[0:16, 0:T], LAM16[0:16, 0:T],
                                           AF.Ln)), waits=[('ve', p.cnt['ve'])])
        p.op('ve', (lambda e: e.tensor_tensor(sinb[0:16, 0:T], LAM16[0:16, 0:T],
                                              SCAL[0:16, 0:T], ALU.subtract)),
             waits=[('se', p.cnt['se']), ('ve', p.cnt['ve'])])
        p.op('ve', (lambda e: e.tensor_copy(SCAL[32:48, 0:T], sinb[0:16, 0:T])),
             waits=[('ve', p.cnt['ve'])])
        for ci in range(NT):
            c0 = ci * 512
            p.op('ve', (lambda e, c0=c0: e.tensor_tensor(
                bcnbf[0:64, 0:512], B_bf[0:64, c0:c0 + 512], C_bf[0:64, c0:c0 + 512],
                ALU.mult)), waits=[('ve', p.cnt['ve'])])
            p.op('te', (lambda e: e.matmul(p.bank(2)[0:1, 0:512],
                                           tt['ones_col_bf'][0:64, 0:1],
                                           bcnbf[0:64, 0:512], start=True, stop=True)),
                 waits=[('ve', p.cnt['ve'])] + (p.bank_waits(2) if ci == 0 else []))
            p.op('ve', (lambda e, c0=c0: e.tensor_copy(
                rowscr[0:1, c0:c0 + 512], p.bank(2)[0:1, 0:512])),
                 waits=[('te', p.cnt['te'])])
            p.bank_release(2, 've', p.cnt['ve'])
        for ci in range(NT):
            c0 = ci * 512
            p.op('te', (lambda e, c0=c0: e.matmul(
                p.bank(3)[0:16, 0:512], tt['ones_row_f32'][:, 0:16],
                rowscr[0:1, c0:c0 + 512], start=True, stop=True)),
                 waits=[('ve', p.cnt['ve'])] + (p.bank_waits(3) if ci == 0 else []))
            p.op('ve', (lambda e, c0=c0: e.tensor_tensor(
                bcsc[0:16, 0:512], scr16[:, c0:c0 + 512],
                p.bank(3)[0:16, 0:512], ALU.mult)),
                 waits=[('te', p.cnt['te']), ('ve', p.cnt['ve'])])
            p.op('ve', (lambda e: e.tensor_scalar(
                bcsc[0:16, 0:512], bcsc[0:16, 0:512],
                ctx.col('l0a_D')[0:16, 0:1], None, ALU.add)),
                 waits=[('ve', p.cnt['ve'])])
            p.op('ve', (lambda e, c0=c0: e.tensor_copy(
                SCAL[64:80, c0:c0 + 512], bcsc[0:16, 0:512])),
                 waits=[('ve', p.cnt['ve'])])
            p.bank_release(3, 've', p.cnt['ve'])
        p.dump('l0a_cumA', SCAL[0:16, 0:T])
        p.dump('l0a_LWMR', SCAL[32:48, 0:T])
        p.dump('l0a_G2', SCAL[64:80, 0:T])
        p.dump('l0a_B', B_bf[:])
        p.dump('l0a_C', C_bf[:])
        p.dump('l0a_dtlam', TSCAL[0:16, 0:T])
        p.barrier()
    with ExitStack() as ses:
        xs_q = sb(nc, ses, 'l0a_xsq', [128, 16 * 512], BF16)
        yq = sb(nc, ses, 'l0a_yq', [128, 16 * 512], BF16)
        Hst = sb(nc, ses, 'l0a_H', [64, 16 * 128], F32)
        BTq = sb(nc, ses, 'l0a_BTq', [128, 256], BF16)
        SCTq = sb(nc, ses, 'l0a_SCTq', [128, 512], F32)
        W1scr = sb(nc, ses, 'l0a_W1', [128, 128], F32)
        Pbf = sb(nc, ses, 'l0a_Pbf', [128, 128], BF16)
        xTbf = sb(nc, ses, 'l0a_xT', [128, 128], BF16)
        btwbf = sb(nc, ses, 'l0a_btw', [128, 64], BF16)
        hbf = sb(nc, ses, 'l0a_hbf', [64, 128], BF16)
        ctil = sb(nc, ses, 'l0a_ctil', [64, 128], BF16)
        combf = sb(nc, ses, 'l0a_comb', [128, 512], F32)
        yacc = sb(nc, ses, 'l0a_yacc', [128, 512], F32)
        zsil = sb(nc, ses, 'l0a_zsil', [128, 512], BF16)
        maskWb = [sb(nc, ses, f'l0a_mw{i}', [128, 128], F32) for i in range(2)]
        colscr2 = sb(nc, ses, 'l0a_colscr2', [128, 8], F32)
        g2row = sb(nc, ses, 'l0a_g2row', [1, 512], F32)
        rows4 = sb(nc, ses, 'l0a_rows4', [1, 1024], F32)
        aend_b = sb(nc, ses, 'l0a_aendb', [64, 256], F32)
        pre_b = sb(nc, ses, 'l0a_preb', [128, 256], F32)
        erow = sb(nc, ses, 'l0a_erow', [1, 128], F32)
        bcmask = sb(nc, ses, 'l0a_bcmask', [128, 128], F32)
        maskW0t = sb(nc, ses, 'l0a_maskW0', [128, 128], F32)
        p.dma('po', bcmask[:], aps['c_bcmask_lhsT'][:, :])
        p.dma('po', maskW0t[:], aps['c_maskW0'][:, :])
        mload_gq = p.cnt['gq']

        p.op('po', lambda e: e.memset(Hst[:], 0.0))
        for i in range(2):
            p.op('ve', (lambda e, i=i: e.tensor_copy(maskWb[i][:], maskW0t[:])),
                 waits=[('ve', p.cnt['ve']), ('gq', mload_gq)])
        h_memset = p.cnt['po']
        p.dma('po', rows4[0:1, 0:256], SCAL[0:16, 127::128],
              waits=[('ve', p.cnt['ve'])])
        rr_gq = p.cnt['gq']
        p.wait_only('ve', [('gq', rr_gq)])
        vst2 = lambda fn, w=(): p.op('ve', fn, waits=[('ve', p.cnt['ve'])] + list(w))
        vst2(lambda e: e.tensor_tensor(rows4[0:1, 257:512], rows4[0:1, 1:256],
                                       rows4[0:1, 0:255], ALU.subtract),
             w=[('gq', rr_gq)])
        vst2(lambda e: e.memset(rows4[0:1, 256:512:16], 0.0))
        p.op('se', (lambda e: e.activation(rows4[0:1, 256:512], rows4[0:1, 256:512],
                                           AF.Exp)), waits=[('ve', p.cnt['ve'])])
        vst2(lambda e: e.tensor_single_scalar(rows4[0:1, 512:768], rows4[0:1, 0:256],
                                              -1.0, ALU.mult),
             w=[('se', p.cnt['se'])])
        negrr = rows4[0:1, 512:768]
        p.op('te', (lambda e: e.matmul(p.bank(5)[0:64, 0:256],
                                       tt['ones_row_f32'][:, 0:64],
                                       rows4[0:1, 256:512], start=True, stop=True)),
             waits=[('ve', p.cnt['ve']), ('se', p.cnt['se'])] + p.bank_waits(5))
        p.op('ve', (lambda e: e.tensor_copy(aend_b[:], p.bank(5)[0:64, 0:256])),
             waits=[('te', p.cnt['te'])])
        p.bank_release(5, 've', p.cnt['ve'])
        p.op('te', (lambda e: e.matmul(p.bank(5)[0:128, 0:256],
                                       tt['ones_row_f32'][:],
                                       rows4[0:1, 0:256], start=True, stop=True)),
             waits=[('ve', p.cnt['ve'])] + p.bank_waits(5))
        p.op('ve', (lambda e: e.tensor_copy(pre_b[:], p.bank(5)[0:128, 0:256])),
             waits=[('te', p.cnt['te'])])
        p.bank_release(5, 've', p.cnt['ve'])

        mwi = [0]
        for tq in range(NT):
            tc0 = tq * 512
            for m in range(16):
                bi, tile, cast = sq_proj.get(m * 128, 128)
                b = 6 + m % 2
                w0 = [('po', cast)] + p.bank_waits(b)
                for k in range(8):
                    p.op('te', (lambda e, k=k, b=b, tile=tile, tc0=tc0:
                                e.matmul(p.bank(b)[0:128, 0:512], tile(k),
                                         x_bf[k][:, tc0:tc0 + 512],
                                         start=(k == 0), stop=(k == 7))),
                         waits=w0 if k == 0 else ())
                sq_proj.done(bi, p.cnt['te'])
                p.op('se', (lambda e, m=m, b=b: e.activation(
                    xs_q[:, m * 512:(m + 1) * 512], p.bank(b)[0:128, 0:512],
                    AF.Silu)), waits=[('te', p.cnt['te'])])
                p.bank_release(b, 'se', p.cnt['se'])
            xs_se = p.cnt['se']
            for crel in range(4):
                c0 = tc0 + crel * 128
                p.op('te', (lambda e, c0=c0, crel=crel: e.transpose(
                    p.bank_bf(4)[:, crel * 64:crel * 64 + 64],
                    B_bf[0:64, c0:c0 + 128], tt['ident_bf'][0:64, 0:64])),
                     waits=([('ve', p.cnt['ve'])] + p.bank_waits(4))
                     if crel == 0 else ())
            p.op('ve', (lambda e: e.tensor_copy(BTq[:], p.bank_bf(4)[:, 0:256])),
                 waits=[('te', p.cnt['te'])])
            p.bank_release(4, 've', p.cnt['ve'])
            for crel in range(4):
                c0 = tc0 + crel * 128
                p.op('te', (lambda e, c0=c0: e.transpose(
                    p.bank(5)[:, 0:128], SCAL[:, c0:c0 + 128], tt['ident_f32'][:])),
                     waits=[('ve', p.cnt['ve']), ('se', p.cnt['se'])] + p.bank_waits(5))
                p.op('ve', (lambda e, crel=crel: e.tensor_copy(
                    SCTq[:, crel * 128:(crel + 1) * 128], p.bank(5)[:, 0:128])),
                     waits=[('te', p.cnt['te'])])
                p.bank_release(5, 've', p.cnt['ve'])
            sct_ve = p.cnt['ve']
            for h in range(16):
                hc_base = h * 16 + tq * 4
                for crel in range(4):
                    c = tq * 4 + crel
                    cg0 = c * 128
                    hc = h * 16 + c
                    mi = mwi[0] % 2
                    mwi[0] += 1
                    p.dma('po', maskWb[mi][0:1, 0:128], SCAL[h:h + 1, cg0:cg0 + 128],
                          waits=[('te', p.cnt['te'])])
                    mw_gq = p.cnt['gq']
                    p.op('te', (lambda e, mi=mi: e.matmul(
                        p.bank(1)[0:128, 0:128], bcmask[:], maskWb[mi][:],
                        start=True, stop=True)),
                         waits=[('gq', mw_gq)] + p.bank_waits(1))
                    te_w = p.cnt['te']
                    p.op('te', (lambda e, cg0=cg0: e.matmul(
                        p.bank(2)[0:128, 0:128], B_bf[0:64, cg0:cg0 + 128],
                        C_bf[0:64, cg0:cg0 + 128], start=True, stop=True)),
                         waits=p.bank_waits(2))
                    te_cbt = p.cnt['te']
                    p.op('se', (lambda e, crel=crel, h=h: e.activation(
                        W1scr[:], p.bank(1)[0:128, 0:128], AF.Exp,
                        bias=SCTq[:, crel * 128 + 32 + h:crel * 128 + 33 + h])),
                         waits=[('te', te_w), ('ve', p.cnt['ve'])])
                    se_w1 = p.cnt['se']
                    if c > 0:
                        p.op('se', (lambda e, hc=hc: e.activation(
                            erow[:], p.bank(1)[0:1, 0:128], AF.Exp,
                            bias=negrr[0:1, hc - 1:hc])),
                             waits=[('se', p.cnt['se']), ('te', p.cnt['te'])])
                    se_er = p.cnt['se']
                    p.bank_release(1, 'se', se_er)
                    p.op('ve', (lambda e: e.tensor_tensor(Pbf[:], W1scr[:],
                                                          p.bank(2)[0:128, 0:128],
                                                          ALU.mult)),
                         waits=[('se', se_w1), ('te', te_cbt), ('ve', p.cnt['ve'])])
                    p.bank_release(2, 've', p.cnt['ve'])
                    p.op('te', (lambda e, h=h, crel=crel: e.transpose(
                        p.bank_bf(4)[:, 0:128],
                        xs_q[:, h * 512 + crel * 128:h * 512 + (crel + 1) * 128],
                        tt['ident_bf'][:])),
                         waits=[('se', xs_se)] + p.bank_waits(4))
                    p.op('ve', (lambda e: e.tensor_copy(xTbf[:], p.bank_bf(4)[:, 0:128])),
                         waits=[('te', p.cnt['te']), ('ve', p.cnt['ve'])])
                    xt_ve = p.cnt['ve']
                    p.bank_release(4, 've', xt_ve)
                    if c > 0:
                        p.op('ve', (lambda e, h=h: e.tensor_copy(
                            hbf[:], Hst[:, h * 128:(h + 1) * 128])),
                             waits=[('ve', p.cnt['ve']), ('te', p.cnt['te'])])
                        p.op('te', (lambda e: e.matmul(
                            p.bank(3)[0:64, 0:128], tt['ones_row_f32'][:, 0:64],
                            erow[0:1, 0:128], start=True, stop=True)),
                             waits=[('se', se_er)] + p.bank_waits(3))
                        p.op('ve', (lambda e, cg0=cg0: e.tensor_tensor(
                            ctil[:], C_bf[0:64, cg0:cg0 + 128], p.bank(3)[0:64, 0:128],
                            ALU.mult)), waits=[('te', p.cnt['te']), ('ve', p.cnt['ve'])])
                        p.bank_release(3, 've', p.cnt['ve'])
                    p.op('te', (lambda e: e.matmul(
                        p.bank(0)[0:128, 0:128], xTbf[:], Pbf[:],
                        start=True, stop=True)),
                         waits=[('ve', p.cnt['ve'])] + p.bank_waits(0))
                    if c > 0:
                        p.op('te', (lambda e: e.matmul(
                            p.bank(0)[0:128, 0:128], hbf[:], ctil[:],
                            start=False, stop=True, skip_group_check=True)))
                    te_y = p.cnt['te']
                    p.op('ve', (lambda e, crel=crel: e.tensor_copy(
                        yacc[:, crel * 128:(crel + 1) * 128], p.bank(0)[0:128, 0:128])),
                         waits=[('te', te_y)])
                    p.bank_release(0, 've', p.cnt['ve'])
                    p.op('se', (lambda e, crel=crel, h=h, hc=hc: e.activation(
                        colscr2[:, 0:1], SCTq[:, crel * 128 + 32 + h:crel * 128 + 33 + h],
                        AF.Exp, bias=pre_b[:, hc:hc + 1])),
                         waits=[('ve', p.cnt['ve']), ('se', p.cnt['se'])])
                    p.op('ve', (lambda e, crel=crel: e.tensor_scalar(
                        btwbf[:], BTq[:, crel * 64:(crel + 1) * 64],
                        colscr2[:, 0:1], None, ALU.mult)),
                         waits=[('se', p.cnt['se']), ('ve', p.cnt['ve']),
                                ('te', p.cnt['te'])])
                    p.op('te', (lambda e: e.matmul(
                        p.bank(3)[0:64, 0:128], btwbf[:], xTbf[:],
                        start=True, stop=True)),
                         waits=[('ve', p.cnt['ve'])] + p.bank_waits(3))
                    p.op('ve', (lambda e, h=h, hc=hc: e.scalar_tensor_tensor(
                        Hst[:, h * 128:(h + 1) * 128], Hst[:, h * 128:(h + 1) * 128],
                        aend_b[:, hc:hc + 1], p.bank(3)[0:64, 0:128],
                        ALU.mult, ALU.add)),
                         waits=[('te', p.cnt['te']), ('ve', p.cnt['ve']),
                                ('po', h_memset)])
                    p.bank_release(3, 've', p.cnt['ve'])
                bi, tile, cast = sq_proj.get(2048 + h * 128, 128)
                bz = 6 + h % 2
                w0 = [('po', cast)] + p.bank_waits(bz)
                for k in range(8):
                    p.op('te', (lambda e, k=k, bz=bz, tile=tile, tc0=tc0:
                                e.matmul(p.bank(bz)[0:128, 0:512], tile(k),
                                         x_bf[k][:, tc0:tc0 + 512],
                                         start=(k == 0), stop=(k == 7))),
                         waits=w0 if k == 0 else ())
                sq_proj.done(bi, p.cnt['te'])
                te_z = p.cnt['te']
                p.dma('po', g2row[0:1, 0:512],
                      SCAL[64 + h:65 + h, tc0:tc0 + 512],
                      waits=[('te', p.cnt['te'])])
                g2_gq = p.cnt['gq']
                p.op('te', (lambda e: e.matmul(
                    p.bank(5)[0:128, 0:512], tt['ones_row_f32'][:],
                    g2row[0:1, 0:512], start=True, stop=True)),
                     waits=[('gq', g2_gq)] + p.bank_waits(5))
                te_g2 = p.cnt['te']
                p.op('ve', (lambda e, h=h: e.tensor_tensor(
                    combf[:], xs_q[:, h * 512:(h + 1) * 512], p.bank(5)[:, 0:512],
                    ALU.mult)), waits=[('te', te_g2), ('ve', p.cnt['ve'])])
                p.bank_release(5, 've', p.cnt['ve'])
                p.op('ve', (lambda e: e.tensor_tensor(
                    combf[:], combf[:], yacc[:], ALU.add)),
                     waits=[('ve', p.cnt['ve'])])
                p.op('se', (lambda e, bz=bz: e.activation(
                    zsil[:], p.bank(bz)[0:128, 0:512], AF.Silu)),
                     waits=[('te', te_z), ('ve', p.cnt['ve'])])
                p.bank_release(bz, 'se', p.cnt['se'])
                p.op('ve', (lambda e, h=h: e.tensor_tensor(
                    yq[:, h * 512:(h + 1) * 512], combf[:], zsil[:], ALU.mult)),
                     waits=[('se', p.cnt['se']), ('ve', p.cnt['ve'])])
            yq_ve = p.cnt['ve']
            for fi in range(NF):
                bi, tile, cast = sq_out.get(fi * 128, 128)
                b = fi % 6
                w0 = [('po', cast), ('ve', yq_ve)] + p.bank_waits(b)
                for k in range(16):
                    p.op('te', (lambda e, k=k, b=b, tile=tile:
                                e.matmul(p.bank(b)[0:128, 0:512], tile(k),
                                         yq[:, k * 512:(k + 1) * 512],
                                         start=(k == 0), stop=(k == 15))),
                         waits=w0 if k == 0 else ())
                sq_out.done(bi, p.cnt['te'])
                p.op('ve', (lambda e, fi=fi, b=b, tc0=tc0: e.scalar_tensor_tensor(
                    x[fi][:, tc0:tc0 + 512], x[fi][:, tc0:tc0 + 512],
                    tt['fconsts'][:, 8:9], p.bank(b)[0:128, 0:512],
                    ALU.mult, ALU.add)),
                     waits=[('te', p.cnt['te']), ('ve', p.cnt['ve'])])
                p.bank_release(b, 've', p.cnt['ve'])
            if tq == 0:
                p.dump('l0a_yq0', yq[:])
                p.dump('l0a_H0', Hst[:])
                p.dump('l0a_x2', x[2][:])
            p.flush()
        p.barrier()
    emit_postln(ctx, es, 'l0a_lnout', x, x, eps=1e-5, dst_bf=x_bf,
                wait_src=[('ve', p.cnt['ve'])])


def build_block_program(block_names, const_arrays):
    nc = bass.Bass("TRN2", target_bir_lowering=False)
    aps = {}
    import ml_dtypes
    for name, arr in const_arrays.items():
        dt = BF16 if arr.dtype == ml_dtypes.bfloat16 else F32
        aps[name] = nc.dram_tensor(name, list(arr.shape), dt, kind="ExternalInput").ap()
    x_in = nc.dram_tensor("x_in", [D, T], F32, kind="ExternalInput").ap()
    x_out = nc.dram_tensor("x_out", [D, T], F32, kind="ExternalOutput").ap()
    es = ExitStack()
    with es:
        p = Prog(nc, es)
        ctx = Ctx(nc, p, aps)
        emit_load_consts(ctx, es)
        x = [sb(nc, es, f'x{i}', [128, T], F32) for i in range(NF)]
        x_bf = [sb(nc, es, f'xbf{i}', [128, T], BF16) for i in range(NF)]
        for f in range(NF):
            p.dma('po', x[f][:], x_in[f * 128:(f + 1) * 128, :])
        ld = p.cnt['gq']
        for f in range(NF):
            p.op('po', (lambda e, f=f: e.tensor_copy(x_bf[f][:], x[f][:])),
                 waits=[('gq', ld)])
        for e in ('te', 've', 'se'):
            p.wait_only(e, [('gq', ld)])
        for bn in block_names:
            with ExitStack() as bes:
                if bn == 'l1_attn':
                    emit_l1_attn(ctx, bes, x, x_bf)
                elif bn == 'l0_attn':
                    emit_l0_attn(ctx, bes, x, x_bf)
                elif bn == 'l0_ffn':
                    emit_l0_ffn(ctx, bes, x, x_bf)
                elif bn == 'l1_ffn':
                    emit_l1_ffn(ctx, bes, x, x_bf)
                else:
                    raise ValueError(bn)
                p.flush()
        for f in range(NF):
            p.dma('po', x_out[f * 128:(f + 1) * 128, :], x[f][:],
                  waits=[('ve', p.cnt['ve']), ('se', p.cnt['se']), ('po', p.cnt['po'])])
        p.wait_only('po', [('gq', p.cnt['gq'])])
        p.flush()
    return nc


BLOCK_SEQ = ['l0_attn', 'l0_ffn', 'l1_attn', 'l1_ffn']
IMPLEMENTED = ['l0_attn', 'l0_ffn', 'l1_attn', 'l1_ffn']


def kernel(**inputs):
    consts = prep_host(inputs)
    xs = [np.ascontiguousarray(inputs['L0_attn_u'][i].T) for i in range(8)]
    for bn in BLOCK_SEQ:
        if bn not in IMPLEMENTED:
            continue
        nc = build_block_program([bn], consts)
        in_maps = []
        for i in range(8):
            m = dict(consts)
            m['x_in'] = xs[i]
            in_maps.append(m)
        res = run_bass_kernel_spmd(nc, in_maps, core_ids=list(range(8)))
        xs = [np.asarray(res.results[i]['x_out']) for i in range(8)]
    return np.ascontiguousarray(np.stack([x.T for x in xs], 0))


# version 05
# speedup vs baseline: 1.0161x; 1.0161x over previous
import numpy as np
from contextlib import ExitStack
import concourse.bass as bass
import concourse.mybir as mybir
from concourse.bass_utils import run_bass_kernel_spmd

F32 = mybir.dt.float32
BF16 = mybir.dt.bfloat16
I32 = mybir.dt.int32
U8 = mybir.dt.uint8
AF = mybir.ActivationFunctionType
ALU = mybir.AluOpType
AX = mybir.AxisListType

T = 2048
NT = 4
D = 1024
NF = D // 128

ENG_ATTR = {'te': 'tensor', 've': 'vector', 'se': 'scalar', 'po': 'gpsimd', 'sy': 'sync'}
DEBUG_DUMPS = False
FCONSTS = [1e-5, 1e-6, 1.1920929e-07, 1e-24, 1.0, 0.0, -1e30, -2.0794415416798357, 2.0]


class Prog:

    def __init__(self, nc, es):
        self.nc = nc
        self.es = es
        self.block = es.enter_context(nc.Block())
        self.sems = {}
        self.cnt = {}
        for s in ('te', 've', 'se', 'po', 'sq', 'gq', 'cq', 'w0', 'w1'):
            self.sems[s] = es.enter_context(nc.semaphore('sem_' + s))
            self.cnt[s] = 0
        self.pend = {e: [] for e in ENG_ATTR}
        self.last_wait = {e: {} for e in ENG_ATTR}
        self.psum = es.enter_context(nc.psum_tensor("psum_all", [128, 4096], F32))
        self.bank_ready = [None] * 8

    def bank(self, i):
        return self.psum.ap()[:, i * 512:(i + 1) * 512]

    def bank_bf(self, i):
        return self.psum.ap().bitcast(BF16)[:, i * 1024:(i + 1) * 1024]

    def bank_span(self, i, n):
        return self.psum.ap()[:, i * 512:(i + n) * 512]

    def op(self, eng, fn, waits=(), tick=None, tick_n=1):
        wl = []
        for w in waits:
            s, v = w if isinstance(w, tuple) else (w, self.cnt[w])
            if v > 0 and self.last_wait[eng].get(s, -1) < v:
                wl.append((s, v))
                self.last_wait[eng][s] = v
        if tick is None:
            tick = eng if eng in ('te', 've', 'se', 'po') else None
        elif tick is False:
            tick = None
        self.pend[eng].append((wl, fn, tick, tick_n))
        if tick:
            self.cnt[tick] += tick_n
            return self.cnt[tick]
        return None

    def dma(self, queue, out, in_, waits=(), sem=None):
        qsem = sem or {'sy': 'sq', 'po': 'gq', 'se': 'cq'}[queue]
        return self.op(queue, (lambda e, o=out, i=in_: e.dma_start(
            out=o, in_=i, allow_slow_non_contiguous=True)),
                       waits=waits, tick=qsem, tick_n=16)

    def wait_only(self, eng, waits):
        self.op(eng, None, waits=waits, tick=False)

    def barrier(self, prods=('te', 've', 'se', 'po'), cons=('sy', 'po', 'se', 've', 'te')):
        w = [(s, self.cnt[s]) for s in prods]
        for e in cons:
            self.op(e, None, waits=w, tick=False)

    def dump(self, name, ap, dt=None):
        if not DEBUG_DUMPS or (isinstance(DEBUG_DUMPS, (set, list)) and name not in DEBUG_DUMPS):
            return
        shape = list(ap.shape)
        dram = self.nc.dram_tensor('dbg_' + name, shape, dt or ap.dtype,
                                   kind="ExternalOutput").ap()
        self.barrier()
        sl = tuple(slice(0, s) for s in shape)
        self.dma('po', dram[sl], ap)
        gq = self.cnt['gq']
        for e in ('sy', 'po', 'se', 've', 'te'):
            self.op(e, None, waits=[('gq', gq)], tick=False)

    def flush(self):
        for eng, lst in self.pend.items():
            if not lst:
                continue
            sems = self.sems

            def body(e, lst=tuple(lst), sems=sems):
                for wl, fn, tick, tick_n in lst:
                    for (s, v) in wl:
                        e.wait_ge(sems[s], v)
                    if fn is not None:
                        ins = fn(e)
                        if tick:
                            ins.then_inc(sems[tick], tick_n)
            getattr(self.block, ENG_ATTR[eng])(body)
            self.pend[eng] = []

    def bank_waits(self, i):
        r = self.bank_ready[i]
        return [r] if r is not None else []

    def bank_release(self, i, sem, val):
        self.bank_ready[i] = (sem, val)


import os
SB_DEBUG = bool(os.environ.get('KERNEL_SB_DEBUG'))


def sb(nc, es, name, shape, dt=F32):
    h = es.enter_context(nc.sbuf_tensor(name, shape, dt))
    if SB_DEBUG:
        nbytes = shape[1] * {F32: 4, BF16: 2, U8: 1, I32: 4}.get(dt, 4)
        print(f"SB {name}: {nbytes}B/part")
    return h


def col_layout(v, pad_to=None):
    v = np.asarray(v, np.float32).reshape(-1)
    n = v.shape[0]
    k = (n + 127) // 128 if pad_to is None else pad_to
    out = np.zeros((128, k), np.float32)
    for i in range(k):
        seg = v[i * 128:(i + 1) * 128]
        out[:len(seg), i] = seg
    return out


COLS = {}


def pack_cols(c, entries):
    arrs = []
    off = 0
    COLS.clear()
    for name, a in entries:
        a = np.asarray(a, np.float32)
        assert a.shape[0] == 128
        COLS[name] = (off, a.shape[1])
        arrs.append(a)
        off += a.shape[1]
    c['c_allcols'] = np.concatenate(arrs, axis=1)


def prep_host(w):
    import ml_dtypes
    c = {}
    c['c_ident_bf'] = np.eye(128).astype(ml_dtypes.bfloat16)
    c['c_ident_f32'] = np.eye(128, dtype=np.float32)
    c['c_ones_col_bf'] = np.ones((128, 1), ml_dtypes.bfloat16)
    c['c_ones_col_f32'] = np.ones((128, 1), np.float32)
    c['c_ones_row_f32'] = np.ones((1, 128), np.float32)
    c['c_ones128'] = np.ones((128, 128), np.float32)
    ii = np.arange(128)
    c['c_mask_keep_le'] = np.where(ii[:, None] > ii[None, :], -1e30, 0.0).astype(np.float32)
    c['c_mask_causal'] = np.where(ii[None, :] > ii[:, None], -1e30, 0.0).astype(np.float32)
    c['c_mask_keep_ge'] = np.where(ii[None, :] < ii[:, None], -1e30, 0.0).astype(np.float32)
    c['c_mask_causal_bf'] = c['c_mask_causal'].astype(ml_dtypes.bfloat16)
    c['c_mask_keep_ge_bf'] = c['c_mask_keep_ge'].astype(ml_dtypes.bfloat16)
    fc = np.array(FCONSTS, np.float32)
    c['c_fconsts'] = np.tile(fc[None, :], (128, 1)).astype(np.float32)

    tp = lambda a: np.ascontiguousarray(np.asarray(a, np.float32).T)
    c['w_l0f_wqT'] = tp(w['L0_ffn_wq'])
    c['w_l0f_wkT'] = tp(w['L0_ffn_wk'])
    c['w_l0f_wvT'] = tp(w['L0_ffn_wv'])
    c['w_l0f_woT'] = tp(w['L0_ffn_wo'])
    c['w_l0f_gateT'] = tp(w['L0_ffn_gate_w'])
    c['w_l0f_e0upT'] = tp(w['L0_ffn_e0_up'])
    c['w_l0f_e0dnT'] = tp(w['L0_ffn_e0_down'])
    c['w_l0f_e1w1T'] = tp(w['L0_ffn_e1_w1'])
    c['w_l0f_e1w2T'] = tp(w['L0_ffn_e1_w2'])
    c['w_l0f_e1dnT'] = tp(w['L0_ffn_e1_down'])
    c['w_l0f_e2l1T'] = tp(w['L0_ffn_e2_l1'])
    c['w_l0f_e2l2T'] = tp(w['L0_ffn_e2_l2'])
    c['w_l0f_e2l3T'] = tp(w['L0_ffn_e2_l3'])
    c['w_l0f_e2l4T'] = tp(w['L0_ffn_e2_l4'])
    c['w_l0f_e3dnT'] = tp(w['L0_ffn_e3_down'])
    c['w_l0f_e3upT'] = tp(w['L0_ffn_e3_up'])
    c['w_l0f_e3outT'] = tp(w['L0_ffn_e3_out'])

    c['w_l1a_WaT'] = np.ascontiguousarray(w['L1_attn_Wa_w'].T)
    c['w_l1a_BT'] = np.ascontiguousarray(w['L1_attn_B_w'].T)
    c['w_l1a_CT'] = np.ascontiguousarray(w['L1_attn_C_w'].T)
    c['w_l1a_DT'] = np.ascontiguousarray(w['L1_attn_D_w'].T)
    c['w_l0a_proj'] = np.ascontiguousarray(np.asarray(w['L0_attn_in_proj_w'], np.float32))
    c['w_l0a_theta'] = np.ascontiguousarray(np.asarray(w['L0_attn_theta_w'], np.float32))
    c['w_l0a_out'] = np.ascontiguousarray(np.asarray(w['L0_attn_out_proj_w'], np.float32))
    bcm = np.eye(128, dtype=np.float32)
    bcm[0, :] = 1.0
    c['c_bcmask_lhsT'] = bcm
    mw0 = np.zeros((128, 128), np.float32)
    ii2 = np.arange(128)
    mw0[1:, :] = np.where(ii2[1:, None] > ii2[None, :], -1e30, 0.0)
    c['c_maskW0'] = mw0
    pm32 = np.zeros((128, 128), np.float32)
    for m in range(128):
        h64 = (m // 64) * 64
        d = m % 64
        pm32[h64 + (d + 32) % 64, m] = 1.0
    c['c_perm32_bf'] = pm32.astype(ml_dtypes.bfloat16)

    HIDP = 2816
    for l in range(2):
        c[f'w_l1f_qkv{l}'] = np.ascontiguousarray(np.asarray(w['L1_ffn_wqkv'][l], np.float32))
        c[f'w_l1f_wo{l}'] = np.ascontiguousarray(np.asarray(w['L1_ffn_wo'][l], np.float32))
        w1 = np.asarray(w['L1_ffn_w1'][l], np.float32)
        w1p = np.zeros((1024, 2 * HIDP), np.float32)
        w1p[:, 0:2730] = w1[:, 0:2730]
        w1p[:, HIDP:HIDP + 2730] = w1[:, 2730:5460]
        c[f'w_l1f_w1{l}'] = w1p
        w2 = np.asarray(w['L1_ffn_w2'][l], np.float32)
        w2p = np.zeros((HIDP, 1024), np.float32)
        w2p[0:2730, :] = w2
        c[f'w_l1f_w2{l}'] = w2p
    DH, WIN = 64, 512
    half_idx = np.arange(0, DH, 2, dtype=np.float32)
    inv_freq = 1.0 / (10000.0 ** (half_idx / DH))
    t_ = np.arange(T, dtype=np.float32)
    freqs = np.concatenate([t_[:, None] * inv_freq[None, :]] * 2, 1)
    base = (half_idx + 0.4 * DH) / (1.4 * DH)
    power = (t_ - T // 2) / float(WIN // 2)
    xsc = np.concatenate([base[None, :] ** power[:, None]] * 2, 1)
    sign = np.concatenate([-np.ones(32), np.ones(32)]).astype(np.float32)
    csq = (np.cos(freqs) * xsc * 8.0).T
    ssq = (np.sin(freqs) * xsc * 8.0).T * sign[:, None]
    csk = (np.cos(freqs) / xsc).T
    ssk = (np.sin(freqs) / xsc).T * sign[:, None]
    tab = np.zeros((128, 4 * T), np.float32)
    for i, tbl in enumerate((csq, ssq, csk, ssk)):
        side, j = i // 2, i % 2
        for ci in range(4):
            blk = tbl[:, ci * 512:(ci + 1) * 512]
            off = ci * 2048 + side * 1024 + j * 512
            tab[0:64, off:off + 512] = blk
            tab[64:128, off:off + 512] = blk
    c['c_l1f_tab'] = tab.astype(ml_dtypes.bfloat16)
    pm = np.zeros((128, 128), np.float32)
    for m in range(128):
        h64 = (m // 64) * 64
        d = m % 64
        k = h64 + (d + 32) % 64
        pm[k, m] = 1.0
    c['c_permrot_bf'] = pm.astype(ml_dtypes.bfloat16)

    pack_cols(c, [
        ('l0f_n1', col_layout(w['L0_ffn_n1_w'])),
        ('l0f_n2', col_layout(w['L0_ffn_n2_w'])),
        ('l0f_qn', np.concatenate([np.asarray(w['L0_ffn_qn_w'], np.float32)] * 2).reshape(128, 1)),
        ('l0f_kn', np.concatenate([np.asarray(w['L0_ffn_kn_w'], np.float32)] * 2).reshape(128, 1)),
        ('l1a_b', col_layout(w['L1_attn_Wa_b'])),
        ('l1f_qs0', np.concatenate([np.asarray(w['L1_ffn_q_scale'][0], np.float32)] * 2).reshape(128, 1)),
        ('l1f_qs1', np.concatenate([np.asarray(w['L1_ffn_q_scale'][1], np.float32)] * 2).reshape(128, 1)),
        ('l1f_ks0', np.concatenate([np.asarray(w['L1_ffn_k_scale'][0], np.float32)] * 2).reshape(128, 1)),
        ('l1f_ks1', np.concatenate([np.asarray(w['L1_ffn_k_scale'][1], np.float32)] * 2).reshape(128, 1)),
        ('l1f_lng0', col_layout(w['L1_ffn_ln_g'][0])),
        ('l1f_lnb0', col_layout(w['L1_ffn_ln_b'][0])),
        ('l1f_lng1', col_layout(w['L1_ffn_ln_g'][1])),
        ('l1f_lnb1', col_layout(w['L1_ffn_ln_b'][1])),
        ('l1f_b1a0', col_layout(np.asarray(w['L1_ffn_b1'][0], np.float32)[0:2730], pad_to=22)),
        ('l1f_b1g0', col_layout(np.asarray(w['L1_ffn_b1'][0], np.float32)[2730:5460], pad_to=22)),
        ('l1f_b1a1', col_layout(np.asarray(w['L1_ffn_b1'][1], np.float32)[0:2730], pad_to=22)),
        ('l1f_b1g1', col_layout(np.asarray(w['L1_ffn_b1'][1], np.float32)[2730:5460], pad_to=22)),
        ('l1f_b20', col_layout(w['L1_ffn_b2'][0])),
        ('l1f_b21', col_layout(w['L1_ffn_b2'][1])),
        ('l0a_lng', col_layout(w['L0_attn_norm_scale'])),
        ('l0a_lnb', col_layout(w['L0_attn_norm_bias'])),
        ('l0a_Alog', col_layout(w['L0_attn_A_log'])),
        ('l0a_dtb', col_layout(w['L0_attn_dt_bias'])),
        ('l0a_bcs', np.concatenate([np.asarray(w['L0_attn_bcB_scale'], np.float32),
                                    np.asarray(w['L0_attn_bcC_scale'], np.float32)]).reshape(128, 1)),
        ('l0a_bcb', np.concatenate([np.asarray(w['L0_attn_B_bias'], np.float32),
                                    np.asarray(w['L0_attn_C_bias'], np.float32)]).reshape(128, 1)),
        ('l0a_D', col_layout(w['L0_attn_D'])),
    ])
    return c


class Ctx:
    def col(self, name, i=0):
        off, k = COLS[name]
        assert i < k
        return self.t['allcols'][:, off + i:off + i + 1]

    def __init__(self, nc, p, aps):
        self.nc = nc
        self.p = p
        self.aps = aps
        self.t = {}
        self.wbuf = None
        self.wbufb = None
        self.wbuf_free = [None, None]
        self.wcast_free = [None, None]


def emit_load_consts(ctx, es):
    nc, p = ctx.nc, ctx.p
    tt = ctx.t
    tt['ident_bf'] = sb(nc, es, 'ident_bf', [128, 128], BF16)
    tt['ident_f32'] = sb(nc, es, 'ident_f32', [128, 128], F32)
    tt['ones_col_bf'] = sb(nc, es, 'ones_col_bf', [128, 1], BF16)
    tt['ones_col_f32'] = sb(nc, es, 'ones_col_f32', [128, 1], F32)
    tt['ones_row_f32'] = sb(nc, es, 'ones_row_f32', [1, 128], F32)
    tt['ones128'] = sb(nc, es, 'ones128', [128, 128], F32)
    tt['mask_causal_bf'] = sb(nc, es, 'mask_causal_bf', [128, 128], BF16)
    tt['mask_keep_ge_bf'] = sb(nc, es, 'mask_keep_ge_bf', [128, 128], BF16)
    tt['permrot_bf'] = sb(nc, es, 'permrot_bf', [128, 128], BF16)
    for name in ('ident_bf', 'ident_f32', 'ones_col_bf', 'ones_col_f32',
                 'ones_row_f32', 'ones128',
                 'mask_causal_bf', 'mask_keep_ge_bf', 'permrot_bf'):
        p.dma('sy', tt[name][:], ctx.aps['c_' + name][:, :])
    tt['fconsts'] = sb(nc, es, 'fconsts', [128, len(FCONSTS)], F32)
    p.dma('sy', tt['fconsts'][:], ctx.aps['c_fconsts'][:, :])
    ncols_total = ctx.aps['c_allcols'].shape[1]
    tt['allcols'] = sb(nc, es, 'allcols', [128, ncols_total], F32)
    p.dma('sy', tt['allcols'][:], ctx.aps['c_allcols'][:, :])
    for i, v in enumerate(FCONSTS):
        nc.const_aps.register(F32, float(v), tt['fconsts'][:, i:i + 1])
    cld = p.cnt['sq']
    for e in ('te', 've', 'se', 'po'):
        p.wait_only(e, [('sq', cld)])
    ctx.wbuf = [sb(nc, es, f'wpool{i}', [128, 1024], F32) for i in range(2)]
    ctx.wbufb = [sb(nc, es, f'wpoolb{i}', [128, 2048], BF16) for i in range(2)]
    for i in range(2):
        p.op('po', (lambda e, i=i: e.memset(ctx.wbuf[i][:], 0.0)))
        ctx.wbuf_free[i] = ('po', p.cnt['po'])


WSLAB_COLS = 128


class Slabs:

    def __init__(self, ctx, lhsT_ap, K, k0=0):
        self.ctx = ctx
        self.ap = lhsT_ap
        self.K = K
        self.k0 = k0
        self.KC = K // 128
        assert K % 128 == 0 and self.KC * WSLAB_COLS <= 2048
        self.i = 0

    def get(self, m0, mw):
        ctx, p = self.ctx, self.ctx.p
        bi = self.i % 2
        self.i += 1
        wsem = 'w%d' % bi
        halves = [(0, self.KC)] if self.KC * WSLAB_COLS <= 1024 else \
            [(0, (self.KC + 1) // 2), ((self.KC + 1) // 2, self.KC - (self.KC + 1) // 2)]
        first = True
        for (kc0, kcn) in halves:
            waits = []
            if first and ctx.wbuf_free[bi]:
                waits.append(ctx.wbuf_free[bi])
            if not first:
                waits.append(('po', p.cnt['po']))
            p.dma('sy', ctx.wbuf[bi][:, 0:kcn * WSLAB_COLS]
                  .rearrange('p (kc m) -> p kc m', m=WSLAB_COLS)[:, :, 0:mw],
                  self.ap[self.k0 + kc0 * 128:self.k0 + (kc0 + kcn) * 128, m0:m0 + mw]
                  .rearrange('(kc p) m -> p kc m', p=128),
                  waits=waits, sem=wsem)
            ld = p.cnt[wsem]
            cw = [(wsem, ld)]
            if first and ctx.wcast_free[bi]:
                cw.append(ctx.wcast_free[bi])
            p.op('po', (lambda e, b=bi, o=kc0 * WSLAB_COLS, n=kcn * WSLAB_COLS:
                        e.tensor_copy(ctx.wbufb[b][:, o:o + n], ctx.wbuf[b][:, 0:n])),
                 waits=cw)
            first = False
        cast_done = p.cnt['po']
        ctx.wbuf_free[bi] = ('po', cast_done)

        def tile(k, b=bi, mw=mw):
            return ctx.wbufb[b][:, k * WSLAB_COLS:k * WSLAB_COLS + mw]
        return bi, tile, cast_done

    def done(self, bi, te_cnt):
        self.ctx.wcast_free[bi] = ('te', te_cnt)


def emit_linear(ctx, es, name, lhsT_ap, K, M, rhs_tiles, out_cb,
                rhs_col0=0, n_cols=T, nchunk=512, m_groups=None,
                wait_rhs=()):
    nc, p = ctx.nc, ctx.p
    KC = K // 128
    assert K % 128 == 0 and KC * WSLAB_COLS <= 2048
    if m_groups is None:
        m_groups = [(i * 128, min(128, M - i * 128)) for i in range((M + 127) // 128)]
    slabs = Slabs(ctx, lhsT_ap, K)
    nci = [(i * nchunk, min(nchunk, n_cols - i * nchunk))
           for i in range((n_cols + nchunk - 1) // nchunk)]
    for gi, (m0, mw) in enumerate(m_groups):
        bi, tile_fn, cast_done = slabs.get(m0, mw)
        for ni, (c0, cw) in enumerate(nci):
            bidx = (gi * len(nci) + ni) % 6
            bank = p.bank(bidx)
            te_waits = [('po', cast_done)] + p.bank_waits(bidx) + list(wait_rhs)
            for k in range(KC):
                p.op('te', (lambda e, k=k, bank=bank, c0=c0, cw=cw, mw=mw,
                            tile_fn=tile_fn, rt=rhs_tiles[k], rc=rhs_col0:
                            e.matmul(bank[0:mw, 0:cw], tile_fn(k),
                                     rt[:, rc + c0:rc + c0 + cw],
                                     start=(k == 0), stop=(k == KC - 1))),
                     waits=te_waits if k == 0 else ())
            te_cnt = p.cnt['te']
            slabs.done(bi, te_cnt)
            out_cb(gi, m0, mw, ni, c0, cw, bidx, bank, te_cnt)
    return


def emit_postln(ctx, es, name, src, dst, eps=1e-5, dst_bf=None, gamma=None, beta=None,
                rms=False, wait_src=(), end_scope=True, src2=None, src_scale=1.0,
                src_builder=None):
    nc, p = ctx.nc, ctx.p
    tt = ctx.t
    nf = len(src)
    dinv = 1.0 / (nf * 128)
    les = es.enter_context(ExitStack())
    stat_t = sb(nc, les, name + '_stat', [1, 2 * T], F32)
    sums = stat_t[:, 0:T]; sumsq = stat_t[:, T:2 * T]
    mu = sums; var = sumsq; rstd = sumsq
    sq_bf = sb(nc, les, name + '_sq', [128, T], BF16)
    src_bf = sb(nc, les, name + '_srcbf', [128, T], BF16)
    rstd_b = sb(nc, les, name + '_rstdb', [128, T], F32)
    m2_b = None if rms else sb(nc, les, name + '_m2b', [128, T], F32)
    need_comb = (src2 is not None) or (src_scale != 1.0) or (src_builder is not None)
    rot = [sb(nc, les, name + f'_rot{i}', [128, T], F32) for i in range(2)] \
        if need_comb else None
    rot_gate = [0, 0]

    def comb(f, phase, waits):
        s2 = src2[f] if src2 is not None else None
        if not need_comb or (src_builder is None and s2 is None and src_scale == 1.0):
            return src[f][:], 0
        rt = rot[f % 2]
        w = list(waits) + ([('ve', p.cnt['ve'])] if True else [])
        if rot_gate[f % 2]:
            w.append(rot_gate[f % 2])
        if src_builder is not None:
            cve = src_builder(f, rt, w)
            return rt[:], cve
        if s2 is None:
            p.op('ve', (lambda e, f=f, rt=rt: e.tensor_single_scalar(
                rt[:], src[f][:], float(src_scale), ALU.mult)), waits=w)
        elif src_scale == 1.0:
            p.op('ve', (lambda e, f=f, rt=rt, s2=s2: e.tensor_tensor(
                rt[:], src[f][:], s2[:], ALU.add)), waits=w)
        else:
            p.op('ve', (lambda e, f=f, rt=rt, s2=s2: e.scalar_tensor_tensor(
                rt[:], src[f][:], float(src_scale), s2[:], ALU.mult, ALU.add)), waits=w)
        return rt[:], p.cnt['ve']
    bq = 6; bx = 7
    te_prev = 0
    for f in range(nf):
        w0 = list(wait_src) if f == 0 else [('te', te_prev)]
        cap, cve = comb(f, 'stats', w0)
        wc = w0 + ([('ve', cve)] if cve else [])
        p.op('se', (lambda e, cap=cap: e.activation(sq_bf[:], cap, AF.Square)),
             waits=wc)
        se_c = p.cnt['se']
        p.op('po', (lambda e, cap=cap: e.tensor_copy(src_bf[:], cap)), waits=wc)
        po_c = p.cnt['po']
        rot_gate[f % 2] = ('se', se_c)
        for ci in range(NT):
            bs = p.bank(ci); bqq = p.bank(4 + ci)
            p.op('te', (lambda e, ci=ci, f=f, bs=bs:
                        e.matmul(bs[0:1, 0:512], tt['ones_col_bf'][:],
                                 src_bf[:, ci * 512:(ci + 1) * 512],
                                 start=(f == 0), stop=(f == nf - 1))),
                 waits=[('po', po_c)] + (p.bank_waits(ci) if f == 0 else []))
            p.op('te', (lambda e, ci=ci, f=f, bq2=bqq:
                        e.matmul(bq2[0:1, 0:512], tt['ones_col_bf'][:],
                                 sq_bf[:, ci * 512:(ci + 1) * 512],
                                 start=(f == 0), stop=(f == nf - 1))),
                 waits=[('se', se_c)] + (p.bank_waits(4 + ci) if f == 0 else []))
        te_prev = p.cnt['te']
    te_stats = p.cnt['te']
    for ci in range(NT):
        p.op('ve', (lambda e, ci=ci: e.tensor_copy(sums[:, ci * 512:(ci + 1) * 512],
                                                   p.bank(ci)[0:1, 0:512])),
             waits=[('te', te_stats)])
        p.op('ve', (lambda e, ci=ci: e.tensor_copy(sumsq[:, ci * 512:(ci + 1) * 512],
                                                   p.bank(4 + ci)[0:1, 0:512])))
    for ci in range(NT):
        p.bank_release(ci, 've', p.cnt['ve'])
        p.bank_release(4 + ci, 've', p.cnt['ve'])
    musq = rstd_b[0:1, 0:T]
    vstep = lambda fn: p.op('ve', fn, waits=[('ve', p.cnt['ve'])])
    if rms:
        vstep(lambda e: e.tensor_single_scalar(var[:], sumsq[:], dinv, ALU.mult))
    else:
        vstep(lambda e: e.tensor_single_scalar(mu[:], sums[:], dinv, ALU.mult))
        vstep(lambda e: e.tensor_single_scalar(var[:], sumsq[:], dinv, ALU.mult))
        vstep(lambda e: e.tensor_tensor(musq[:], mu[:], mu[:], ALU.mult))
        vstep(lambda e: e.tensor_tensor(var[:], var[:], musq[:], ALU.subtract))
    ve_var = p.cnt['ve']
    p.op('se', (lambda e: e.activation(rstd[:], var[:], AF.Ln, bias=float(eps))),
         waits=[('ve', ve_var)])
    p.op('se', (lambda e: e.activation(rstd[:], rstd[:], AF.Exp, scale=-0.5)),
         waits=[('se', p.cnt['se'])])
    se_rstd = p.cnt['se']
    ve_m2 = p.cnt['ve']
    for ci in range(NT):
        bidx = ci % 2
        p.op('te', (lambda e, ci=ci, b=p.bank(bidx):
                    e.matmul(b[:, 0:512], tt['ones_row_f32'][:],
                             rstd[:, ci * 512:(ci + 1) * 512], start=True, stop=True)),
             waits=[('se', se_rstd)] + p.bank_waits(bidx))
        p.op('ve', (lambda e, ci=ci, b=p.bank(bidx):
                    e.tensor_copy(rstd_b[:, ci * 512:(ci + 1) * 512], b[:, 0:512])),
             waits=[('te', p.cnt['te'])])
        p.bank_release(bidx, 've', p.cnt['ve'])
        if not rms:
            bidx2 = 2 + ci % 2
            p.op('te', (lambda e, ci=ci, b=p.bank(bidx2):
                        e.matmul(b[:, 0:512], tt['ones_row_f32'][:],
                                 mu[:, ci * 512:(ci + 1) * 512], start=True, stop=True)),
                 waits=[('ve', ve_var)] + p.bank_waits(bidx2))
            p.op('ve', (lambda e, ci=ci, b=p.bank(bidx2):
                        e.tensor_copy(m2_b[:, ci * 512:(ci + 1) * 512], b[:, 0:512])),
                 waits=[('te', p.cnt['te'])])
            p.bank_release(bidx2, 've', p.cnt['ve'])
    ve_bcast = p.cnt['ve']
    scr = None
    if dst is None and not need_comb:
        scr = [sb(nc, les, name + f'_scr{i}', [128, T], F32) for i in range(2)]
        dstl = lambda f: scr[f % 2]
        scr_gate = [0, 0]
    elif dst is None:
        dstl = lambda f: rot[f % 2]
        scr_gate = [0, 0]
        scr = 'rot'
    else:
        dstl = lambda f: dst[f]
    for f in range(nf):
        dtile = dstl(f)
        w0 = [('ve', ve_bcast)]
        g = scr_gate[f % 2] if scr is not None else 0
        if g:
            w0.append(('po', g[1]) if isinstance(g, tuple) else ('se', g))
        if scr == 'rot' and isinstance(rot_gate[f % 2], tuple):
            w0.append(rot_gate[f % 2])
        cap, cve = comb(f, 'apply', w0)
        if cve:
            w0 = w0 + [('ve', cve)]
        if not rms:
            p.op('ve', (lambda e, d=dtile, cap=cap: e.tensor_tensor(d[:], cap, m2_b[:], ALU.subtract)),
                 waits=w0)
            p.op('ve', (lambda e, d=dtile: e.tensor_tensor(d[:], d[:], rstd_b[:], ALU.mult)),
                 waits=[('ve', p.cnt['ve'])])
        else:
            p.op('ve', (lambda e, d=dtile, cap=cap: e.tensor_tensor(d[:], cap, rstd_b[:], ALU.mult)),
                 waits=w0)
        if gamma is not None:
            p.op('se', (lambda e, d=dtile, f=f: e.activation(
                (dst_bf[f][:] if dst_bf is not None else d[:]), d[:], AF.Identity,
                bias=beta[f], scale=gamma[f])), waits=[('ve', p.cnt['ve'])])
            if scr is not None:
                scr_gate[f % 2] = p.cnt['se']
        elif dst_bf is not None:
            p.op('po', (lambda e, d=dtile, f=f: e.tensor_copy(dst_bf[f][:], d[:])),
                 waits=[('ve', p.cnt['ve'])])
            if scr is not None:
                scr_gate[f % 2] = ('po-gate', p.cnt['po'])
        if need_comb:
            rot_gate[f % 2] = ('ve', p.cnt['ve'])
    if end_scope:
        p.barrier()
        les.close()


def emit_l1_attn(ctx, es, x, x_bf):
    nc, p = ctx.nc, ctx.p
    tt = ctx.t
    h_bf = [sb(nc, es, f'l1a_hb{i}', [128, T], BF16) for i in range(2)]
    bias_ld = 0
    with ExitStack() as ies:
        a_t = sb(nc, ies, 'l1a_a', [128, T], F32)
        b_t = sb(nc, ies, 'l1a_b', [128, T], F32)
        h_t = sb(nc, ies, 'l1a_h', [128, T], F32)
        for i in range(2):
            ascan = p.cnt['ve']

            def evac_a(gi, m0, mw, ni, c0, cw, bidx, pap, te_cnt, i=i, ascan=ascan):
                p.op('se', (lambda e, c0=c0, cw=cw, pap=pap, i=i:
                            e.activation(a_t[:, c0:c0 + cw], pap[0:128, 0:cw],
                                         AF.Sigmoid, bias=ctx.col('l1a_b', i))),
                     waits=[('te', te_cnt), ('ve', ascan)])
                p.bank_release(bidx, 'se', p.cnt['se'])
            emit_linear(ctx, ies, 'l1a_wa', ctx.aps['w_l1a_WaT'], D, 128, x_bf, evac_a,
                        m_groups=[(i * 128, 128)])

            def evac_b(gi, m0, mw, ni, c0, cw, bidx, pap, te_cnt, ascan=ascan):
                p.op('ve', (lambda e, c0=c0, cw=cw, pap=pap:
                            e.tensor_copy(b_t[:, c0:c0 + cw], pap[0:128, 0:cw])),
                     waits=[('te', te_cnt), ('ve', ascan)])
                p.bank_release(bidx, 've', p.cnt['ve'])
            emit_linear(ctx, ies, 'l1a_bm', ctx.aps['w_l1a_BT'], D, 128, x_bf, evac_b,
                        m_groups=[(i * 128, 128)])
            p.op('ve', (lambda e: e.tensor_tensor_scan(h_t[:], a_t[:], b_t[:],
                                                       0.0, ALU.mult, ALU.add)),
                 waits=[('se', p.cnt['se']), ('ve', p.cnt['ve'])])
            p.op('po', (lambda e, i=i: e.tensor_copy(h_bf[i][:], h_t[:])),
                 waits=[('ve', p.cnt['ve'])])
    scan_done_po = p.cnt['po']

    KC_C, KC_D = 2, 8
    s_dt = Slabs(ctx, ctx.aps['w_l1a_DT'], D)
    wc = sb(nc, es, 'l1a_wc', [128, 256], F32)
    wcb = sb(nc, es, 'l1a_wcb', [128, 256], BF16)
    wc_gate = None
    for mf in range(NF):
        waits = [wc_gate] if wc_gate else []
        p.dma('po', wc[:].rearrange('p (kc m) -> p kc m', m=128),
              ctx.aps['w_l1a_CT'][:, mf * 128:(mf + 1) * 128]
              .rearrange('(kc p) m -> p kc m', p=128), waits=waits)
        gl = p.cnt['gq']
        p.op('po', (lambda e: e.tensor_copy(wcb[:], wc[:])), waits=[('gq', gl)])
        wc_cast = p.cnt['po']
        bi, tile, cast = s_dt.get(mf * 128, 128)
        for ni in range(NT):
            bidx = (mf * NT + ni) % 6
            bank = p.bank(bidx)
            c0 = ni * 512
            for k in range(KC_C):
                p.op('te', (lambda e, k=k, bank=bank, c0=c0:
                            e.matmul(bank[:, 0:512], wcb[:, k * 128:(k + 1) * 128],
                                     h_bf[k][:, c0:c0 + 512], start=(k == 0), stop=False)),
                     waits=([('po', wc_cast)] + p.bank_waits(bidx)) if k == 0 else ())
            for k in range(KC_D):
                p.op('te', (lambda e, k=k, bank=bank, c0=c0, tile=tile:
                            e.matmul(bank[:, 0:512], tile(k),
                                     x_bf[k][:, c0:c0 + 512], start=False,
                                     stop=(k == KC_D - 1))),
                     waits=[('po', cast)] if k == 0 else ())
            te_cnt = p.cnt['te']
            p.op('ve', (lambda e, mf=mf, c0=c0, bank=bank:
                        e.tensor_tensor(x[mf][:, c0:c0 + 512], x[mf][:, c0:c0 + 512],
                                        bank[:, 0:512], ALU.add)),
                 waits=[('te', te_cnt)])
            p.bank_release(bidx, 've', p.cnt['ve'])
        s_dt.done(bi, p.cnt['te'])
        wc_gate = ('te', p.cnt['te'])
    emit_postln(ctx, es, 'l1a_ln', x, x, eps=1e-5, dst_bf=x_bf,
                wait_src=[('ve', p.cnt['ve'])])


def emit_l0_ffn(ctx, es, x, x_bf):
    nc, p = ctx.nc, ctx.p
    tt = ctx.t
    aps = ctx.aps
    EPS = 1.1920929e-07

    zerocol = tt['fconsts'][:, 5:6]

    emit_postln(ctx, es, 'l0f_n1', x, None, eps=EPS, dst_bf=x_bf, rms=True,
                gamma=[ctx.col('l0f_n1', f) for f in range(NF)], beta=[zerocol] * NF,
                wait_src=[('sq', p.cnt['sq'])])

    p.dump('n1_xn0', x_bf[0][:])
    delta = [sb(nc, es, f'l0f_delta{i}', [128, T], BF16) for i in range(6)]
    oes = es.enter_context(ExitStack())
    o_all = [sb(nc, oes, f'l0f_o{i}', [128, T], BF16) for i in range(NF)]
    sq_wq = Slabs(ctx, aps['w_l0f_wqT'], D)
    sq_wk = Slabs(ctx, aps['w_l0f_wkT'], D)
    sq_wv = Slabs(ctx, aps['w_l0f_wvT'], D)

    with ExitStack() as pes:
        q_bf = sb(nc, pes, 'l0f_q', [128, T], BF16)
        k_bf = sb(nc, pes, 'l0f_k', [128, T], BF16)
        vtm = sb(nc, pes, 'l0f_vtm', [128, T], BF16)
        P_s = sb(nc, pes, 'l0f_P', [128, T], BF16)
        PT_s = sb(nc, pes, 'l0f_PT', [128, T], BF16)
        vfm = PT_s
        rstd_scr = sb(nc, pes, 'l0f_rstd', [128, 1536], F32)
        bsc = sb(nc, pes, 'l0f_bsc', [128, 512], F32)
        cols = sb(nc, pes, 'l0f_cols', [128, 8], F32)
        otok = sb(nc, pes, 'l0f_otok', [128, 64], BF16)
        SLOTS = [(4, 0), (4, 32), (4, 64), (5, 0), (5, 32), (5, 64), (6, 0), (6, 32)]

        def proj_pair(slab, hp, wait_rhs):
            bi, tile, cast = slab.get(hp * 128, 128)
            for ci in range(NT):
                w0 = [('po', cast)] + p.bank_waits(ci) + wait_rhs
                for k in range(8):
                    p.op('te', (lambda e, k=k, ci=ci, tile=tile:
                                e.matmul(p.bank(ci)[0:128, 0:512], tile(k),
                                         x_bf[k][:, ci * 512:(ci + 1) * 512],
                                         start=(k == 0), stop=(k == 7))),
                         waits=w0 if k == 0 else ())
            slab.done(bi, p.cnt['te'])
            return p.cnt['te']

        def qk_rms(out_bf, gcol, fold8, te_proj):
            p.op('se', (lambda e: e.activation(P_s[:], p.bank_span(0, 4), AF.Square)),
                 waits=[('te', te_proj)] + p.bank_waits(4) + p.bank_waits(5) + p.bank_waits(6))
            sq_done = p.cnt['se']
            for si, (h2, ci) in enumerate([(a, b) for a in range(2) for b in range(4)]):
                b, r = SLOTS[si]
                p.op('te', (lambda e, h2=h2, ci=ci, b=b, r=r:
                            e.matmul(p.bank(b)[r:r + 1, 0:512],
                                     tt['ones_col_bf'][h2 * 64:(h2 + 1) * 64, 0:1],
                                     P_s[h2 * 64:(h2 + 1) * 64, ci * 512:(ci + 1) * 512],
                                     start=True, stop=True)),
                     waits=[('se', sq_done)] if si == 0 else ())
            te_norm = p.cnt['te']
            for si in range(8):
                b, r = SLOTS[si]
                p.op('se', (lambda e, b=b, r=r: e.activation(
                    p.bank(b)[r:r + 1, 0:512], p.bank(b)[r:r + 1, 0:512], AF.Ln,
                    scale=1.0 / 64.0, bias=float(EPS))),
                     waits=([('te', te_norm)] if si == 0 else [('se', p.cnt['se'])]))
                p.op('se', (lambda e, b=b, r=r: e.activation(
                    p.bank(b)[r:r + 1, 0:512], p.bank(b)[r:r + 1, 0:512], AF.Exp,
                    scale=-0.5, bias=(-2.0794415416798357 if fold8 else 0.0))),
                     waits=[('se', p.cnt['se'])])
            se_rows = p.cnt['se']
            for si in range(8):
                b, r = SLOTS[si]
                j = (4, 5, 6).index(b)
                p.op('ve', (lambda e, b=b, r=r, j=j: e.tensor_copy(
                    rstd_scr[r:r + 1, j * 512:(j + 1) * 512],
                    p.bank(b)[r:r + 1, 0:512])),
                     waits=[('se', se_rows)] if si == 0 else ())
            ve_rows = p.cnt['ve']
            for b in (4, 5):
                p.bank_release(b, 've', ve_rows)
            for si, (h2, ci) in enumerate([(a, b) for a in range(2) for b in range(4)]):
                b, r = SLOTS[si]
                j = (4, 5, 6).index(b)
                p.op('te', (lambda e, h2=h2, ci=ci, r=r, j=j:
                            e.matmul(p.bank(4 + ci)[h2 * 64:h2 * 64 + 64, 0:512],
                                     tt['ones128'][r:r + 1, 0:64],
                                     rstd_scr[r:r + 1, j * 512:(j + 1) * 512],
                                     start=True, stop=True)),
                     waits=([('ve', ve_rows)] + p.bank_waits(4) + p.bank_waits(5)
                            + p.bank_waits(6) + p.bank_waits(7)) if si == 0 else ())
            te_b = p.cnt['te']
            for ci in range(NT):
                p.op('ve', (lambda e, ci=ci: e.tensor_copy(bsc[:], p.bank(4 + ci)[:, 0:512])),
                     waits=([('te', te_b)] if ci == 0 else []) + [('se', p.cnt['se']),
                                                                  ('ve', p.cnt['ve'])])
                p.bank_release(4 + ci, 've', p.cnt['ve'])
                p.op('ve', (lambda e, ci=ci: e.tensor_tensor(
                    bsc[:], p.bank(ci)[:, 0:512], bsc[:], ALU.mult)),
                     waits=[('ve', p.cnt['ve'])])
                p.bank_release(ci, 've', p.cnt['ve'])
                p.op('se', (lambda e, ci=ci, out_bf=out_bf, gcol=gcol: e.activation(
                    out_bf[:, ci * 512:(ci + 1) * 512], bsc[:], AF.Identity,
                    bias=zerocol, scale=gcol)), waits=[('ve', p.cnt['ve'])])
            return p.cnt['se']

        dma_w = p.cnt['sq']
        for hp in range(NF):
            wq_rhs = [('se', p.cnt['se']), ('sq', dma_w)]
            tq = proj_pair(sq_wq, hp, wq_rhs)
            qk_done = qk_rms(q_bf, ctx.col('l0f_qn'), True, tq)
            tk = proj_pair(sq_wk, hp, [])
            qk_done = qk_rms(k_bf, ctx.col('l0f_kn'), False, tk)
            bi, tile, cast = sq_wv.get(hp * 128, 128)
            for ci in range(NT):
                w0 = [('po', cast)] + p.bank_waits(ci)
                for k in range(8):
                    p.op('te', (lambda e, k=k, ci=ci, tile=tile:
                                e.matmul(p.bank(ci)[0:128, 0:512], tile(k),
                                         x_bf[k][:, ci * 512:(ci + 1) * 512],
                                         start=(k == 0), stop=(k == 7))),
                         waits=w0 if k == 0 else ())
                p.op('ve', (lambda e, ci=ci: e.tensor_copy(
                    vfm[:, ci * 512:(ci + 1) * 512], p.bank(ci)[:, 0:512])),
                     waits=[('te', p.cnt['te'])])
                p.bank_release(ci, 've', p.cnt['ve'])
            sq_wv.done(bi, p.cnt['te'])
            ve_vfm = p.cnt['ve']
            for m in range(16):
                b = 4 + m // 8
                p.op('te', (lambda e, m=m, b=b: e.transpose(
                    p.bank_bf(b)[:, (m % 8) * 128:(m % 8) * 128 + 128],
                    vfm[:, m * 128:(m + 1) * 128], tt['ident_bf'][:])),
                     waits=([('ve', ve_vfm)] + p.bank_waits(4) + p.bank_waits(5))
                     if m == 0 else ())
                if m % 8 == 7:
                    p.op('ve', (lambda e, m=m, b=b: e.tensor_copy(
                        vtm[:, (m - 7) * 128:(m + 1) * 128], p.bank_bf(b)[:, 0:1024])),
                         waits=[('te', p.cnt['te'])])
                    p.bank_release(b, 've', p.cnt['ve'])
            if hp == 0:
                p.dump('q0', q_bf[:])
                p.dump('k0', k_bf[:])
                p.dump('vtm0', vtm[:])
            negmax = cols[:, 0:1]; sums = cols[:, 1:2]; rsum = cols[:, 2:3]
            rsum_read = 0
            otok_read = 0
            pt_read = 0
            for h2 in range(2):
                for qt in range(16):
                    kw = (qt + 1) * 128
                    nk = qt + 1
                    nkc = (kw + 511) // 512
                    w0 = [('se', p.cnt['se']), ('ve', p.cnt['ve'])]
                    for b2 in range(nkc):
                        w0 += p.bank_waits(b2)
                    for c in range(nkc):
                        c0, c1 = c * 512, min(kw, (c + 1) * 512)
                        p.op('te', (lambda e, h2=h2, qt=qt, c0=c0, c1=c1:
                                    e.matmul(p.psum.ap()[0:128, c0:c1],
                                             q_bf[h2 * 64:h2 * 64 + 64, qt * 128:(qt + 1) * 128],
                                             k_bf[h2 * 64:h2 * 64 + 64, c0:c1],
                                             start=True, stop=True)),
                             waits=w0 if c == 0 else ())
                    p.op('te', (lambda e, kw=kw: e.matmul(
                        p.psum.ap()[0:128, kw - 128:kw], tt['ident_bf'][:],
                        tt['mask_causal_bf'][:], start=False, stop=True,
                        skip_group_check=True)))
                    te_s = p.cnt['te']
                    p.op('ve', (lambda e, kw=kw: e.tensor_reduce(
                        negmax[:], p.psum.ap()[0:128, 0:kw], AX.X, ALU.max, negate=True)),
                         waits=[('te', te_s)])
                    ve_max = p.cnt['ve']
                    p.op('se', (lambda e, kw=kw: e.activation(
                        P_s[:, 0:kw], p.psum.ap()[0:128, 0:kw], AF.Exp,
                        bias=negmax[:, 0:1], accum_out=sums[:, 0:1])),
                         waits=[('ve', ve_max)]
                         + ([('ve', rsum_read)] if rsum_read else [])
                         + ([('te', pt_read)] if pt_read else []))
                    se_exp = p.cnt['se']
                    for b2 in range(nkc):
                        p.bank_release(b2, 'se', se_exp)
                    p.op('ve', (lambda e: e.reciprocal(rsum[:], sums[:, 0:1])),
                         waits=[('se', se_exp)])
                    rsum_done = p.cnt['ve']
                    rsum_read = rsum_done
                    for ki in range(nk):
                        b = 4 + ki // 8
                        p.op('te', (lambda e, ki=ki, b=b: e.transpose(
                            p.bank_bf(b)[:, (ki % 8) * 128:(ki % 8) * 128 + 128],
                            P_s[:, ki * 128:(ki + 1) * 128], tt['ident_bf'][:])),
                             waits=([('se', se_exp)] + p.bank_waits(4) + p.bank_waits(5))
                             if ki == 0 else ())
                    te_pt = p.cnt['te']
                    pt_read = te_pt
                    nspan = min(nk, 8)
                    p.op('ve', (lambda e, nspan=nspan: e.tensor_copy(
                        PT_s[:, 0:nspan * 128], p.bank_bf(4)[:, 0:nspan * 128])),
                         waits=[('te', te_pt)])
                    if nk > 8:
                        p.op('ve', (lambda e, nk=nk: e.tensor_copy(
                            PT_s[:, 1024:nk * 128], p.bank_bf(5)[:, 0:(nk - 8) * 128])))
                    ve_pt = p.cnt['ve']
                    p.bank_release(4, 've', ve_pt)
                    p.bank_release(5, 've', ve_pt)
                    for ki in range(nk):
                        p.op('te', (lambda e, ki=ki, h2=h2, last=(ki == nk - 1):
                                    e.matmul(p.bank(6)[0:128, 0:64],
                                             PT_s[:, ki * 128:(ki + 1) * 128],
                                             vtm[:, ki * 128 + h2 * 64:ki * 128 + h2 * 64 + 64],
                                             start=(ki == 0), stop=last)),
                             waits=([('ve', ve_pt)] + p.bank_waits(6)) if ki == 0 else ())
                    te_pv = p.cnt['te']
                    p.op('ve', (lambda e: e.tensor_scalar(
                        otok[:], p.bank(6)[0:128, 0:64], rsum[:, 0:1], None, ALU.mult)),
                         waits=[('te', te_pv), ('ve', rsum_done)]
                         + ([('te', otok_read)] if otok_read else []))
                    ve_o = p.cnt['ve']
                    p.bank_release(6, 've', ve_o)
                    p.op('te', (lambda e, qt=qt: e.transpose(
                        p.bank_bf(7)[0:64, (qt % 8) * 128:(qt % 8) * 128 + 128],
                        otok[:], tt['ident_bf'][:])),
                         waits=[('ve', ve_o)] + (p.bank_waits(7) if qt % 8 == 0 else []))
                    otok_read = p.cnt['te']
                    if qt % 8 == 7:
                        p.op('ve', (lambda e, h2=h2, hp=hp, qt=qt: e.tensor_copy(
                            o_all[hp][h2 * 64:h2 * 64 + 64, (qt - 7) * 128:(qt + 1) * 128],
                            p.bank_bf(7)[0:64, 0:1024])),
                             waits=[('te', otok_read)])
                        p.bank_release(7, 've', p.cnt['ve'])
    p.dump('o_all0', o_all[0][:])
    p.dump('o_all7', o_all[7][:])
    def evac_wo(gi, m0, mw, ni, c0, cw, bidx, pap, te_cnt):
        p.op('ve', (lambda e, gi=gi, c0=c0, cw=cw, pap=pap: e.tensor_copy(
            delta[gi][:, c0:c0 + cw], pap[0:128, 0:cw])),
             waits=[('te', te_cnt)])
        p.bank_release(bidx, 've', p.cnt['ve'])
    emit_linear(ctx, oes, 'l0f_wo', aps['w_l0f_woT'], D, D, o_all, evac_wo,
                m_groups=[(256 + i * 128, 128) for i in range(6)],
                wait_rhs=[('ve', p.cnt['ve'])])
    p.barrier()
    oes.close()

    p.dump('postwo_x2', x[2][:])
    emit_postln(ctx, es, 'l0f_n2', x, None, eps=EPS, dst_bf=x_bf, rms=True,
                gamma=[ctx.col('l0f_n2', f) for f in range(NF)], beta=[zerocol] * NF,
                wait_src=[('ve', p.cnt['ve'])],
                src2=[None, None] + [delta[i] for i in range(6)])

    wts_tm = sb(nc, es, 'l0f_wtstm', [128, 64], F32)
    with ExitStack() as ges:
        logits = sb(nc, ges, 'l0f_logits', [4, T], F32)
        pscr = sb(nc, ges, 'l0f_pscr', [128, 4], F32)
        pscr2 = sb(nc, ges, 'l0f_pscr2', [128, 4], F32)
        keepm = sb(nc, ges, 'l0f_keepm', [128, 4], F32)
        cmp8 = sb(nc, ges, 'l0f_cmp8', [128, 4], U8)
        gc = sb(nc, ges, 'l0f_gcols', [128, 8], F32)

        def evac_gate(gi, m0, mw, ni, c0, cw, bidx, pap, te_cnt):
            p.op('ve', (lambda e, c0=c0, cw=cw, pap=pap: e.tensor_copy(
                logits[:, c0:c0 + cw], pap[0:4, 0:cw])), waits=[('te', te_cnt)])
            p.bank_release(bidx, 've', p.cnt['ve'])
        emit_linear(ctx, ges, 'l0f_gate', aps['w_l0f_gateT'], D, 4, x_bf, evac_gate,
                    m_groups=[(0, 4)])
        ve_lg = p.cnt['ve']
        for ci in range(16):
            p.op('te', (lambda e, ci=ci: e.transpose(
                p.bank(4)[0:128, ci * 4:ci * 4 + 4],
                logits[0:4, ci * 128:(ci + 1) * 128], tt['ident_f32'][0:4, 0:4])),
                 waits=([('ve', ve_lg)] + p.bank_waits(4)) if ci == 0 else ())
        te_lt = p.cnt['te']
        nm = gc[:, 0:1]; ssum = gc[:, 1:2]; rs = gc[:, 2:3]; pm1 = gc[:, 3:4]
        pm2 = gc[:, 4:5]; dcol = gc[:, 5:6]; dc2 = gc[:, 6:7]; rdc = gc[:, 7:8]
        eqf = sb(nc, ges, 'l0f_eqf', [128, 4], F32)
        t1s = sb(nc, ges, 'l0f_t1', [128, 4], F32)
        t2s = sb(nc, ges, 'l0f_t2', [128, 4], F32)
        kfs = sb(nc, ges, 'l0f_kf', [128, 4], F32)
        cmp8b = sb(nc, ges, 'l0f_cmp8b', [128, 4], U8)

        def vstep(fn, extra=()):
            p.op('ve', fn, waits=[('ve', p.cnt['ve'])] + list(extra))
        for ci in range(16):
            sl = p.bank(4)[0:128, ci * 4:ci * 4 + 4]
            vstep((lambda e, sl=sl: e.tensor_reduce(nm[:], sl, AX.X, ALU.max,
                                                    negate=True)),
                  extra=([('te', te_lt)] if ci == 0 else [('se', p.cnt['se'])]))
            p.op('se', (lambda e, sl=sl: e.activation(pscr[:], sl, AF.Exp,
                                                      bias=nm[:, 0:1],
                                                      accum_out=ssum[:, 0:1])),
                 waits=[('ve', p.cnt['ve'])])
            vstep(lambda e: e.reciprocal(rs[:], ssum[:, 0:1]),
                  extra=[('se', p.cnt['se'])])
            vstep(lambda e: e.tensor_scalar(pscr2[:], pscr[:], rs[:, 0:1], None, ALU.mult))
            vstep(lambda e: e.tensor_reduce(pm1[:], pscr2[:], AX.X, ALU.max))
            vstep(lambda e: e.tensor_scalar(cmp8[:], pscr2[:], pm1[:, 0:1], None, ALU.is_ge))
            vstep(lambda e: e.tensor_copy(eqf[:], cmp8[:]))
            vstep(lambda e: e.tensor_tensor(t1s[:], pscr2[:], eqf[:], ALU.mult))
            vstep(lambda e: e.tensor_tensor(t2s[:], pscr2[:], t1s[:], ALU.subtract))
            vstep(lambda e: e.tensor_reduce(pm2[:], t2s[:], AX.X, ALU.max))
            vstep(lambda e: e.tensor_scalar(cmp8b[:], pscr2[:], pm2[:, 0:1], None, ALU.is_ge))
            vstep(lambda e: e.tensor_copy(keepm[:], cmp8b[:]))
            vstep(lambda e: e.tensor_tensor(dcol[:], pm1[:], pm2[:], ALU.add))
            vstep(lambda e: e.tensor_single_scalar(dc2[:], dcol[:], 1e-8, ALU.add))
            vstep(lambda e: e.reciprocal(rdc[:], dc2[:]))
            vstep(lambda e: e.tensor_tensor(kfs[:], keepm[:], pscr2[:], ALU.mult))
            vstep((lambda e, ci=ci: e.tensor_scalar(wts_tm[:, ci * 4:ci * 4 + 4],
                                                    kfs[:], rdc[:, 0:1], None, ALU.mult)))
        p.bank_release(4, 've', p.cnt['ve'])
        p.dump('xf0', x_bf[0][:])
        p.dump('logits', logits[:])
        p.dump('wts_tm', wts_tm[:])
        p.barrier()

    slabs_e = {nm2: Slabs(ctx, aps['w_l0f_' + nm2], kk) for nm2, kk in [
        ('e0upT', 1024), ('e0dnT', 2048), ('e1w1T', 1024), ('e1w2T', 1024),
        ('e1dnT', 2048), ('e2l1T', 1024), ('e2l2T', 2048), ('e2l3T', 2048),
        ('e2l4T', 2048), ('e3dnT', 1024), ('e3upT', 1024), ('e3outT', 2048)]}
    with ExitStack() as ees:
        hidA = sb(nc, ees, 'l0f_hidA', [128, 8192], BF16)
        hidB = sb(nc, ees, 'l0f_hidB', [128, 8192], BF16)
        wts_b = sb(nc, ees, 'l0f_wtsb', [128, T], F32)
        wrow = sb(nc, ees, 'l0f_wrow', [1, T], F32)
        hid_gate = {id(hidA): 0, id(hidB): 0}

        def hid_layer(slab, src_fn, KC, nm_t, act, out_t, tq, wait_extra=(), src_t=None):
            for m in range(nm_t):
                bi, tile, cast = slab.get(m * 128, 128)
                b = 6 + m % 2
                w0 = [('po', cast)] + p.bank_waits(b) + list(wait_extra)
                if m == 0 and hid_gate[id(out_t)]:
                    w0.append(('te', hid_gate[id(out_t)]))
                for k in range(KC):
                    p.op('te', (lambda e, k=k, b=b, tile=tile, src_fn=src_fn:
                                e.matmul(p.bank(b)[0:128, 0:512], tile(k), src_fn(k),
                                         start=(k == 0), stop=(k == KC - 1))),
                         waits=w0 if k == 0 else ())
                slab.done(bi, p.cnt['te'])
                hg = hid_gate[id(out_t)]
                p.op('se', (lambda e, m=m, b=b, out_t=out_t, act=act: e.activation(
                    out_t[:, m * 512:(m + 1) * 512], p.bank(b)[0:128, 0:512], act)),
                     waits=[('te', p.cnt['te'])] + ([('te', hg)] if hg else []))
                p.bank_release(b, 'se', p.cnt['se'])
            if src_t is not None:
                hid_gate[id(src_t)] = p.cnt['te']

        def weight_hid(out_t, nm_t, ew):
            for m in range(nm_t):
                p.op('ve', (lambda e, m=m, out_t=out_t, ew=ew: e.tensor_tensor(
                    out_t[:, m * 512:(m + 1) * 512], out_t[:, m * 512:(m + 1) * 512],
                    wts_b[:, ew * 512:(ew + 1) * 512], ALU.mult)),
                     waits=[('se', p.cnt['se']), ('ve', p.cnt['ve'])])

        def down_proj(slab, hid_t, KC, first, last):
            for fi in range(6):
                bi, tile, cast = slab.get(256 + fi * 128, 128)
                w0 = [('po', cast), ('ve', p.cnt['ve'])] + (p.bank_waits(fi) if first else [])
                for k in range(KC):
                    p.op('te', (lambda e, k=k, fi=fi, tile=tile, hid_t=hid_t:
                                e.matmul(p.bank(fi)[0:128, 0:512], tile(k),
                                         hid_t[:, k * 512:(k + 1) * 512],
                                         start=(first and k == 0),
                                         stop=(last and k == KC - 1),
                                         skip_group_check=True)),
                         waits=w0 if k == 0 else ())
                slab.done(bi, p.cnt['te'])
            hid_gate[id(hid_t)] = p.cnt['te']

        for tq in range(NT):
            tc0 = tq * 512
            xfq = lambda k, tc0=tc0: x_bf[k][:, tc0:tc0 + 512]
            vw = p.cnt['ve']
            for ew in range(4):
                for c4 in range(4):
                    ti = tq * 4 + c4
                    p.op('te', (lambda e, ew=ew, c4=c4, ti=ti: e.transpose(
                        p.bank(4 + ew)[0:1, c4 * 128:(c4 + 1) * 128],
                        wts_tm[0:128, ti * 4 + ew:ti * 4 + ew + 1], tt['ident_f32'][:])),
                         waits=([('ve', vw)] + p.bank_waits(4 + ew)) if c4 == 0 else ())
            te_w = p.cnt['te']
            for ew in range(4):
                p.op('ve', (lambda e, ew=ew: e.tensor_copy(
                    wrow[0:1, ew * 512:(ew + 1) * 512], p.bank(4 + ew)[0:1, 0:512])),
                     waits=[('te', te_w)])
                p.bank_release(4 + ew, 've', p.cnt['ve'])
            ve_wr = p.cnt['ve']
            for ew in range(4):
                p.op('te', (lambda e, ew=ew: e.matmul(
                    p.bank(4 + ew)[0:128, 0:512], tt['ones_row_f32'][:],
                    wrow[0:1, ew * 512:(ew + 1) * 512], start=True, stop=True)),
                     waits=[('ve', ve_wr)] + p.bank_waits(4 + ew))
            te_wb = p.cnt['te']
            for ew in range(4):
                p.op('ve', (lambda e, ew=ew: e.tensor_copy(
                    wts_b[:, ew * 512:(ew + 1) * 512], p.bank(4 + ew)[0:128, 0:512])),
                     waits=[('te', te_wb)])
                p.bank_release(4 + ew, 've', p.cnt['ve'])

            hid_layer(slabs_e['e0upT'], xfq, 8, 16, AF.Gelu, hidA, tq)
            if tq == 0:
                p.dump('hid0', hidA[:])
                p.dump('wtsb0', wts_b[:])
            weight_hid(hidA, 16, 0)
            down_proj(slabs_e['e0dnT'], hidA, 16, True, False)
            hid_layer(slabs_e['e1w1T'], xfq, 8, 16, AF.Silu, hidB, tq)
            for m in range(16):
                bi, tile, cast = slabs_e['e1w2T'].get(m * 128, 128)
                b = 6 + m % 2
                w0 = [('po', cast)] + p.bank_waits(b)
                for k in range(8):
                    p.op('te', (lambda e, k=k, b=b, tile=tile, xfq=xfq:
                                e.matmul(p.bank(b)[0:128, 0:512], tile(k), xfq(k),
                                         start=(k == 0), stop=(k == 7))),
                         waits=w0 if k == 0 else ())
                slabs_e['e1w2T'].done(bi, p.cnt['te'])
                hg = hid_gate[id(hidB)]
                p.op('ve', (lambda e, m=m, b=b: e.tensor_tensor(
                    hidB[:, m * 512:(m + 1) * 512], hidB[:, m * 512:(m + 1) * 512],
                    p.bank(b)[0:128, 0:512], ALU.mult)),
                     waits=[('te', p.cnt['te']), ('se', p.cnt['se'])])
                p.bank_release(b, 've', p.cnt['ve'])
            weight_hid(hidB, 16, 1)
            down_proj(slabs_e['e1dnT'], hidB, 16, False, False)
            hid_layer(slabs_e['e2l1T'], xfq, 8, 16, AF.Gelu, hidA, tq)
            hid_layer(slabs_e['e2l2T'],
                      (lambda k: hidA[:, k * 512:(k + 1) * 512]), 16, 16, AF.Gelu, hidB, tq,
                      wait_extra=[('se', p.cnt['se'])], src_t=hidA)
            hid_layer(slabs_e['e2l3T'],
                      (lambda k: hidB[:, k * 512:(k + 1) * 512]), 16, 16, AF.Gelu, hidA, tq,
                      wait_extra=[('se', p.cnt['se'])], src_t=hidB)
            weight_hid(hidA, 16, 2)
            down_proj(slabs_e['e2l4T'], hidA, 16, False, False)
            hid_layer(slabs_e['e3dnT'], xfq, 8, 8, AF.Gelu, hidB, tq)
            hid_layer(slabs_e['e3upT'],
                      (lambda k: hidB[:, k * 512:(k + 1) * 512]), 8, 16, AF.Gelu, hidA, tq,
                      wait_extra=[('se', p.cnt['se'])], src_t=hidB)
            weight_hid(hidA, 16, 3)
            down_proj(slabs_e['e3outT'], hidA, 16, False, True)
            for fi in range(6):
                p.op('ve', (lambda e, fi=fi, tc0=tc0: e.tensor_tensor(
                    delta[fi][:, tc0:tc0 + 512], delta[fi][:, tc0:tc0 + 512],
                    p.bank(fi)[0:128, 0:512], ALU.add)),
                     waits=[('te', p.cnt['te'])])
                p.bank_release(fi, 've', p.cnt['ve'])
            p.flush()
        p.barrier()

    for ff in range(NF):
        p.dump(f'premoe_x{ff}', x[ff][:])
    emit_postln(ctx, es, 'l0f_ln', x, x, eps=1e-5, dst_bf=x_bf,
                wait_src=[('ve', p.cnt['ve'])],
                src2=[None, None] + [delta[i] for i in range(6)], src_scale=2.0)


def emit_l1_ffn(ctx, es, x, x_bf):
    nc, p = ctx.nc, ctx.p
    tt = ctx.t
    aps = ctx.aps
    zerocol = tt['fconsts'][:, 5:6]
    delta = [sb(nc, es, f'l1f_delta{i}', [128, T], BF16) for i in range(NF)]
    SLOTS = [(4, 0), (4, 32), (4, 64), (5, 0), (5, 32), (5, 64), (6, 0), (6, 32)]
    sh_gate = [0, 0]

    def build_sh(l, dcur):
        for f in range(NF):
            w = [('ve', p.cnt['ve']), ('te', p.cnt['te']), ('se', p.cnt['se']),
                 ('po', p.cnt['po'])]
            if f < 4:
                if dcur is None:
                    continue
                p.op('ve', (lambda e, f=f, dcur=dcur: e.tensor_tensor(
                    x_bf[f][:], x[f][:], dcur[f][:], ALU.add)), waits=w)
            else:
                p.op('ve', (lambda e, f=f: e.memset(x_bf[f][:, 0:1], 0.0)), waits=w)
                if dcur is None:
                    p.op('po', (lambda e, f=f: e.tensor_copy(
                        x_bf[f][:, 1:T], x[f][:, 0:T - 1])), waits=[('ve', p.cnt['ve'])])
                else:
                    p.op('ve', (lambda e, f=f, dcur=dcur: e.tensor_tensor(
                        x_bf[f][:, 1:T], x[f][:, 0:T - 1], dcur[f][:, 0:T - 1],
                        ALU.add)))

    def qk_l2rms(out_t, gcol, te_proj, vstage, rstd_scr, bsc):
        p.op('se', (lambda e: e.activation(vstage[:], p.bank_span(0, 4), AF.Square)),
             waits=[('te', te_proj)] + p.bank_waits(4) + p.bank_waits(5) + p.bank_waits(6))
        sq_done = p.cnt['se']
        for si, (h2, ci) in enumerate([(a, b) for a in range(2) for b in range(4)]):
            b, r = SLOTS[si]
            p.op('te', (lambda e, h2=h2, ci=ci, b=b, r=r:
                        e.matmul(p.bank(b)[r:r + 1, 0:512],
                                 tt['ones_col_bf'][h2 * 64:(h2 + 1) * 64, 0:1],
                                 vstage[h2 * 64:(h2 + 1) * 64, ci * 512:(ci + 1) * 512],
                                 start=True, stop=True)),
                 waits=[('se', sq_done)] if si == 0 else ())
        te_norm = p.cnt['te']
        for si in range(8):
            b, r = SLOTS[si]
            p.op('se', (lambda e, b=b, r=r: e.activation(
                p.bank(b)[r:r + 1, 0:512], p.bank(b)[r:r + 1, 0:512], AF.Ln,
                bias=1e-24)),
                 waits=([('te', te_norm)] if si == 0 else [('se', p.cnt['se'])]))
            p.op('se', (lambda e, b=b, r=r: e.activation(
                p.bank(b)[r:r + 1, 0:512], p.bank(b)[r:r + 1, 0:512], AF.Exp,
                scale=-0.5)), waits=[('se', p.cnt['se'])])
        se_rows = p.cnt['se']

        def slot_row(b, r):
            if b == 6:
                return bsc[r:r + 1, 0:512]
            j = (4, 5).index(b)
            return rstd_scr[r:r + 1, j * 512:(j + 1) * 512]
        for si in range(8):
            b, r = SLOTS[si]
            p.op('ve', (lambda e, b=b, r=r: e.tensor_copy(
                slot_row(b, r), p.bank(b)[r:r + 1, 0:512])),
                 waits=[('se', se_rows)] if si == 0 else ())
        ve_rows = p.cnt['ve']
        for b in (4, 5):
            p.bank_release(b, 've', ve_rows)
        for si, (h2, ci) in enumerate([(a, b) for a in range(2) for b in range(4)]):
            b, r = SLOTS[si]
            p.op('te', (lambda e, h2=h2, ci=ci, b=b, r=r:
                        e.matmul(p.bank(4 + ci)[h2 * 64:h2 * 64 + 64, 0:512],
                                 tt['ones128'][r:r + 1, 0:64],
                                 slot_row(b, r),
                                 start=True, stop=True)),
                 waits=([('ve', ve_rows)] + p.bank_waits(4) + p.bank_waits(5)
                        + p.bank_waits(6) + p.bank_waits(7)) if si == 0 else ())
        te_b = p.cnt['te']
        for ci in range(NT):
            p.op('ve', (lambda e, ci=ci: e.tensor_copy(bsc[:], p.bank(4 + ci)[:, 0:512])),
                 waits=([('te', te_b)] if ci == 0 else []) + [('se', p.cnt['se']),
                                                              ('ve', p.cnt['ve'])])
            p.bank_release(4 + ci, 've', p.cnt['ve'])
            p.op('ve', (lambda e, ci=ci: e.tensor_tensor(
                bsc[:], p.bank(ci)[:, 0:512], bsc[:], ALU.mult)),
                 waits=[('ve', p.cnt['ve'])])
            p.bank_release(ci, 've', p.cnt['ve'])
            p.op('se', (lambda e, ci=ci, out_t=out_t, gcol=gcol: e.activation(
                out_t[:, ci * 512:(ci + 1) * 512], bsc[:], AF.Identity,
                bias=zerocol, scale=gcol)), waits=[('ve', p.cnt['ve'])])
        return p.cnt['se']

    for l in range(2):
        les1 = ExitStack()
        q_t = [sb(nc, les1, f'l1f_q{i}_{l}', [128, T], BF16) for i in range(4)]
        k_t = [sb(nc, les1, f'l1f_k{i}_{l}', [128, T], BF16) for i in range(4)]
        vtm = sb(nc, les1, f'l1f_vtm_{l}', [128, 16 * 512], BF16)
        vstage = sb(nc, les1, f'l1f_vstage_{l}', [128, T], BF16)
        tabb = sb(nc, les1, f'l1f_tab_{l}', [128, 1024], BF16)
        rstd_scr = sb(nc, les1, f'l1f_rstd_{l}', [128, 1024], F32)
        bsc = sb(nc, les1, f'l1f_bsc_{l}', [128, 512], F32)
        cols = sb(nc, les1, f'l1f_cols_{l}', [128, 8], F32)
        otok = sb(nc, les1, f'l1f_otok_{l}', [128, 64], BF16)
        sq_qkv = Slabs(ctx, aps[f'w_l1f_qkv{l}'], D)
        sq_wo = Slabs(ctx, aps[f'w_l1f_wo{l}'], 512)
        sq_w1 = Slabs(ctx, aps[f'w_l1f_w1{l}'], D)
        sq_w2a = Slabs(ctx, aps[f'w_l1f_w2{l}'], 1408)
        sq_w2b = Slabs(ctx, aps[f'w_l1f_w2{l}'], 1408, k0=1408)
        build_sh(l, None if l == 0 else delta)
        sh_done = [('po', p.cnt['po']), ('ve', p.cnt['ve'])]
        for gi in range(12):
            bi, tile, cast = sq_qkv.get(gi * 128, 128)
            for ci in range(NT):
                w0 = [('po', cast)] + p.bank_waits(ci) + (sh_done if gi == 0 else [])
                for k in range(8):
                    p.op('te', (lambda e, k=k, ci=ci, tile=tile:
                                e.matmul(p.bank(ci)[0:128, 0:512], tile(k),
                                         x_bf[k][:, ci * 512:(ci + 1) * 512],
                                         start=(k == 0), stop=(k == 7))),
                         waits=w0 if k == 0 else ())
            sq_qkv.done(bi, p.cnt['te'])
            if gi < 4:
                qk_l2rms(q_t[gi], ctx.col(f'l1f_qs{l}'), p.cnt['te'], vstage, rstd_scr, bsc)
            elif gi < 8:
                qk_l2rms(k_t[gi - 4], ctx.col(f'l1f_ks{l}'), p.cnt['te'], vstage, rstd_scr, bsc)
            else:
                vt = gi - 8
                for ci in range(NT):
                    p.op('ve', (lambda e, ci=ci: e.tensor_copy(
                        vstage[:, ci * 512:(ci + 1) * 512], p.bank(ci)[:, 0:512])),
                         waits=[('te', p.cnt['te'])] if ci == 0 else ())
                    p.bank_release(ci, 've', p.cnt['ve'])
                ve_v = p.cnt['ve']
                for m in range(16):
                    b = 4 + m // 8
                    p.op('te', (lambda e, m=m, b=b: e.transpose(
                        p.bank_bf(b)[:, (m % 8) * 128:(m % 8) * 128 + 128],
                        vstage[:, m * 128:(m + 1) * 128], tt['ident_bf'][:])),
                         waits=([('ve', ve_v)] + p.bank_waits(4) + p.bank_waits(5))
                         if m == 0 else ())
                    if m % 8 == 7:
                        p.op('ve', (lambda e, m=m, b=b, vt=vt: e.tensor_copy(
                            vtm[:].rearrange('p (tt c) -> p tt c', c=512)
                            [:, m - 7:m + 1, vt * 128:(vt + 1) * 128],
                            p.bank_bf(b)[:, 0:1024]
                            .rearrange('p (tt c) -> p tt c', c=128))),
                             waits=[('te', p.cnt['te'])])
                        p.bank_release(b, 've', p.cnt['ve'])
        for ci in range(NT):
            for side in range(2):
                p.dma('po', tabb[:],
                      aps['c_l1f_tab'][:, ci * 2048 + side * 1024:
                                       ci * 2048 + side * 1024 + 1024],
                      waits=[('te', p.cnt['te']), ('ve', p.cnt['ve'])])
                tl = p.cnt['gq']
                for f in range(4):
                    qt_ = q_t[f] if side == 0 else k_t[f]
                    p.op('te', (lambda e, qt_=qt_, ci=ci: e.matmul(
                        p.bank(6)[:, 0:512],
                        tt['permrot_bf'][:], qt_[:, ci * 512:(ci + 1) * 512],
                        start=True, stop=True)),
                         waits=[('gq', tl), ('se', p.cnt['se']),
                                ('ve', p.cnt['ve'])] + p.bank_waits(6))
                    te_pm = p.cnt['te']
                    p.op('ve', (lambda e, qt_=qt_, ci=ci: e.tensor_tensor(
                        bsc[:], qt_[:, ci * 512:(ci + 1) * 512], tabb[:, 0:512],
                        ALU.mult)),
                         waits=[('gq', tl), ('ve', p.cnt['ve']), ('se', p.cnt['se'])])
                    p.op('ve', (lambda e, qt_=qt_, ci=ci: e.tensor_tensor(
                        qt_[:, ci * 512:(ci + 1) * 512], p.bank(6)[:, 0:512],
                        tabb[:, 512:1024], ALU.mult)),
                         waits=[('te', te_pm), ('ve', p.cnt['ve'])])
                    p.op('ve', (lambda e, qt_=qt_, ci=ci: e.tensor_tensor(
                        qt_[:, ci * 512:(ci + 1) * 512],
                        qt_[:, ci * 512:(ci + 1) * 512], bsc[:], ALU.add)),
                         waits=[('ve', p.cnt['ve'])])
                    p.bank_release(6, 've', p.cnt['ve'])
        negmax = cols[:, 0:1]; sums = cols[:, 1:2]; rsum = cols[:, 2:3]
        rsum_read = 0
        otok_read = 0
        pt_read = 0
        for h2x in range(8):
            ft, h2 = h2x // 2, h2x % 2
            qtile = q_t[ft]; ktile = k_t[ft]
            for qt in range(16):
                nk = min(qt + 1, 5)
                kc0 = qt + 1 - nk
                kw = nk * 128
                w0 = [('se', p.cnt['se']), ('ve', p.cnt['ve'])]
                w0 += p.bank_waits(0) + p.bank_waits(1)
                nkc = (kw + 511) // 512
                for c in range(nkc):
                    c0, c1 = c * 512, min(kw, (c + 1) * 512)
                    p.op('te', (lambda e, h2=h2, qt=qt, c0=c0, c1=c1, kc0=kc0,
                                qtile=qtile, ktile=ktile:
                                e.matmul(p.psum.ap()[0:128, c0:c1],
                                         qtile[h2 * 64:h2 * 64 + 64, qt * 128:(qt + 1) * 128],
                                         ktile[h2 * 64:h2 * 64 + 64,
                                               kc0 * 128 + c0:kc0 * 128 + c1],
                                         start=True, stop=True)),
                         waits=w0 if c == 0 else ())
                p.op('te', (lambda e, kw=kw: e.matmul(
                    p.psum.ap()[0:128, kw - 128:kw], tt['ident_bf'][:],
                    tt['mask_causal_bf'][:], start=False, stop=True,
                    skip_group_check=True)))
                if nk == 5:
                    p.op('te', (lambda e: e.matmul(
                        p.psum.ap()[0:128, 0:128], tt['ident_bf'][:],
                        tt['mask_keep_ge_bf'][:], start=False, stop=True,
                        skip_group_check=True)))
                te_s = p.cnt['te']
                p.op('ve', (lambda e, kw=kw: e.tensor_reduce(
                    negmax[:], p.psum.ap()[0:128, 0:kw], AX.X, ALU.max, negate=True)),
                     waits=[('te', te_s)])
                ve_max = p.cnt['ve']
                p.op('se', (lambda e, kw=kw: e.activation(
                    vstage[:, 0:kw], p.psum.ap()[0:128, 0:kw], AF.Exp,
                    bias=negmax[:, 0:1], accum_out=sums[:, 0:1])),
                     waits=[('ve', ve_max)]
                     + ([('ve', rsum_read)] if rsum_read else [])
                     + ([('te', pt_read)] if pt_read else []))
                se_exp = p.cnt['se']
                for b2 in range(nkc):
                    p.bank_release(b2, 'se', se_exp)
                p.op('ve', (lambda e: e.reciprocal(rsum[:], sums[:, 0:1])),
                     waits=[('se', se_exp)])
                rsum_done = p.cnt['ve']
                rsum_read = rsum_done
                for ki in range(nk):
                    p.op('te', (lambda e, ki=ki: e.transpose(
                        p.bank_bf(4)[:, ki * 128:ki * 128 + 128],
                        vstage[:, ki * 128:(ki + 1) * 128], tt['ident_bf'][:])),
                         waits=([('se', se_exp)] + p.bank_waits(4)) if ki == 0 else ())
                te_pt = p.cnt['te']
                pt_read = te_pt
                p.op('ve', (lambda e, kw=kw: e.tensor_copy(
                    vstage[:, 1024:1024 + kw], p.bank_bf(4)[:, 0:kw])), waits=[('te', te_pt)])
                ve_pt = p.cnt['ve']
                p.bank_release(4, 've', ve_pt)
                for ki in range(nk):
                    p.op('te', (lambda e, ki=ki, kc0=kc0, h2x=h2x, last=(ki == nk - 1):
                                e.matmul(p.bank(6)[0:128, 0:64],
                                         vstage[:, 1024 + ki * 128:1024 + (ki + 1) * 128],
                                         vtm[:, (kc0 + ki) * 512 + h2x * 64:
                                             (kc0 + ki) * 512 + h2x * 64 + 64],
                                         start=(ki == 0), stop=last)),
                         waits=([('ve', ve_pt)] + p.bank_waits(6)) if ki == 0 else ())
                te_pv = p.cnt['te']
                p.op('ve', (lambda e: e.tensor_scalar(
                    otok[:], p.bank(6)[0:128, 0:64], rsum[:, 0:1], None, ALU.mult)),
                     waits=[('te', te_pv), ('ve', rsum_done)]
                     + ([('te', otok_read)] if otok_read else []))
                ve_o = p.cnt['ve']
                p.bank_release(6, 've', ve_o)
                p.op('te', (lambda e, qt=qt: e.transpose(
                    p.bank_bf(7)[0:64, (qt % 8) * 128:(qt % 8) * 128 + 128],
                    otok[:], tt['ident_bf'][:])),
                     waits=[('ve', ve_o)] + (p.bank_waits(7) if qt % 8 == 0 else []))
                otok_read = p.cnt['te']
                if qt % 8 == 7:
                    p.op('ve', (lambda e, h2=h2, ft=ft, qt=qt: e.tensor_copy(
                        x_bf[ft][h2 * 64:h2 * 64 + 64, (qt - 7) * 128:(qt + 1) * 128],
                        p.bank_bf(7)[0:64, 0:1024])),
                         waits=[('te', otok_read)])
                    p.bank_release(7, 've', p.cnt['ve'])
        def evac_wo(gi, m0, mw, ni, c0, cw, bidx, pap, te_cnt, l=l):
            if l == 0:
                p.op('ve', (lambda e, gi=gi, c0=c0, cw=cw, pap=pap: e.tensor_copy(
                    delta[gi][:, c0:c0 + cw], pap[0:128, 0:cw])),
                     waits=[('te', te_cnt)])
            else:
                p.op('ve', (lambda e, gi=gi, c0=c0, cw=cw, pap=pap: e.tensor_tensor(
                    delta[gi][:, c0:c0 + cw], delta[gi][:, c0:c0 + cw],
                    pap[0:128, 0:cw], ALU.add)),
                     waits=[('te', te_cnt)])
            p.bank_release(bidx, 've', p.cnt['ve'])
        emit_linear(ctx, es, f'l1f_wo{l}', aps[f'w_l1f_wo{l}'], 512, D,
                    x_bf[0:4], evac_wo, wait_rhs=[('ve', p.cnt['ve'])])
        p.barrier()
        les1.close()

        def shift_builder(f, rt, waits, l=l):
            sc = delta[f]
            if f < 4:
                p.op('ve', (lambda e, f=f, rt=rt: e.tensor_tensor(
                    rt[:], x[f][:], delta[f][:], ALU.add)), waits=waits)
            else:
                p.op('ve', (lambda e, f=f, rt=rt: e.tensor_tensor(
                    rt[:, 1:T], x[f][:, 0:T - 1], delta[f][:, 0:T - 1], ALU.add)),
                     waits=waits)
                p.op('ve', (lambda e, rt=rt: e.memset(rt[:, 0:1], 0.0)),
                     waits=[('ve', p.cnt['ve'])])
            return p.cnt['ve']
        emit_postln(ctx, es, f'l1f_ln{l}', x, None, eps=1e-5, dst_bf=x_bf,
                    gamma=[ctx.col(f'l1f_lng{l}', f) for f in range(NF)],
                    beta=[ctx.col(f'l1f_lnb{l}', f) for f in range(NF)],
                    wait_src=[('ve', p.cnt['ve']), ('se', p.cnt['se'])],
                    src_builder=shift_builder)

        with ExitStack() as fes:
            agA = sb(nc, fes, f'l1f_agA{l}', [128, 22 * 512], BF16)
            gg = sb(nc, fes, f'l1f_gg{l}', [128, 22 * 512], BF16)
            hid_gate = {id(agA): None, id(gg): None}
            for tq in range(NT):
                tc0 = tq * 512
                for part, out_t, act, bcol in ((0, agA, AF.Identity, f'l1f_b1a{l}'),
                                               (1, gg, AF.Gelu, f'l1f_b1g{l}')):
                    for m in range(22):
                        bi, tile, cast = sq_w1.get(part * 2816 + m * 128, 128)
                        b = 6 + m % 2
                        w0 = [('po', cast)] + p.bank_waits(b)
                        if m == 0 and hid_gate[id(out_t)]:
                            w0.append(hid_gate[id(out_t)])
                        for k in range(8):
                            p.op('te', (lambda e, k=k, b=b, tile=tile, tc0=tc0:
                                        e.matmul(p.bank(b)[0:128, 0:512], tile(k),
                                                 x_bf[k][:, tc0:tc0 + 512],
                                                 start=(k == 0), stop=(k == 7))),
                                 waits=w0 if k == 0 else ())
                        sq_w1.done(bi, p.cnt['te'])
                        p.op('se', (lambda e, m=m, b=b, out_t=out_t, act=act, bcol=bcol:
                                    e.activation(out_t[:, m * 512:(m + 1) * 512],
                                                 p.bank(b)[0:128, 0:512], act,
                                                 bias=ctx.col(bcol, m))),
                             waits=[('te', p.cnt['te'])])
                        p.bank_release(b, 'se', p.cnt['se'])
                for m in range(22):
                    p.op('ve', (lambda e, m=m: e.tensor_tensor(
                        agA[:, m * 512:(m + 1) * 512], agA[:, m * 512:(m + 1) * 512],
                        gg[:, m * 512:(m + 1) * 512], ALU.mult)),
                         waits=[('se', p.cnt['se'])] if m == 0 else ())
                ve_h2 = p.cnt['ve']
                for fi in range(NF):
                    bi1, tile1, cast1 = sq_w2a.get(fi * 128, 128)
                    b = fi % 6
                    w0 = [('po', cast1), ('ve', ve_h2)] + p.bank_waits(b)
                    for k in range(11):
                        p.op('te', (lambda e, k=k, b=b, tile=tile1:
                                    e.matmul(p.bank(b)[0:128, 0:512], tile(k),
                                             agA[:, k * 512:(k + 1) * 512],
                                             start=(k == 0), stop=False)),
                             waits=w0 if k == 0 else ())
                    sq_w2a.done(bi1, p.cnt['te'])
                    bi2, tile2, cast2 = sq_w2b.get(fi * 128, 128)
                    for k in range(11):
                        p.op('te', (lambda e, k=k, b=b, tile=tile2:
                                    e.matmul(p.bank(b)[0:128, 0:512], tile(k),
                                             agA[:, (11 + k) * 512:(12 + k) * 512],
                                             start=False, stop=(k == 10))),
                             waits=[('po', cast2)] if k == 0 else ())
                    sq_w2b.done(bi2, p.cnt['te'])
                    p.op('ve', (lambda e, fi=fi, b=b, tc0=tc0, l=l:
                                e.scalar_tensor_tensor(
                                    delta[fi][:, tc0:tc0 + 512], p.bank(b)[0:128, 0:512],
                                    ctx.col(f'l1f_b2{l}', fi), delta[fi][:, tc0:tc0 + 512],
                                    ALU.add, ALU.add)),
                         waits=[('te', p.cnt['te'])])
                    p.bank_release(b, 've', p.cnt['ve'])
                hid_gate[id(agA)] = ('te', p.cnt['te'])
                hid_gate[id(gg)] = ('ve', ve_h2)
                p.flush()
        p.barrier()
    emit_postln(ctx, es, 'l1f_lnout', x, x, eps=1e-5, dst_bf=x_bf,
                wait_src=[('ve', p.cnt['ve'])],
                src2=[delta[i] for i in range(NF)], src_scale=2.0)


def emit_l0_attn(ctx, es, x, x_bf):
    nc, p = ctx.nc, ctx.p
    tt = ctx.t
    aps = ctx.aps
    TWO_PI = 6.283185307179586

    SCAL = sb(nc, es, 'l0a_scal', [128, T], F32)
    B_bf = sb(nc, es, 'l0a_Bbf', [64, T], BF16)
    C_bf = sb(nc, es, 'l0a_Cbf', [64, T], BF16)
    theta = sb(nc, es, 'l0a_theta', [32, T], F32)

    def evac_theta(gi, m0, mw, ni, c0, cw, bidx, pap, te_cnt):
        p.op('ve', (lambda e, c0=c0, cw=cw, pap=pap: e.tensor_copy(
            theta[:, c0:c0 + cw], pap[0:32, 0:cw])), waits=[('te', te_cnt)])
        p.bank_release(bidx, 've', p.cnt['ve'])
    emit_linear(ctx, es, 'l0a_th', aps['w_l0a_theta'], D, 32, x_bf, evac_theta,
                m_groups=[(0, 32)], wait_rhs=[('po', p.cnt['po'])])

    emit_postln(ctx, es, 'l0a_ln', x, None, eps=1e-6, dst_bf=x_bf,
                gamma=[ctx.col('l0a_lng', f) for f in range(NF)],
                beta=[ctx.col('l0a_lnb', f) for f in range(NF)],
                wait_src=[('ve', p.cnt['ve']), ('te', p.cnt['te'])])

    sq_proj = Slabs(ctx, aps['w_l0a_proj'], D)
    sq_out = Slabs(ctx, aps['w_l0a_out'], 2048)
    SLOTS = [(4, 0), (4, 32), (4, 64), (5, 0), (5, 32), (5, 64), (6, 0), (6, 32)]
    with ExitStack() as tes:
        BC_raw = sb(nc, tes, 'l0a_bcraw', [128, T], F32)
        TSCAL = sb(nc, tes, 'l0a_tscal', [16, T], F32)
        LAM16 = sb(nc, tes, 'l0a_lam16', [16, T], F32)
        ONES32 = sb(nc, tes, 'l0a_ones32', [32, T], F32)
        phib = sb(nc, tes, 'l0a_phi', [32, T], F32)
        sinb = sb(nc, tes, 'l0a_sin', [32, T], F32)
        ni32 = sb(nc, tes, 'l0a_ni32', [32, T], I32)
        rr32 = ONES32
        cs2 = sb(nc, tes, 'l0a_cs2', [128, T], BF16)
        ss2 = sb(nc, tes, 'l0a_ss2', [128, T], BF16)
        colscr = sb(nc, tes, 'l0a_colscr', [128, 8], F32)
        rowscr = phib[0:1, 0:T]
        ni32f = ni32[:].bitcast(F32)
        bcsc = sb(nc, tes, 'l0a_bcsc', [128, 512], F32)
        bcnbf = sb(nc, tes, 'l0a_bcnbf', [128, 512], BF16)
        perm32 = sb(nc, tes, 'l0a_perm32', [128, 128], BF16)
        p.dma('po', perm32[:], aps['c_perm32_bf'][:, :])
        perm_gq = p.cnt['gq']
        p.op('ve', lambda e: e.memset(ONES32[:], 1.0))

        def evac_bc(gi, m0, mw, ni, c0, cw, bidx, pap, te_cnt):
            p.op('ve', (lambda e, c0=c0, cw=cw, pap=pap: e.tensor_copy(
                BC_raw[:, c0:c0 + cw], pap[0:128, 0:cw])), waits=[('te', te_cnt)])
            p.bank_release(bidx, 've', p.cnt['ve'])
        emit_linear(ctx, tes, 'l0a_bc', aps['w_l0a_proj'], D, 128, x_bf, evac_bc,
                    m_groups=[(4096, 128)], wait_rhs=[('se', p.cnt['se'])])

        def evac_dt(gi, m0, mw, ni, c0, cw, bidx, pap, te_cnt):
            p.op('ve', (lambda e, c0=c0, cw=cw, pap=pap: e.tensor_copy(
                TSCAL[0:16, c0:c0 + cw], pap[0:16, 0:cw])), waits=[('te', te_cnt)])
            p.bank_release(bidx, 've', p.cnt['ve'])
        emit_linear(ctx, tes, 'l0a_dtm', aps['w_l0a_proj'], D, 16, x_bf, evac_dt,
                    m_groups=[(4224, 16)])

        def evac_lam(gi, m0, mw, ni, c0, cw, bidx, pap, te_cnt):
            p.op('se', (lambda e, c0=c0, cw=cw, pap=pap: e.activation(
                LAM16[0:16, c0:c0 + cw], pap[0:16, 0:cw], AF.Sigmoid)),
                 waits=[('te', te_cnt)])
            p.bank_release(bidx, 'se', p.cnt['se'])
        emit_linear(ctx, tes, 'l0a_lamm', aps['w_l0a_proj'], D, 16, x_bf, evac_lam,
                    m_groups=[(4240, 16)])

        dt = TSCAL[0:16, 0:T]
        lam = LAM16[0:16, 0:T]
        scr16 = ni32[0:16, 0:T].bitcast(F32)

        def vst(fn, w=()):
            p.op('ve', fn, waits=[('ve', p.cnt['ve'])] + list(w))
        vst(lambda e: e.tensor_scalar(dt, dt, ctx.col('l0a_dtb')[0:16, 0:1], None, ALU.add),
            w=[('se', p.cnt['se'])])
        vst(lambda e: e.tensor_single_scalar(scr16, dt, -1.0, ALU.mult))
        vst(lambda e: e.tensor_tensor(scr16, dt, scr16, ALU.min))
        p.op('se', (lambda e: e.activation(scr16, scr16, AF.Exp)),
             waits=[('ve', p.cnt['ve'])])
        p.op('se', (lambda e: e.activation(scr16, scr16, AF.Ln, bias=1.0)),
             waits=[('se', p.cnt['se'])])
        vst(lambda e: e.tensor_single_scalar(dt, dt, 0.0, ALU.max),
            w=[('se', p.cnt['se'])])
        vst(lambda e: e.tensor_tensor(dt, dt, scr16, ALU.add))
        dt_done = p.cnt['ve']
        p.op('se', (lambda e: e.activation(colscr[:, 0:1], ctx.col('l0a_Alog'),
                                           AF.Exp)), waits=[('ve', dt_done)])
        vst(lambda e: e.tensor_single_scalar(colscr[:, 0:1], colscr[:, 0:1],
                                             -1.0, ALU.mult), w=[('se', p.cnt['se'])])
        vst(lambda e: e.tensor_scalar(scr16, dt, colscr[0:16, 0:1], None, ALU.mult))
        vst(lambda e: e.tensor_tensor_scan(SCAL[0:16, 0:T], ONES32[0:16, 0:T], scr16,
                                           0.0, ALU.mult, ALU.add))
        for ci in range(NT):
            p.op('te', (lambda e, ci=ci: e.matmul(
                p.bank(0)[0:1, 0:512], tt['ones_col_f32'][0:16, 0:1],
                dt[:, ci * 512:(ci + 1) * 512], start=True, stop=True)),
                 waits=[('ve', p.cnt['ve'])] + p.bank_waits(0))
            p.op('ve', (lambda e, ci=ci: e.tensor_single_scalar(
                rowscr[:, ci * 512:(ci + 1) * 512], p.bank(0)[0:1, 0:512],
                1.0 / 16.0, ALU.mult)), waits=[('te', p.cnt['te'])])
            p.bank_release(0, 've', p.cnt['ve'])
        for ci in range(NT):
            p.op('te', (lambda e, ci=ci: e.matmul(
                p.bank(1)[0:32, 0:512], tt['ones_row_f32'][:, 0:32],
                rowscr[0:1, ci * 512:(ci + 1) * 512], start=True, stop=True)),
                 waits=[('ve', p.cnt['ve'])] + p.bank_waits(1))
            p.op('ve', (lambda e, ci=ci: e.tensor_tensor(
                theta[:, ci * 512:(ci + 1) * 512], theta[:, ci * 512:(ci + 1) * 512],
                p.bank(1)[0:32, 0:512], ALU.mult)),
                 waits=[('te', p.cnt['te']), ('ve', p.cnt['ve'])])
            p.bank_release(1, 've', p.cnt['ve'])
        vst(lambda e: e.tensor_tensor_scan(phib[:], ONES32[:], theta[:],
                                           0.0, ALU.mult, ALU.add))
        for dst, shift in ((sinb, 0.0), (theta, 0.25)):
            vst(lambda e, dst=dst, shift=shift: e.tensor_scalar(
                dst[:], phib[:], 1.0 / TWO_PI, shift + 0.5 + 16384.0,
                ALU.mult, ALU.add))
            vst(lambda e, dst=dst: e.tensor_copy(ni32[:], dst[:]))
            vst(lambda e: e.tensor_copy(rr32[:], ni32[:]))
            vst(lambda e, dst=dst: e.tensor_tensor(dst[:], dst[:], rr32[:],
                                                   ALU.subtract))
            vst(lambda e, dst=dst: e.tensor_single_scalar(ni32[:], dst[:], 0.0,
                                                          ALU.is_lt))
            vst(lambda e: e.tensor_copy(rr32[:], ni32[:]))
            vst(lambda e, dst=dst: e.tensor_tensor(dst[:], dst[:], rr32[:], ALU.add))
            vst(lambda e, dst=dst: e.tensor_scalar(dst[:], dst[:], TWO_PI,
                                                   -3.141592653589793,
                                                   ALU.mult, ALU.add))
            p.op('se', (lambda e, dst=dst: e.activation(dst[:], dst[:], AF.Sin)),
                 waits=[('ve', p.cnt['ve'])])
        trig_se = p.cnt['se']
        p.op('ve', (lambda e: e.tensor_copy(cs2[0:32, 0:T], theta[:])),
             waits=[('se', trig_se), ('ve', p.cnt['ve'])])
        p.op('ve', (lambda e: e.tensor_copy(cs2[32:64, 0:T], cs2[0:32, 0:T])),
             waits=[('ve', p.cnt['ve'])])
        p.op('ve', (lambda e: e.tensor_copy(cs2[64:96, 0:T], cs2[0:32, 0:T])))
        p.op('ve', (lambda e: e.tensor_single_scalar(ss2[0:32, 0:T], sinb[:], -1.0,
                                                     ALU.mult)))
        p.op('ve', (lambda e: e.tensor_copy(ss2[32:64, 0:T], sinb[:])))
        p.op('ve', (lambda e: e.tensor_single_scalar(ss2[64:96, 0:T], sinb[:], -1.0,
                                                     ALU.mult)))
        vrep = p.cnt['ve']
        p.dma('po', cs2[96:128, 0:T], cs2[0:32, 0:T], waits=[('ve', vrep)])
        p.dma('po', ss2[96:128, 0:T], ss2[32:64, 0:T])
        rep_gq = p.cnt['gq']

        for ci in range(NT):
            p.op('se', (lambda e, ci=ci: e.activation(
                bcnbf[:], BC_raw[:, ci * 512:(ci + 1) * 512], AF.Square)),
                 waits=[('ve', p.cnt['ve']), ('te', p.cnt['te']),
                        ('se', p.cnt['se'])])
            sq_se = p.cnt['se']
            for h2 in range(2):
                b, r = SLOTS[h2 * 4 + ci]
                p.op('te', (lambda e, h2=h2, b=b, r=r: e.matmul(
                    p.bank(b)[r:r + 1, 0:512],
                    tt['ones_col_bf'][h2 * 64:(h2 + 1) * 64, 0:1],
                    bcnbf[h2 * 64:(h2 + 1) * 64, 0:512], start=True, stop=True)),
                     waits=[('se', sq_se)] + (p.bank_waits(b) if ci == 0 else []))
        te_norm = p.cnt['te']
        for si in range(8):
            b, r = SLOTS[si]
            p.op('se', (lambda e, b=b, r=r: e.activation(
                p.bank(b)[r:r + 1, 0:512], p.bank(b)[r:r + 1, 0:512], AF.Ln,
                scale=1.0 / 64.0, bias=1e-06)),
                 waits=([('te', te_norm)] if si == 0 else [('se', p.cnt['se'])]))
            p.op('se', (lambda e, b=b, r=r: e.activation(
                p.bank(b)[r:r + 1, 0:512], p.bank(b)[r:r + 1, 0:512], AF.Exp,
                scale=-0.5)), waits=[('se', p.cnt['se'])])
        se_rows = p.cnt['se']

        def slot_row(b, r):
            i = SLOTS.index((b, r))
            if i < 4:
                return phib[0:1, i * 512:(i + 1) * 512]
            return ni32f[0:1, (i - 4) * 512:(i - 3) * 512]
        for si in range(8):
            b, r = SLOTS[si]
            p.op('ve', (lambda e, b=b, r=r: e.tensor_copy(
                slot_row(b, r), p.bank(b)[r:r + 1, 0:512])),
                 waits=[('se', se_rows)] if si == 0 else ())
        ve_rows = p.cnt['ve']
        for b in (4, 5, 6):
            p.bank_release(b, 've', ve_rows)
        for si in range(8):
            h2, ci = si // 4, si % 4
            b, r = SLOTS[si]
            p.op('te', (lambda e, h2=h2, ci=ci, b=b, r=r: e.matmul(
                p.bank(4 + ci)[h2 * 64:h2 * 64 + 64, 0:512],
                tt['ones128'][0:1, 0:64], slot_row(b, r), start=True, stop=True)),
                 waits=([('ve', ve_rows)] + p.bank_waits(4) + p.bank_waits(5)
                        + p.bank_waits(6) + p.bank_waits(7)) if si == 0 else ())
        te_b = p.cnt['te']
        for ci in range(NT):
            p.op('ve', (lambda e, ci=ci: e.tensor_tensor(
                bcsc[:], BC_raw[:, ci * 512:(ci + 1) * 512], p.bank(4 + ci)[:, 0:512],
                ALU.mult)),
                 waits=([('te', te_b)] if ci == 0 else []) + [('se', p.cnt['se']),
                                                              ('ve', p.cnt['ve'])])
            p.bank_release(4 + ci, 've', p.cnt['ve'])
            p.op('se', (lambda e, ci=ci: e.activation(
                BC_raw[:, ci * 512:(ci + 1) * 512], bcsc[:], AF.Identity,
                bias=ctx.col('l0a_bcb'), scale=ctx.col('l0a_bcs'))),
                 waits=[('ve', p.cnt['ve'])])
        bcn_se = p.cnt['se']

        for ci in range(NT):
            c0 = ci * 512
            p.op('ve', (lambda e, c0=c0: e.tensor_copy(
                bcnbf[:], BC_raw[:, c0:c0 + 512])),
                 waits=[('se', bcn_se), ('ve', p.cnt['ve'])])
            cast_ve = p.cnt['ve']
            p.op('te', (lambda e: e.matmul(p.bank(1)[:, 0:512], perm32[:],
                                           bcnbf[:], start=True, stop=True)),
                 waits=[('ve', cast_ve), ('gq', max(rep_gq, perm_gq))]
                 + p.bank_waits(1))
            te_pm = p.cnt['te']
            for half, out_t in ((0, B_bf), (1, C_bf)):
                hb = half * 64
                p.op('ve', (lambda e, c0=c0, hb=hb, out_t=out_t: e.tensor_tensor(
                    out_t[0:64, c0:c0 + 512], BC_raw[hb:hb + 64, c0:c0 + 512],
                    cs2[hb:hb + 64, c0:c0 + 512], ALU.mult)),
                     waits=[('ve', p.cnt['ve'])])
                p.op('ve', (lambda e, c0=c0, hb=hb: e.tensor_tensor(
                    bcsc[0:64, 0:512], p.bank(1)[hb:hb + 64, 0:512],
                    ss2[hb:hb + 64, c0:c0 + 512], ALU.mult)),
                     waits=[('te', te_pm), ('ve', p.cnt['ve'])])
                p.op('ve', (lambda e, c0=c0, out_t=out_t: e.tensor_tensor(
                    out_t[0:64, c0:c0 + 512], out_t[0:64, c0:c0 + 512],
                    bcsc[0:64, 0:512], ALU.add)), waits=[('ve', p.cnt['ve'])])
            p.bank_release(1, 've', p.cnt['ve'])

        vst(lambda e: e.tensor_tensor(scr16, dt, lam, ALU.mult),
            w=[('se', p.cnt['se']), ('te', p.cnt['te'])])
        vst(lambda e: e.tensor_tensor(scr16, dt, scr16, ALU.subtract))
        vst(lambda e: e.tensor_tensor(dt, dt, lam, ALU.mult))
        dtlam = dt
        vst(lambda e: e.tensor_copy(lam, dtlam))
        woff = lam
        vst(lambda e: e.tensor_tensor(LAM16[0:16, 0:T - 1],
                                      LAM16[0:16, 0:T - 1],
                                      scr16[:, 1:T], ALU.add))
        vst(lambda e: e.tensor_tensor(scr16, dtlam, woff, ALU.subtract))
        p.op('se', (lambda e: e.activation(LAM16[0:16, 0:T], LAM16[0:16, 0:T],
                                           AF.Ln)), waits=[('ve', p.cnt['ve'])])
        p.op('ve', (lambda e: e.tensor_tensor(sinb[0:16, 0:T], LAM16[0:16, 0:T],
                                              SCAL[0:16, 0:T], ALU.subtract)),
             waits=[('se', p.cnt['se']), ('ve', p.cnt['ve'])])
        p.op('ve', (lambda e: e.tensor_copy(SCAL[32:48, 0:T], sinb[0:16, 0:T])),
             waits=[('ve', p.cnt['ve'])])
        for ci in range(NT):
            c0 = ci * 512
            p.op('ve', (lambda e, c0=c0: e.tensor_tensor(
                bcnbf[0:64, 0:512], B_bf[0:64, c0:c0 + 512], C_bf[0:64, c0:c0 + 512],
                ALU.mult)), waits=[('ve', p.cnt['ve'])])
            p.op('te', (lambda e: e.matmul(p.bank(2)[0:1, 0:512],
                                           tt['ones_col_bf'][0:64, 0:1],
                                           bcnbf[0:64, 0:512], start=True, stop=True)),
                 waits=[('ve', p.cnt['ve'])] + (p.bank_waits(2) if ci == 0 else []))
            p.op('ve', (lambda e, c0=c0: e.tensor_copy(
                rowscr[0:1, c0:c0 + 512], p.bank(2)[0:1, 0:512])),
                 waits=[('te', p.cnt['te'])])
            p.bank_release(2, 've', p.cnt['ve'])
        for ci in range(NT):
            c0 = ci * 512
            p.op('te', (lambda e, c0=c0: e.matmul(
                p.bank(3)[0:16, 0:512], tt['ones_row_f32'][:, 0:16],
                rowscr[0:1, c0:c0 + 512], start=True, stop=True)),
                 waits=[('ve', p.cnt['ve'])] + (p.bank_waits(3) if ci == 0 else []))
            p.op('ve', (lambda e, c0=c0: e.tensor_tensor(
                bcsc[0:16, 0:512], scr16[:, c0:c0 + 512],
                p.bank(3)[0:16, 0:512], ALU.mult)),
                 waits=[('te', p.cnt['te']), ('ve', p.cnt['ve'])])
            p.op('ve', (lambda e: e.tensor_scalar(
                bcsc[0:16, 0:512], bcsc[0:16, 0:512],
                ctx.col('l0a_D')[0:16, 0:1], None, ALU.add)),
                 waits=[('ve', p.cnt['ve'])])
            p.op('ve', (lambda e, c0=c0: e.tensor_copy(
                SCAL[64:80, c0:c0 + 512], bcsc[0:16, 0:512])),
                 waits=[('ve', p.cnt['ve'])])
            p.bank_release(3, 've', p.cnt['ve'])
        p.dump('l0a_cumA', SCAL[0:16, 0:T])
        p.dump('l0a_LWMR', SCAL[32:48, 0:T])
        p.dump('l0a_G2', SCAL[64:80, 0:T])
        p.dump('l0a_B', B_bf[:])
        p.dump('l0a_C', C_bf[:])
        p.dump('l0a_dtlam', TSCAL[0:16, 0:T])
        p.barrier()
    with ExitStack() as ses:
        xs_q = sb(nc, ses, 'l0a_xsq', [128, 16 * 512], BF16)
        yq = sb(nc, ses, 'l0a_yq', [128, 16 * 512], BF16)
        Hst = sb(nc, ses, 'l0a_H', [64, 16 * 128], F32)
        BTq = sb(nc, ses, 'l0a_BTq', [128, 256], BF16)
        SCTq = sb(nc, ses, 'l0a_SCTq', [128, 512], F32)
        W1scr = sb(nc, ses, 'l0a_W1', [128, 128], F32)
        Pbf = sb(nc, ses, 'l0a_Pbf', [128, 128], BF16)
        xTbf = sb(nc, ses, 'l0a_xT', [128, 128], BF16)
        btwbf = sb(nc, ses, 'l0a_btw', [128, 64], BF16)
        hbf = sb(nc, ses, 'l0a_hbf', [64, 128], BF16)
        ctil = sb(nc, ses, 'l0a_ctil', [64, 128], BF16)
        combf = sb(nc, ses, 'l0a_comb', [128, 512], F32)
        yacc = sb(nc, ses, 'l0a_yacc', [128, 512], F32)
        zsil = sb(nc, ses, 'l0a_zsil', [128, 512], BF16)
        maskWb = [sb(nc, ses, f'l0a_mw{i}', [128, 128], F32) for i in range(2)]
        colscr2 = sb(nc, ses, 'l0a_colscr2', [128, 8], F32)
        g2row = sb(nc, ses, 'l0a_g2row', [1, 512], F32)
        rows4 = sb(nc, ses, 'l0a_rows4', [1, 1024], F32)
        aend_b = sb(nc, ses, 'l0a_aendb', [64, 256], F32)
        pre_b = sb(nc, ses, 'l0a_preb', [128, 256], F32)
        erow = sb(nc, ses, 'l0a_erow', [1, 128], F32)
        bcmask = sb(nc, ses, 'l0a_bcmask', [128, 128], F32)
        maskW0t = sb(nc, ses, 'l0a_maskW0', [128, 128], F32)
        p.dma('po', bcmask[:], aps['c_bcmask_lhsT'][:, :])
        p.dma('po', maskW0t[:], aps['c_maskW0'][:, :])
        mload_gq = p.cnt['gq']

        p.op('po', lambda e: e.memset(Hst[:], 0.0))
        for i in range(2):
            p.op('ve', (lambda e, i=i: e.tensor_copy(maskWb[i][:], maskW0t[:])),
                 waits=[('ve', p.cnt['ve']), ('gq', mload_gq)])
        h_memset = p.cnt['po']
        p.dma('po', rows4[0:1, 0:256], SCAL[0:16, 127::128],
              waits=[('ve', p.cnt['ve'])])
        rr_gq = p.cnt['gq']
        p.wait_only('ve', [('gq', rr_gq)])
        vst2 = lambda fn, w=(): p.op('ve', fn, waits=[('ve', p.cnt['ve'])] + list(w))
        vst2(lambda e: e.tensor_tensor(rows4[0:1, 257:512], rows4[0:1, 1:256],
                                       rows4[0:1, 0:255], ALU.subtract),
             w=[('gq', rr_gq)])
        vst2(lambda e: e.memset(rows4[0:1, 256:512:16], 0.0))
        p.op('se', (lambda e: e.activation(rows4[0:1, 256:512], rows4[0:1, 256:512],
                                           AF.Exp)), waits=[('ve', p.cnt['ve'])])
        vst2(lambda e: e.tensor_single_scalar(rows4[0:1, 512:768], rows4[0:1, 0:256],
                                              -1.0, ALU.mult),
             w=[('se', p.cnt['se'])])
        negrr = rows4[0:1, 512:768]
        p.op('te', (lambda e: e.matmul(p.bank(5)[0:64, 0:256],
                                       tt['ones_row_f32'][:, 0:64],
                                       rows4[0:1, 256:512], start=True, stop=True)),
             waits=[('ve', p.cnt['ve']), ('se', p.cnt['se'])] + p.bank_waits(5))
        p.op('ve', (lambda e: e.tensor_copy(aend_b[:], p.bank(5)[0:64, 0:256])),
             waits=[('te', p.cnt['te'])])
        p.bank_release(5, 've', p.cnt['ve'])
        p.op('te', (lambda e: e.matmul(p.bank(5)[0:128, 0:256],
                                       tt['ones_row_f32'][:],
                                       rows4[0:1, 0:256], start=True, stop=True)),
             waits=[('ve', p.cnt['ve'])] + p.bank_waits(5))
        p.op('ve', (lambda e: e.tensor_copy(pre_b[:], p.bank(5)[0:128, 0:256])),
             waits=[('te', p.cnt['te'])])
        p.bank_release(5, 've', p.cnt['ve'])

        mwi = [0]
        for tq in range(NT):
            tc0 = tq * 512
            for m in range(16):
                bi, tile, cast = sq_proj.get(m * 128, 128)
                b = 6 + m % 2
                w0 = [('po', cast)] + p.bank_waits(b)
                for k in range(8):
                    p.op('te', (lambda e, k=k, b=b, tile=tile, tc0=tc0:
                                e.matmul(p.bank(b)[0:128, 0:512], tile(k),
                                         x_bf[k][:, tc0:tc0 + 512],
                                         start=(k == 0), stop=(k == 7))),
                         waits=w0 if k == 0 else ())
                sq_proj.done(bi, p.cnt['te'])
                p.op('se', (lambda e, m=m, b=b: e.activation(
                    xs_q[:, m * 512:(m + 1) * 512], p.bank(b)[0:128, 0:512],
                    AF.Silu)), waits=[('te', p.cnt['te'])])
                p.bank_release(b, 'se', p.cnt['se'])
            xs_se = p.cnt['se']
            for crel in range(4):
                c0 = tc0 + crel * 128
                p.op('te', (lambda e, c0=c0, crel=crel: e.transpose(
                    p.bank_bf(4)[:, crel * 64:crel * 64 + 64],
                    B_bf[0:64, c0:c0 + 128], tt['ident_bf'][0:64, 0:64])),
                     waits=([('ve', p.cnt['ve'])] + p.bank_waits(4))
                     if crel == 0 else ())
            p.op('ve', (lambda e: e.tensor_copy(BTq[:], p.bank_bf(4)[:, 0:256])),
                 waits=[('te', p.cnt['te'])])
            p.bank_release(4, 've', p.cnt['ve'])
            for crel in range(4):
                c0 = tc0 + crel * 128
                p.op('te', (lambda e, c0=c0: e.transpose(
                    p.bank(5)[:, 0:128], SCAL[:, c0:c0 + 128], tt['ident_f32'][:])),
                     waits=[('ve', p.cnt['ve']), ('se', p.cnt['se'])] + p.bank_waits(5))
                p.op('ve', (lambda e, crel=crel: e.tensor_copy(
                    SCTq[:, crel * 128:(crel + 1) * 128], p.bank(5)[:, 0:128])),
                     waits=[('te', p.cnt['te'])])
                p.bank_release(5, 've', p.cnt['ve'])
            sct_ve = p.cnt['ve']
            for h in range(16):
                hc_base = h * 16 + tq * 4
                for crel in range(4):
                    c = tq * 4 + crel
                    cg0 = c * 128
                    hc = h * 16 + c
                    mi = mwi[0] % 2
                    mwi[0] += 1
                    p.dma('po', maskWb[mi][0:1, 0:128], SCAL[h:h + 1, cg0:cg0 + 128],
                          waits=[('te', p.cnt['te'])])
                    mw_gq = p.cnt['gq']
                    p.op('te', (lambda e, mi=mi: e.matmul(
                        p.bank(1)[0:128, 0:128], bcmask[:], maskWb[mi][:],
                        start=True, stop=True)),
                         waits=[('gq', mw_gq)] + p.bank_waits(1))
                    te_w = p.cnt['te']
                    p.op('te', (lambda e, cg0=cg0: e.matmul(
                        p.bank(2)[0:128, 0:128], B_bf[0:64, cg0:cg0 + 128],
                        C_bf[0:64, cg0:cg0 + 128], start=True, stop=True)),
                         waits=p.bank_waits(2))
                    te_cbt = p.cnt['te']
                    p.op('se', (lambda e, crel=crel, h=h: e.activation(
                        W1scr[:], p.bank(1)[0:128, 0:128], AF.Exp,
                        bias=SCTq[:, crel * 128 + 32 + h:crel * 128 + 33 + h])),
                         waits=[('te', te_w), ('ve', p.cnt['ve'])])
                    se_w1 = p.cnt['se']
                    if c > 0:
                        p.op('se', (lambda e, hc=hc: e.activation(
                            erow[:], p.bank(1)[0:1, 0:128], AF.Exp,
                            bias=negrr[0:1, hc - 1:hc])),
                             waits=[('se', p.cnt['se']), ('te', p.cnt['te'])])
                    se_er = p.cnt['se']
                    p.bank_release(1, 'se', se_er)
                    p.op('ve', (lambda e: e.tensor_tensor(Pbf[:], W1scr[:],
                                                          p.bank(2)[0:128, 0:128],
                                                          ALU.mult)),
                         waits=[('se', se_w1), ('te', te_cbt), ('ve', p.cnt['ve'])])
                    p.bank_release(2, 've', p.cnt['ve'])
                    p.op('te', (lambda e, h=h, crel=crel: e.transpose(
                        p.bank_bf(4)[:, 0:128],
                        xs_q[:, h * 512 + crel * 128:h * 512 + (crel + 1) * 128],
                        tt['ident_bf'][:])),
                         waits=[('se', xs_se)] + p.bank_waits(4))
                    p.op('ve', (lambda e: e.tensor_copy(xTbf[:], p.bank_bf(4)[:, 0:128])),
                         waits=[('te', p.cnt['te']), ('ve', p.cnt['ve'])])
                    xt_ve = p.cnt['ve']
                    p.bank_release(4, 've', xt_ve)
                    if c > 0:
                        p.op('ve', (lambda e, h=h: e.tensor_copy(
                            hbf[:], Hst[:, h * 128:(h + 1) * 128])),
                             waits=[('ve', p.cnt['ve']), ('te', p.cnt['te'])])
                        p.op('te', (lambda e: e.matmul(
                            p.bank(3)[0:64, 0:128], tt['ones_row_f32'][:, 0:64],
                            erow[0:1, 0:128], start=True, stop=True)),
                             waits=[('se', se_er)] + p.bank_waits(3))
                        p.op('ve', (lambda e, cg0=cg0: e.tensor_tensor(
                            ctil[:], C_bf[0:64, cg0:cg0 + 128], p.bank(3)[0:64, 0:128],
                            ALU.mult)), waits=[('te', p.cnt['te']), ('ve', p.cnt['ve'])])
                        p.bank_release(3, 've', p.cnt['ve'])
                    p.op('te', (lambda e: e.matmul(
                        p.bank(0)[0:128, 0:128], xTbf[:], Pbf[:],
                        start=True, stop=True)),
                         waits=[('ve', p.cnt['ve'])] + p.bank_waits(0))
                    if c > 0:
                        p.op('te', (lambda e: e.matmul(
                            p.bank(0)[0:128, 0:128], hbf[:], ctil[:],
                            start=False, stop=True, skip_group_check=True)))
                    te_y = p.cnt['te']
                    p.op('ve', (lambda e, crel=crel: e.tensor_copy(
                        yacc[:, crel * 128:(crel + 1) * 128], p.bank(0)[0:128, 0:128])),
                         waits=[('te', te_y)])
                    p.bank_release(0, 've', p.cnt['ve'])
                    p.op('se', (lambda e, crel=crel, h=h, hc=hc: e.activation(
                        colscr2[:, 0:1], SCTq[:, crel * 128 + 32 + h:crel * 128 + 33 + h],
                        AF.Exp, bias=pre_b[:, hc:hc + 1])),
                         waits=[('ve', p.cnt['ve']), ('se', p.cnt['se'])])
                    p.op('ve', (lambda e, crel=crel: e.tensor_scalar(
                        btwbf[:], BTq[:, crel * 64:(crel + 1) * 64],
                        colscr2[:, 0:1], None, ALU.mult)),
                         waits=[('se', p.cnt['se']), ('ve', p.cnt['ve']),
                                ('te', p.cnt['te'])])
                    p.op('te', (lambda e: e.matmul(
                        p.bank(3)[0:64, 0:128], btwbf[:], xTbf[:],
                        start=True, stop=True)),
                         waits=[('ve', p.cnt['ve'])] + p.bank_waits(3))
                    p.op('ve', (lambda e, h=h, hc=hc: e.scalar_tensor_tensor(
                        Hst[:, h * 128:(h + 1) * 128], Hst[:, h * 128:(h + 1) * 128],
                        aend_b[:, hc:hc + 1], p.bank(3)[0:64, 0:128],
                        ALU.mult, ALU.add)),
                         waits=[('te', p.cnt['te']), ('ve', p.cnt['ve']),
                                ('po', h_memset)])
                    p.bank_release(3, 've', p.cnt['ve'])
                bi, tile, cast = sq_proj.get(2048 + h * 128, 128)
                bz = 6 + h % 2
                w0 = [('po', cast)] + p.bank_waits(bz)
                for k in range(8):
                    p.op('te', (lambda e, k=k, bz=bz, tile=tile, tc0=tc0:
                                e.matmul(p.bank(bz)[0:128, 0:512], tile(k),
                                         x_bf[k][:, tc0:tc0 + 512],
                                         start=(k == 0), stop=(k == 7))),
                         waits=w0 if k == 0 else ())
                sq_proj.done(bi, p.cnt['te'])
                te_z = p.cnt['te']
                p.dma('po', g2row[0:1, 0:512],
                      SCAL[64 + h:65 + h, tc0:tc0 + 512],
                      waits=[('te', p.cnt['te'])])
                g2_gq = p.cnt['gq']
                p.op('te', (lambda e: e.matmul(
                    p.bank(5)[0:128, 0:512], tt['ones_row_f32'][:],
                    g2row[0:1, 0:512], start=True, stop=True)),
                     waits=[('gq', g2_gq)] + p.bank_waits(5))
                te_g2 = p.cnt['te']
                p.op('ve', (lambda e, h=h: e.tensor_tensor(
                    combf[:], xs_q[:, h * 512:(h + 1) * 512], p.bank(5)[:, 0:512],
                    ALU.mult)), waits=[('te', te_g2), ('ve', p.cnt['ve'])])
                p.bank_release(5, 've', p.cnt['ve'])
                p.op('ve', (lambda e: e.tensor_tensor(
                    combf[:], combf[:], yacc[:], ALU.add)),
                     waits=[('ve', p.cnt['ve'])])
                p.op('se', (lambda e, bz=bz: e.activation(
                    zsil[:], p.bank(bz)[0:128, 0:512], AF.Silu)),
                     waits=[('te', te_z), ('ve', p.cnt['ve'])])
                p.bank_release(bz, 'se', p.cnt['se'])
                p.op('ve', (lambda e, h=h: e.tensor_tensor(
                    yq[:, h * 512:(h + 1) * 512], combf[:], zsil[:], ALU.mult)),
                     waits=[('se', p.cnt['se']), ('ve', p.cnt['ve'])])
            yq_ve = p.cnt['ve']
            for fi in range(NF):
                bi, tile, cast = sq_out.get(fi * 128, 128)
                b = fi % 6
                w0 = [('po', cast), ('ve', yq_ve)] + p.bank_waits(b)
                for k in range(16):
                    p.op('te', (lambda e, k=k, b=b, tile=tile:
                                e.matmul(p.bank(b)[0:128, 0:512], tile(k),
                                         yq[:, k * 512:(k + 1) * 512],
                                         start=(k == 0), stop=(k == 15))),
                         waits=w0 if k == 0 else ())
                sq_out.done(bi, p.cnt['te'])
                p.op('ve', (lambda e, fi=fi, b=b, tc0=tc0: e.scalar_tensor_tensor(
                    x[fi][:, tc0:tc0 + 512], x[fi][:, tc0:tc0 + 512],
                    tt['fconsts'][:, 8:9], p.bank(b)[0:128, 0:512],
                    ALU.mult, ALU.add)),
                     waits=[('te', p.cnt['te']), ('ve', p.cnt['ve'])])
                p.bank_release(b, 've', p.cnt['ve'])
            if tq == 0:
                p.dump('l0a_yq0', yq[:])
                p.dump('l0a_H0', Hst[:])
                p.dump('l0a_x2', x[2][:])
            p.flush()
        p.barrier()
    emit_postln(ctx, es, 'l0a_lnout', x, x, eps=1e-5, dst_bf=x_bf,
                wait_src=[('ve', p.cnt['ve'])])


def build_block_program(block_names, const_arrays):
    nc = bass.Bass("TRN2", target_bir_lowering=False)
    aps = {}
    import ml_dtypes
    for name, arr in const_arrays.items():
        dt = BF16 if arr.dtype == ml_dtypes.bfloat16 else F32
        aps[name] = nc.dram_tensor(name, list(arr.shape), dt, kind="ExternalInput").ap()
    x_in = nc.dram_tensor("x_in", [D, T], F32, kind="ExternalInput").ap()
    x_out = nc.dram_tensor("x_out", [D, T], F32, kind="ExternalOutput").ap()
    es = ExitStack()
    with es:
        p = Prog(nc, es)
        ctx = Ctx(nc, p, aps)
        emit_load_consts(ctx, es)
        x = [sb(nc, es, f'x{i}', [128, T], F32) for i in range(NF)]
        x_bf = [sb(nc, es, f'xbf{i}', [128, T], BF16) for i in range(NF)]
        for f in range(NF):
            p.dma('po', x[f][:], x_in[f * 128:(f + 1) * 128, :])
        ld = p.cnt['gq']
        for f in range(NF):
            p.op('po', (lambda e, f=f: e.tensor_copy(x_bf[f][:], x[f][:])),
                 waits=[('gq', ld)])
        for e in ('te', 've', 'se'):
            p.wait_only(e, [('gq', ld)])
        for bn in block_names:
            with ExitStack() as bes:
                if bn == 'l1_attn':
                    emit_l1_attn(ctx, bes, x, x_bf)
                elif bn == 'l0_attn':
                    emit_l0_attn(ctx, bes, x, x_bf)
                elif bn == 'l0_ffn':
                    emit_l0_ffn(ctx, bes, x, x_bf)
                elif bn == 'l1_ffn':
                    emit_l1_ffn(ctx, bes, x, x_bf)
                else:
                    raise ValueError(bn)
                p.barrier(prods=('te', 've', 'se', 'po', 'sq', 'gq', 'cq',
                                 'w0', 'w1'))
                p.flush()
        for f in range(NF):
            p.dma('po', x_out[f * 128:(f + 1) * 128, :], x[f][:],
                  waits=[('ve', p.cnt['ve']), ('se', p.cnt['se']), ('po', p.cnt['po'])])
        p.wait_only('po', [('gq', p.cnt['gq'])])
        p.flush()
    return nc


BLOCK_SEQ = ['l0_attn', 'l0_ffn', 'l1_attn', 'l1_ffn']
FUSED = True


def kernel(**inputs):
    consts = prep_host(inputs)
    xs = [np.ascontiguousarray(inputs['L0_attn_u'][i].T) for i in range(8)]
    groups = [BLOCK_SEQ] if FUSED else [[bn] for bn in BLOCK_SEQ]
    for grp in groups:
        nc = build_block_program(grp, consts)
        in_maps = []
        for i in range(8):
            m = dict(consts)
            m['x_in'] = xs[i]
            in_maps.append(m)
        res = run_bass_kernel_spmd(nc, in_maps, core_ids=list(range(8)))
        xs = [np.asarray(res.results[i]['x_out']) for i in range(8)]
    return np.ascontiguousarray(np.stack([x.T for x in xs], 0))


# version 06 (final)
# speedup vs baseline: 1.0192x; 1.0030x over previous
import numpy as np
from contextlib import ExitStack
import concourse.bass as bass
import concourse.mybir as mybir
from concourse.bass_utils import run_bass_kernel_spmd

F32 = mybir.dt.float32
BF16 = mybir.dt.bfloat16
I32 = mybir.dt.int32
U8 = mybir.dt.uint8
AF = mybir.ActivationFunctionType
ALU = mybir.AluOpType
AX = mybir.AxisListType

T = 2048
NT = 4
D = 1024
NF = D // 128

ENG_ATTR = {'te': 'tensor', 've': 'vector', 'se': 'scalar', 'po': 'gpsimd', 'sy': 'sync'}
DEBUG_DUMPS = False
FCONSTS = [1e-5, 1e-6, 1.1920929e-07, 1e-24, 1.0, 0.0, -1e30, -2.0794415416798357, 2.0]


class Prog:

    def __init__(self, nc, es):
        self.nc = nc
        self.es = es
        self.block = es.enter_context(nc.Block())
        self.sems = {}
        self.cnt = {}
        for s in ('te', 've', 'se', 'po', 'sq', 'gq', 'cq', 'w0', 'w1'):
            self.sems[s] = es.enter_context(nc.semaphore('sem_' + s))
            self.cnt[s] = 0
        self.pend = {e: [] for e in ENG_ATTR}
        self.last_wait = {e: {} for e in ENG_ATTR}
        self.psum = es.enter_context(nc.psum_tensor("psum_all", [128, 4096], F32))
        self.bank_ready = [None] * 8

    def bank(self, i):
        return self.psum.ap()[:, i * 512:(i + 1) * 512]

    def bank_bf(self, i):
        return self.psum.ap().bitcast(BF16)[:, i * 1024:(i + 1) * 1024]

    def bank_span(self, i, n):
        return self.psum.ap()[:, i * 512:(i + n) * 512]

    def op(self, eng, fn, waits=(), tick=None, tick_n=1):
        wl = []
        for w in waits:
            s, v = w if isinstance(w, tuple) else (w, self.cnt[w])
            if v > 0 and self.last_wait[eng].get(s, -1) < v:
                wl.append((s, v))
                self.last_wait[eng][s] = v
        if tick is None:
            tick = eng if eng in ('te', 've', 'se', 'po') else None
        elif tick is False:
            tick = None
        self.pend[eng].append((wl, fn, tick, tick_n))
        if tick:
            self.cnt[tick] += tick_n
            return self.cnt[tick]
        return None

    def dma(self, queue, out, in_, waits=(), sem=None):
        qsem = sem or {'sy': 'sq', 'po': 'gq', 'se': 'cq'}[queue]
        return self.op(queue, (lambda e, o=out, i=in_: e.dma_start(
            out=o, in_=i, allow_slow_non_contiguous=True)),
                       waits=waits, tick=qsem, tick_n=16)

    def wait_only(self, eng, waits):
        self.op(eng, None, waits=waits, tick=False)

    def barrier(self, prods=('te', 've', 'se', 'po'), cons=('sy', 'po', 'se', 've', 'te')):
        w = [(s, self.cnt[s]) for s in prods]
        for e in cons:
            self.op(e, None, waits=w, tick=False)

    def dump(self, name, ap, dt=None):
        if not DEBUG_DUMPS or (isinstance(DEBUG_DUMPS, (set, list)) and name not in DEBUG_DUMPS):
            return
        shape = list(ap.shape)
        dram = self.nc.dram_tensor('dbg_' + name, shape, dt or ap.dtype,
                                   kind="ExternalOutput").ap()
        self.barrier()
        sl = tuple(slice(0, s) for s in shape)
        self.dma('po', dram[sl], ap)
        gq = self.cnt['gq']
        for e in ('sy', 'po', 'se', 've', 'te'):
            self.op(e, None, waits=[('gq', gq)], tick=False)

    def flush(self):
        for eng, lst in self.pend.items():
            if not lst:
                continue
            sems = self.sems

            def body(e, lst=tuple(lst), sems=sems):
                for wl, fn, tick, tick_n in lst:
                    for (s, v) in wl:
                        e.wait_ge(sems[s], v)
                    if fn is not None:
                        ins = fn(e)
                        if tick:
                            ins.then_inc(sems[tick], tick_n)
            getattr(self.block, ENG_ATTR[eng])(body)
            self.pend[eng] = []

    def bank_waits(self, i):
        r = self.bank_ready[i]
        return [r] if r is not None else []

    def bank_release(self, i, sem, val):
        self.bank_ready[i] = (sem, val)


import os
SB_DEBUG = bool(os.environ.get('KERNEL_SB_DEBUG'))


def sb(nc, es, name, shape, dt=F32):
    h = es.enter_context(nc.sbuf_tensor(name, shape, dt))
    if SB_DEBUG:
        nbytes = shape[1] * {F32: 4, BF16: 2, U8: 1, I32: 4}.get(dt, 4)
        print(f"SB {name}: {nbytes}B/part")
    return h


def col_layout(v, pad_to=None):
    v = np.asarray(v, np.float32).reshape(-1)
    n = v.shape[0]
    k = (n + 127) // 128 if pad_to is None else pad_to
    out = np.zeros((128, k), np.float32)
    for i in range(k):
        seg = v[i * 128:(i + 1) * 128]
        out[:len(seg), i] = seg
    return out


COLS = {}


def pack_cols(c, entries):
    arrs = []
    off = 0
    COLS.clear()
    for name, a in entries:
        a = np.asarray(a, np.float32)
        assert a.shape[0] == 128
        COLS[name] = (off, a.shape[1])
        arrs.append(a)
        off += a.shape[1]
    c['c_allcols'] = np.concatenate(arrs, axis=1)


def prep_host(w):
    import ml_dtypes
    c = {}
    c['c_ident_bf'] = np.eye(128).astype(ml_dtypes.bfloat16)
    c['c_ident_f32'] = np.eye(128, dtype=np.float32)
    c['c_ones_col_bf'] = np.ones((128, 1), ml_dtypes.bfloat16)
    c['c_ones_col_f32'] = np.ones((128, 1), np.float32)
    c['c_ones_row_f32'] = np.ones((1, 128), np.float32)
    c['c_ones128'] = np.ones((128, 128), np.float32)
    ii = np.arange(128)
    c['c_mask_keep_le'] = np.where(ii[:, None] > ii[None, :], -1e30, 0.0).astype(np.float32)
    c['c_mask_causal'] = np.where(ii[None, :] > ii[:, None], -1e30, 0.0).astype(np.float32)
    c['c_mask_keep_ge'] = np.where(ii[None, :] < ii[:, None], -1e30, 0.0).astype(np.float32)
    c['c_mask_causal_bf'] = c['c_mask_causal'].astype(ml_dtypes.bfloat16)
    c['c_mask_keep_ge_bf'] = c['c_mask_keep_ge'].astype(ml_dtypes.bfloat16)
    fc = np.array(FCONSTS, np.float32)
    c['c_fconsts'] = np.tile(fc[None, :], (128, 1)).astype(np.float32)

    tp = lambda a: np.ascontiguousarray(np.asarray(a, np.float32).T)
    c['w_l0f_wqT'] = tp(w['L0_ffn_wq'])
    c['w_l0f_wkT'] = tp(w['L0_ffn_wk'])
    c['w_l0f_wvT'] = tp(w['L0_ffn_wv'])
    c['w_l0f_woT'] = tp(w['L0_ffn_wo'])
    c['w_l0f_gateT'] = tp(w['L0_ffn_gate_w'])
    c['w_l0f_e0upT'] = tp(w['L0_ffn_e0_up'])
    c['w_l0f_e0dnT'] = tp(w['L0_ffn_e0_down'])
    c['w_l0f_e1w1T'] = tp(w['L0_ffn_e1_w1'])
    c['w_l0f_e1w2T'] = tp(w['L0_ffn_e1_w2'])
    c['w_l0f_e1dnT'] = tp(w['L0_ffn_e1_down'])
    c['w_l0f_e2l1T'] = tp(w['L0_ffn_e2_l1'])
    c['w_l0f_e2l2T'] = tp(w['L0_ffn_e2_l2'])
    c['w_l0f_e2l3T'] = tp(w['L0_ffn_e2_l3'])
    c['w_l0f_e2l4T'] = tp(w['L0_ffn_e2_l4'])
    c['w_l0f_e3dnT'] = tp(w['L0_ffn_e3_down'])
    c['w_l0f_e3upT'] = tp(w['L0_ffn_e3_up'])
    c['w_l0f_e3outT'] = tp(w['L0_ffn_e3_out'])

    c['w_l1a_WaT'] = np.ascontiguousarray(w['L1_attn_Wa_w'].T)
    c['w_l1a_BT'] = np.ascontiguousarray(w['L1_attn_B_w'].T)
    c['w_l1a_CT'] = np.ascontiguousarray(w['L1_attn_C_w'].T)
    c['w_l1a_DT'] = np.ascontiguousarray(w['L1_attn_D_w'].T)
    c['w_l0a_proj'] = np.ascontiguousarray(np.asarray(w['L0_attn_in_proj_w'], np.float32))
    c['w_l0a_theta'] = np.ascontiguousarray(np.asarray(w['L0_attn_theta_w'], np.float32))
    c['w_l0a_out'] = np.ascontiguousarray(np.asarray(w['L0_attn_out_proj_w'], np.float32))
    bcm = np.eye(128, dtype=np.float32)
    bcm[0, :] = 1.0
    c['c_bcmask_lhsT'] = bcm
    mw0 = np.zeros((128, 128), np.float32)
    ii2 = np.arange(128)
    mw0[1:, :] = np.where(ii2[1:, None] > ii2[None, :], -1e30, 0.0)
    c['c_maskW0'] = mw0
    pm32 = np.zeros((128, 128), np.float32)
    for m in range(128):
        h64 = (m // 64) * 64
        d = m % 64
        pm32[h64 + (d + 32) % 64, m] = 1.0
    c['c_perm32_bf'] = pm32.astype(ml_dtypes.bfloat16)

    HIDP = 2816
    for l in range(2):
        c[f'w_l1f_qkv{l}'] = np.ascontiguousarray(np.asarray(w['L1_ffn_wqkv'][l], np.float32))
        c[f'w_l1f_wo{l}'] = np.ascontiguousarray(np.asarray(w['L1_ffn_wo'][l], np.float32))
        w1 = np.asarray(w['L1_ffn_w1'][l], np.float32)
        w1p = np.zeros((1024, 2 * HIDP), np.float32)
        w1p[:, 0:2730] = w1[:, 0:2730]
        w1p[:, HIDP:HIDP + 2730] = w1[:, 2730:5460]
        c[f'w_l1f_w1{l}'] = w1p
        w2 = np.asarray(w['L1_ffn_w2'][l], np.float32)
        w2p = np.zeros((HIDP, 1024), np.float32)
        w2p[0:2730, :] = w2
        c[f'w_l1f_w2{l}'] = w2p
    DH, WIN = 64, 512
    half_idx = np.arange(0, DH, 2, dtype=np.float32)
    inv_freq = 1.0 / (10000.0 ** (half_idx / DH))
    t_ = np.arange(T, dtype=np.float32)
    freqs = np.concatenate([t_[:, None] * inv_freq[None, :]] * 2, 1)
    base = (half_idx + 0.4 * DH) / (1.4 * DH)
    power = (t_ - T // 2) / float(WIN // 2)
    xsc = np.concatenate([base[None, :] ** power[:, None]] * 2, 1)
    sign = np.concatenate([-np.ones(32), np.ones(32)]).astype(np.float32)
    csq = (np.cos(freqs) * xsc * 8.0).T
    ssq = (np.sin(freqs) * xsc * 8.0).T * sign[:, None]
    csk = (np.cos(freqs) / xsc).T
    ssk = (np.sin(freqs) / xsc).T * sign[:, None]
    tab = np.zeros((128, 4 * T), np.float32)
    for i, tbl in enumerate((csq, ssq, csk, ssk)):
        side, j = i // 2, i % 2
        for ci in range(4):
            blk = tbl[:, ci * 512:(ci + 1) * 512]
            off = ci * 2048 + side * 1024 + j * 512
            tab[0:64, off:off + 512] = blk
            tab[64:128, off:off + 512] = blk
    c['c_l1f_tab'] = tab.astype(ml_dtypes.bfloat16)
    pm = np.zeros((128, 128), np.float32)
    for m in range(128):
        h64 = (m // 64) * 64
        d = m % 64
        k = h64 + (d + 32) % 64
        pm[k, m] = 1.0
    c['c_permrot_bf'] = pm.astype(ml_dtypes.bfloat16)

    pack_cols(c, [
        ('l0f_n1', col_layout(w['L0_ffn_n1_w'])),
        ('l0f_n2', col_layout(w['L0_ffn_n2_w'])),
        ('l0f_qn', np.concatenate([np.asarray(w['L0_ffn_qn_w'], np.float32)] * 2).reshape(128, 1)),
        ('l0f_kn', np.concatenate([np.asarray(w['L0_ffn_kn_w'], np.float32)] * 2).reshape(128, 1)),
        ('l1a_b', col_layout(w['L1_attn_Wa_b'])),
        ('l1f_qs0', np.concatenate([np.asarray(w['L1_ffn_q_scale'][0], np.float32)] * 2).reshape(128, 1)),
        ('l1f_qs1', np.concatenate([np.asarray(w['L1_ffn_q_scale'][1], np.float32)] * 2).reshape(128, 1)),
        ('l1f_ks0', np.concatenate([np.asarray(w['L1_ffn_k_scale'][0], np.float32)] * 2).reshape(128, 1)),
        ('l1f_ks1', np.concatenate([np.asarray(w['L1_ffn_k_scale'][1], np.float32)] * 2).reshape(128, 1)),
        ('l1f_lng0', col_layout(w['L1_ffn_ln_g'][0])),
        ('l1f_lnb0', col_layout(w['L1_ffn_ln_b'][0])),
        ('l1f_lng1', col_layout(w['L1_ffn_ln_g'][1])),
        ('l1f_lnb1', col_layout(w['L1_ffn_ln_b'][1])),
        ('l1f_b1a0', col_layout(np.asarray(w['L1_ffn_b1'][0], np.float32)[0:2730], pad_to=22)),
        ('l1f_b1g0', col_layout(np.asarray(w['L1_ffn_b1'][0], np.float32)[2730:5460], pad_to=22)),
        ('l1f_b1a1', col_layout(np.asarray(w['L1_ffn_b1'][1], np.float32)[0:2730], pad_to=22)),
        ('l1f_b1g1', col_layout(np.asarray(w['L1_ffn_b1'][1], np.float32)[2730:5460], pad_to=22)),
        ('l1f_b20', col_layout(w['L1_ffn_b2'][0])),
        ('l1f_b21', col_layout(w['L1_ffn_b2'][1])),
        ('l0a_lng', col_layout(w['L0_attn_norm_scale'])),
        ('l0a_lnb', col_layout(w['L0_attn_norm_bias'])),
        ('l0a_Alog', col_layout(w['L0_attn_A_log'])),
        ('l0a_dtb', col_layout(w['L0_attn_dt_bias'])),
        ('l0a_bcs', np.concatenate([np.asarray(w['L0_attn_bcB_scale'], np.float32),
                                    np.asarray(w['L0_attn_bcC_scale'], np.float32)]).reshape(128, 1)),
        ('l0a_bcb', np.concatenate([np.asarray(w['L0_attn_B_bias'], np.float32),
                                    np.asarray(w['L0_attn_C_bias'], np.float32)]).reshape(128, 1)),
        ('l0a_D', col_layout(w['L0_attn_D'])),
    ])
    return c


class Ctx:
    def col(self, name, i=0):
        off, k = COLS[name]
        assert i < k
        return self.t['allcols'][:, off + i:off + i + 1]

    def __init__(self, nc, p, aps):
        self.nc = nc
        self.p = p
        self.aps = aps
        self.t = {}
        self.wbuf = None
        self.wbufb = None
        self.wbuf_free = [None, None]
        self.wcast_free = [None, None]


def emit_load_consts(ctx, es):
    nc, p = ctx.nc, ctx.p
    tt = ctx.t
    tt['ident_bf'] = sb(nc, es, 'ident_bf', [128, 128], BF16)
    tt['ident_f32'] = sb(nc, es, 'ident_f32', [128, 128], F32)
    tt['ones_col_bf'] = sb(nc, es, 'ones_col_bf', [128, 1], BF16)
    tt['ones_col_f32'] = sb(nc, es, 'ones_col_f32', [128, 1], F32)
    tt['ones_row_f32'] = sb(nc, es, 'ones_row_f32', [1, 128], F32)
    tt['ones128'] = sb(nc, es, 'ones128', [128, 128], F32)
    tt['mask_causal_bf'] = sb(nc, es, 'mask_causal_bf', [128, 128], BF16)
    tt['mask_keep_ge_bf'] = sb(nc, es, 'mask_keep_ge_bf', [128, 128], BF16)
    tt['permrot_bf'] = sb(nc, es, 'permrot_bf', [128, 128], BF16)
    for name in ('ident_bf', 'ident_f32', 'ones_col_bf', 'ones_col_f32',
                 'ones_row_f32', 'ones128',
                 'mask_causal_bf', 'mask_keep_ge_bf', 'permrot_bf'):
        p.dma('sy', tt[name][:], ctx.aps['c_' + name][:, :])
    tt['fconsts'] = sb(nc, es, 'fconsts', [128, len(FCONSTS)], F32)
    p.dma('sy', tt['fconsts'][:], ctx.aps['c_fconsts'][:, :])
    ncols_total = ctx.aps['c_allcols'].shape[1]
    tt['allcols'] = sb(nc, es, 'allcols', [128, ncols_total], F32)
    p.dma('sy', tt['allcols'][:], ctx.aps['c_allcols'][:, :])
    for i, v in enumerate(FCONSTS):
        nc.const_aps.register(F32, float(v), tt['fconsts'][:, i:i + 1])
    cld = p.cnt['sq']
    for e in ('te', 've', 'se', 'po'):
        p.wait_only(e, [('sq', cld)])
    ctx.wbuf = [sb(nc, es, f'wpool{i}', [128, 1024], F32) for i in range(2)]
    ctx.wbufb = [sb(nc, es, f'wpoolb{i}', [128, 2048], BF16) for i in range(2)]
    for i in range(2):
        p.op('po', (lambda e, i=i: e.memset(ctx.wbuf[i][:], 0.0)))
        ctx.wbuf_free[i] = ('po', p.cnt['po'])


WSLAB_COLS = 128


class Slabs:

    def __init__(self, ctx, lhsT_ap, K, k0=0):
        self.ctx = ctx
        self.ap = lhsT_ap
        self.K = K
        self.k0 = k0
        self.KC = K // 128
        assert K % 128 == 0 and self.KC * WSLAB_COLS <= 2048
        self.i = 0

    def get(self, m0, mw):
        ctx, p = self.ctx, self.ctx.p
        bi = self.i % 2
        self.i += 1
        wsem = 'w%d' % bi
        halves = [(0, self.KC)] if self.KC * WSLAB_COLS <= 1024 else \
            [(0, (self.KC + 1) // 2), ((self.KC + 1) // 2, self.KC - (self.KC + 1) // 2)]
        first = True
        for (kc0, kcn) in halves:
            waits = []
            if first and ctx.wbuf_free[bi]:
                waits.append(ctx.wbuf_free[bi])
            if not first:
                waits.append(('po', p.cnt['po']))
            p.dma('sy', ctx.wbuf[bi][:, 0:kcn * WSLAB_COLS]
                  .rearrange('p (kc m) -> p kc m', m=WSLAB_COLS)[:, :, 0:mw],
                  self.ap[self.k0 + kc0 * 128:self.k0 + (kc0 + kcn) * 128, m0:m0 + mw]
                  .rearrange('(kc p) m -> p kc m', p=128),
                  waits=waits, sem=wsem)
            ld = p.cnt[wsem]
            cw = [(wsem, ld)]
            if first and ctx.wcast_free[bi]:
                cw.append(ctx.wcast_free[bi])
            p.op('po', (lambda e, b=bi, o=kc0 * WSLAB_COLS, n=kcn * WSLAB_COLS:
                        e.tensor_copy(ctx.wbufb[b][:, o:o + n], ctx.wbuf[b][:, 0:n])),
                 waits=cw)
            first = False
        cast_done = p.cnt['po']
        ctx.wbuf_free[bi] = ('po', cast_done)

        def tile(k, b=bi, mw=mw):
            return ctx.wbufb[b][:, k * WSLAB_COLS:k * WSLAB_COLS + mw]
        return bi, tile, cast_done

    def done(self, bi, te_cnt):
        self.ctx.wcast_free[bi] = ('te', te_cnt)


def emit_linear(ctx, es, name, lhsT_ap, K, M, rhs_tiles, out_cb,
                rhs_col0=0, n_cols=T, nchunk=512, m_groups=None,
                wait_rhs=()):
    nc, p = ctx.nc, ctx.p
    KC = K // 128
    assert K % 128 == 0 and KC * WSLAB_COLS <= 2048
    if m_groups is None:
        m_groups = [(i * 128, min(128, M - i * 128)) for i in range((M + 127) // 128)]
    slabs = Slabs(ctx, lhsT_ap, K)
    nci = [(i * nchunk, min(nchunk, n_cols - i * nchunk))
           for i in range((n_cols + nchunk - 1) // nchunk)]
    for gi, (m0, mw) in enumerate(m_groups):
        bi, tile_fn, cast_done = slabs.get(m0, mw)
        for ni, (c0, cw) in enumerate(nci):
            bidx = (gi * len(nci) + ni) % 6
            bank = p.bank(bidx)
            te_waits = [('po', cast_done)] + p.bank_waits(bidx) + list(wait_rhs)
            for k in range(KC):
                p.op('te', (lambda e, k=k, bank=bank, c0=c0, cw=cw, mw=mw,
                            tile_fn=tile_fn, rt=rhs_tiles[k], rc=rhs_col0:
                            e.matmul(bank[0:mw, 0:cw], tile_fn(k),
                                     rt[:, rc + c0:rc + c0 + cw],
                                     start=(k == 0), stop=(k == KC - 1))),
                     waits=te_waits if k == 0 else ())
            te_cnt = p.cnt['te']
            slabs.done(bi, te_cnt)
            out_cb(gi, m0, mw, ni, c0, cw, bidx, bank, te_cnt)
    return


def emit_postln(ctx, es, name, src, dst, eps=1e-5, dst_bf=None, gamma=None, beta=None,
                rms=False, wait_src=(), end_scope=True, src2=None, src_scale=1.0,
                src_builder=None):
    nc, p = ctx.nc, ctx.p
    tt = ctx.t
    nf = len(src)
    dinv = 1.0 / (nf * 128)
    les = es.enter_context(ExitStack())
    stat_t = sb(nc, les, name + '_stat', [1, 2 * T], F32)
    sums = stat_t[:, 0:T]; sumsq = stat_t[:, T:2 * T]
    mu = sums; var = sumsq; rstd = sumsq
    sq_bf = sb(nc, les, name + '_sq', [128, T], BF16)
    src_bf = sb(nc, les, name + '_srcbf', [128, T], BF16)
    rstd_b = sb(nc, les, name + '_rstdb', [128, T], F32)
    m2_b = None if rms else sb(nc, les, name + '_m2b', [128, T], F32)
    need_comb = (src2 is not None) or (src_scale != 1.0) or (src_builder is not None)
    rot = [sb(nc, les, name + f'_rot{i}', [128, T], F32) for i in range(2)] \
        if need_comb else None
    rot_gate = [0, 0]

    def comb(f, phase, waits):
        s2 = src2[f] if src2 is not None else None
        if not need_comb or (src_builder is None and s2 is None and src_scale == 1.0):
            return src[f][:], 0
        rt = rot[f % 2]
        w = list(waits) + ([('ve', p.cnt['ve'])] if True else [])
        if rot_gate[f % 2]:
            w.append(rot_gate[f % 2])
        if src_builder is not None:
            cve = src_builder(f, rt, w)
            return rt[:], cve
        if s2 is None:
            p.op('ve', (lambda e, f=f, rt=rt: e.tensor_single_scalar(
                rt[:], src[f][:], float(src_scale), ALU.mult)), waits=w)
        elif src_scale == 1.0:
            p.op('ve', (lambda e, f=f, rt=rt, s2=s2: e.tensor_tensor(
                rt[:], src[f][:], s2[:], ALU.add)), waits=w)
        else:
            p.op('ve', (lambda e, f=f, rt=rt, s2=s2: e.scalar_tensor_tensor(
                rt[:], src[f][:], float(src_scale), s2[:], ALU.mult, ALU.add)), waits=w)
        return rt[:], p.cnt['ve']
    bq = 6; bx = 7
    te_prev = 0
    for f in range(nf):
        w0 = list(wait_src) if f == 0 else [('te', te_prev)]
        cap, cve = comb(f, 'stats', w0)
        wc = w0 + ([('ve', cve)] if cve else [])
        p.op('se', (lambda e, cap=cap: e.activation(sq_bf[:], cap, AF.Square)),
             waits=wc)
        se_c = p.cnt['se']
        p.op('po', (lambda e, cap=cap: e.tensor_copy(src_bf[:], cap)), waits=wc)
        po_c = p.cnt['po']
        rot_gate[f % 2] = ('se', se_c)
        for ci in range(NT):
            bs = p.bank(ci); bqq = p.bank(4 + ci)
            p.op('te', (lambda e, ci=ci, f=f, bs=bs:
                        e.matmul(bs[0:1, 0:512], tt['ones_col_bf'][:],
                                 src_bf[:, ci * 512:(ci + 1) * 512],
                                 start=(f == 0), stop=(f == nf - 1))),
                 waits=[('po', po_c)] + (p.bank_waits(ci) if f == 0 else []))
            p.op('te', (lambda e, ci=ci, f=f, bq2=bqq:
                        e.matmul(bq2[0:1, 0:512], tt['ones_col_bf'][:],
                                 sq_bf[:, ci * 512:(ci + 1) * 512],
                                 start=(f == 0), stop=(f == nf - 1))),
                 waits=[('se', se_c)] + (p.bank_waits(4 + ci) if f == 0 else []))
        te_prev = p.cnt['te']
    te_stats = p.cnt['te']
    for ci in range(NT):
        p.op('ve', (lambda e, ci=ci: e.tensor_copy(sums[:, ci * 512:(ci + 1) * 512],
                                                   p.bank(ci)[0:1, 0:512])),
             waits=[('te', te_stats)])
        p.op('ve', (lambda e, ci=ci: e.tensor_copy(sumsq[:, ci * 512:(ci + 1) * 512],
                                                   p.bank(4 + ci)[0:1, 0:512])))
    for ci in range(NT):
        p.bank_release(ci, 've', p.cnt['ve'])
        p.bank_release(4 + ci, 've', p.cnt['ve'])
    musq = rstd_b[0:1, 0:T]
    vstep = lambda fn: p.op('ve', fn, waits=[('ve', p.cnt['ve'])])
    if rms:
        vstep(lambda e: e.tensor_single_scalar(var[:], sumsq[:], dinv, ALU.mult))
    else:
        vstep(lambda e: e.tensor_single_scalar(mu[:], sums[:], dinv, ALU.mult))
        vstep(lambda e: e.tensor_single_scalar(var[:], sumsq[:], dinv, ALU.mult))
        vstep(lambda e: e.tensor_tensor(musq[:], mu[:], mu[:], ALU.mult))
        vstep(lambda e: e.tensor_tensor(var[:], var[:], musq[:], ALU.subtract))
    ve_var = p.cnt['ve']
    p.op('se', (lambda e: e.activation(rstd[:], var[:], AF.Ln, bias=float(eps))),
         waits=[('ve', ve_var)])
    p.op('se', (lambda e: e.activation(rstd[:], rstd[:], AF.Exp, scale=-0.5)),
         waits=[('se', p.cnt['se'])])
    se_rstd = p.cnt['se']
    ve_m2 = p.cnt['ve']
    for ci in range(NT):
        bidx = ci % 2
        p.op('te', (lambda e, ci=ci, b=p.bank(bidx):
                    e.matmul(b[:, 0:512], tt['ones_row_f32'][:],
                             rstd[:, ci * 512:(ci + 1) * 512], start=True, stop=True)),
             waits=[('se', se_rstd)] + p.bank_waits(bidx))
        p.op('ve', (lambda e, ci=ci, b=p.bank(bidx):
                    e.tensor_copy(rstd_b[:, ci * 512:(ci + 1) * 512], b[:, 0:512])),
             waits=[('te', p.cnt['te'])])
        p.bank_release(bidx, 've', p.cnt['ve'])
        if not rms:
            bidx2 = 2 + ci % 2
            p.op('te', (lambda e, ci=ci, b=p.bank(bidx2):
                        e.matmul(b[:, 0:512], tt['ones_row_f32'][:],
                                 mu[:, ci * 512:(ci + 1) * 512], start=True, stop=True)),
                 waits=[('ve', ve_var)] + p.bank_waits(bidx2))
            p.op('ve', (lambda e, ci=ci, b=p.bank(bidx2):
                        e.tensor_copy(m2_b[:, ci * 512:(ci + 1) * 512], b[:, 0:512])),
                 waits=[('te', p.cnt['te'])])
            p.bank_release(bidx2, 've', p.cnt['ve'])
    ve_bcast = p.cnt['ve']
    scr = None
    if dst is None and not need_comb:
        scr = [sb(nc, les, name + f'_scr{i}', [128, T], F32) for i in range(2)]
        dstl = lambda f: scr[f % 2]
        scr_gate = [0, 0]
    elif dst is None:
        dstl = lambda f: rot[f % 2]
        scr_gate = [0, 0]
        scr = 'rot'
    else:
        dstl = lambda f: dst[f]
    for f in range(nf):
        dtile = dstl(f)
        w0 = [('ve', ve_bcast)]
        g = scr_gate[f % 2] if scr is not None else 0
        if g:
            w0.append(('po', g[1]) if isinstance(g, tuple) else ('se', g))
        if scr == 'rot' and isinstance(rot_gate[f % 2], tuple):
            w0.append(rot_gate[f % 2])
        cap, cve = comb(f, 'apply', w0)
        if cve:
            w0 = w0 + [('ve', cve)]
        if not rms:
            p.op('ve', (lambda e, d=dtile, cap=cap: e.tensor_tensor(d[:], cap, m2_b[:], ALU.subtract)),
                 waits=w0)
            p.op('ve', (lambda e, d=dtile: e.tensor_tensor(d[:], d[:], rstd_b[:], ALU.mult)),
                 waits=[('ve', p.cnt['ve'])])
        else:
            p.op('ve', (lambda e, d=dtile, cap=cap: e.tensor_tensor(d[:], cap, rstd_b[:], ALU.mult)),
                 waits=w0)
        if gamma is not None:
            p.op('se', (lambda e, d=dtile, f=f: e.activation(
                (dst_bf[f][:] if dst_bf is not None else d[:]), d[:], AF.Identity,
                bias=beta[f], scale=gamma[f])), waits=[('ve', p.cnt['ve'])])
            if scr is not None:
                scr_gate[f % 2] = p.cnt['se']
        elif dst_bf is not None:
            p.op('po', (lambda e, d=dtile, f=f: e.tensor_copy(dst_bf[f][:], d[:])),
                 waits=[('ve', p.cnt['ve'])])
            if scr is not None:
                scr_gate[f % 2] = ('po-gate', p.cnt['po'])
        if need_comb:
            rot_gate[f % 2] = ('ve', p.cnt['ve'])
    if end_scope:
        p.barrier()
        les.close()


def emit_l1_attn(ctx, es, x, x_bf):
    nc, p = ctx.nc, ctx.p
    tt = ctx.t
    h_bf = [sb(nc, es, f'l1a_hb{i}', [128, T], BF16) for i in range(2)]
    bias_ld = 0
    with ExitStack() as ies:
        a_t = sb(nc, ies, 'l1a_a', [128, T], F32)
        b_t = sb(nc, ies, 'l1a_b', [128, T], F32)
        h_t = sb(nc, ies, 'l1a_h', [128, T], F32)
        for i in range(2):
            ascan = p.cnt['ve']

            def evac_a(gi, m0, mw, ni, c0, cw, bidx, pap, te_cnt, i=i, ascan=ascan):
                p.op('se', (lambda e, c0=c0, cw=cw, pap=pap, i=i:
                            e.activation(a_t[:, c0:c0 + cw], pap[0:128, 0:cw],
                                         AF.Sigmoid, bias=ctx.col('l1a_b', i))),
                     waits=[('te', te_cnt), ('ve', ascan)])
                p.bank_release(bidx, 'se', p.cnt['se'])
            emit_linear(ctx, ies, 'l1a_wa', ctx.aps['w_l1a_WaT'], D, 128, x_bf, evac_a,
                        m_groups=[(i * 128, 128)])

            def evac_b(gi, m0, mw, ni, c0, cw, bidx, pap, te_cnt, ascan=ascan):
                p.op('ve', (lambda e, c0=c0, cw=cw, pap=pap:
                            e.tensor_copy(b_t[:, c0:c0 + cw], pap[0:128, 0:cw])),
                     waits=[('te', te_cnt), ('ve', ascan)])
                p.bank_release(bidx, 've', p.cnt['ve'])
            emit_linear(ctx, ies, 'l1a_bm', ctx.aps['w_l1a_BT'], D, 128, x_bf, evac_b,
                        m_groups=[(i * 128, 128)])
            p.op('ve', (lambda e: e.tensor_tensor_scan(h_t[:], a_t[:], b_t[:],
                                                       0.0, ALU.mult, ALU.add)),
                 waits=[('se', p.cnt['se']), ('ve', p.cnt['ve'])])
            p.op('po', (lambda e, i=i: e.tensor_copy(h_bf[i][:], h_t[:])),
                 waits=[('ve', p.cnt['ve'])])
    scan_done_po = p.cnt['po']

    KC_C, KC_D = 2, 8
    s_dt = Slabs(ctx, ctx.aps['w_l1a_DT'], D)
    wc = sb(nc, es, 'l1a_wc', [128, 256], F32)
    wcb = sb(nc, es, 'l1a_wcb', [128, 256], BF16)
    wc_gate = None
    for mf in range(NF):
        waits = [wc_gate] if wc_gate else []
        p.dma('po', wc[:].rearrange('p (kc m) -> p kc m', m=128),
              ctx.aps['w_l1a_CT'][:, mf * 128:(mf + 1) * 128]
              .rearrange('(kc p) m -> p kc m', p=128), waits=waits)
        gl = p.cnt['gq']
        p.op('po', (lambda e: e.tensor_copy(wcb[:], wc[:])), waits=[('gq', gl)])
        wc_cast = p.cnt['po']
        bi, tile, cast = s_dt.get(mf * 128, 128)
        for ni in range(NT):
            bidx = (mf * NT + ni) % 6
            bank = p.bank(bidx)
            c0 = ni * 512
            for k in range(KC_C):
                p.op('te', (lambda e, k=k, bank=bank, c0=c0:
                            e.matmul(bank[:, 0:512], wcb[:, k * 128:(k + 1) * 128],
                                     h_bf[k][:, c0:c0 + 512], start=(k == 0), stop=False)),
                     waits=([('po', wc_cast)] + p.bank_waits(bidx)) if k == 0 else ())
            for k in range(KC_D):
                p.op('te', (lambda e, k=k, bank=bank, c0=c0, tile=tile:
                            e.matmul(bank[:, 0:512], tile(k),
                                     x_bf[k][:, c0:c0 + 512], start=False,
                                     stop=(k == KC_D - 1))),
                     waits=[('po', cast)] if k == 0 else ())
            te_cnt = p.cnt['te']
            p.op('ve', (lambda e, mf=mf, c0=c0, bank=bank:
                        e.tensor_tensor(x[mf][:, c0:c0 + 512], x[mf][:, c0:c0 + 512],
                                        bank[:, 0:512], ALU.add)),
                 waits=[('te', te_cnt)])
            p.bank_release(bidx, 've', p.cnt['ve'])
        s_dt.done(bi, p.cnt['te'])
        wc_gate = ('te', p.cnt['te'])
    emit_postln(ctx, es, 'l1a_ln', x, x, eps=1e-5, dst_bf=x_bf,
                wait_src=[('ve', p.cnt['ve'])])


def emit_l0_ffn(ctx, es, x, x_bf):
    nc, p = ctx.nc, ctx.p
    tt = ctx.t
    aps = ctx.aps
    EPS = 1.1920929e-07

    zerocol = tt['fconsts'][:, 5:6]

    emit_postln(ctx, es, 'l0f_n1', x, None, eps=EPS, dst_bf=x_bf, rms=True,
                gamma=[ctx.col('l0f_n1', f) for f in range(NF)], beta=[zerocol] * NF,
                wait_src=[('sq', p.cnt['sq'])])

    p.dump('n1_xn0', x_bf[0][:])
    delta = [sb(nc, es, f'l0f_delta{i}', [128, T], BF16) for i in range(6)]
    oes = es.enter_context(ExitStack())
    o_all = [sb(nc, oes, f'l0f_o{i}', [128, T], BF16) for i in range(NF)]
    sq_wq = Slabs(ctx, aps['w_l0f_wqT'], D)
    sq_wk = Slabs(ctx, aps['w_l0f_wkT'], D)
    sq_wv = Slabs(ctx, aps['w_l0f_wvT'], D)

    with ExitStack() as pes:
        q_bf = sb(nc, pes, 'l0f_q', [128, T], BF16)
        k_bf = sb(nc, pes, 'l0f_k', [128, T], BF16)
        vtm = sb(nc, pes, 'l0f_vtm', [128, T], BF16)
        P_s = sb(nc, pes, 'l0f_P', [128, T], BF16)
        PT_s = sb(nc, pes, 'l0f_PT', [128, T], BF16)
        vfm = PT_s
        rstd_scr = sb(nc, pes, 'l0f_rstd', [128, 1536], F32)
        bsc = sb(nc, pes, 'l0f_bsc', [128, 512], F32)
        cols = sb(nc, pes, 'l0f_cols', [128, 8], F32)
        otok = sb(nc, pes, 'l0f_otok', [128, 64], BF16)
        SLOTS = [(4, 0), (4, 32), (4, 64), (5, 0), (5, 32), (5, 64), (6, 0), (6, 32)]

        def proj_pair(slab, hp, wait_rhs):
            bi, tile, cast = slab.get(hp * 128, 128)
            for ci in range(NT):
                w0 = [('po', cast)] + p.bank_waits(ci) + wait_rhs
                for k in range(8):
                    p.op('te', (lambda e, k=k, ci=ci, tile=tile:
                                e.matmul(p.bank(ci)[0:128, 0:512], tile(k),
                                         x_bf[k][:, ci * 512:(ci + 1) * 512],
                                         start=(k == 0), stop=(k == 7))),
                         waits=w0 if k == 0 else ())
            slab.done(bi, p.cnt['te'])
            return p.cnt['te']

        def qk_rms(out_bf, gcol, fold8, te_proj):
            p.op('se', (lambda e: e.activation(P_s[:], p.bank_span(0, 4), AF.Square)),
                 waits=[('te', te_proj)] + p.bank_waits(4) + p.bank_waits(5) + p.bank_waits(6))
            sq_done = p.cnt['se']
            for si, (h2, ci) in enumerate([(a, b) for a in range(2) for b in range(4)]):
                b, r = SLOTS[si]
                p.op('te', (lambda e, h2=h2, ci=ci, b=b, r=r:
                            e.matmul(p.bank(b)[r:r + 1, 0:512],
                                     tt['ones_col_bf'][h2 * 64:(h2 + 1) * 64, 0:1],
                                     P_s[h2 * 64:(h2 + 1) * 64, ci * 512:(ci + 1) * 512],
                                     start=True, stop=True)),
                     waits=[('se', sq_done)] if si == 0 else ())
            te_norm = p.cnt['te']
            for si in range(8):
                b, r = SLOTS[si]
                p.op('se', (lambda e, b=b, r=r: e.activation(
                    p.bank(b)[r:r + 1, 0:512], p.bank(b)[r:r + 1, 0:512], AF.Ln,
                    scale=1.0 / 64.0, bias=float(EPS))),
                     waits=([('te', te_norm)] if si == 0 else [('se', p.cnt['se'])]))
                p.op('se', (lambda e, b=b, r=r: e.activation(
                    p.bank(b)[r:r + 1, 0:512], p.bank(b)[r:r + 1, 0:512], AF.Exp,
                    scale=-0.5, bias=(-2.0794415416798357 if fold8 else 0.0))),
                     waits=[('se', p.cnt['se'])])
            se_rows = p.cnt['se']
            for si in range(8):
                b, r = SLOTS[si]
                j = (4, 5, 6).index(b)
                p.op('ve', (lambda e, b=b, r=r, j=j: e.tensor_copy(
                    rstd_scr[r:r + 1, j * 512:(j + 1) * 512],
                    p.bank(b)[r:r + 1, 0:512])),
                     waits=[('se', se_rows)] if si == 0 else ())
            ve_rows = p.cnt['ve']
            for b in (4, 5):
                p.bank_release(b, 've', ve_rows)
            for si, (h2, ci) in enumerate([(a, b) for a in range(2) for b in range(4)]):
                b, r = SLOTS[si]
                j = (4, 5, 6).index(b)
                p.op('te', (lambda e, h2=h2, ci=ci, r=r, j=j:
                            e.matmul(p.bank(4 + ci)[h2 * 64:h2 * 64 + 64, 0:512],
                                     tt['ones128'][r:r + 1, 0:64],
                                     rstd_scr[r:r + 1, j * 512:(j + 1) * 512],
                                     start=True, stop=True)),
                     waits=([('ve', ve_rows)] + p.bank_waits(4) + p.bank_waits(5)
                            + p.bank_waits(6) + p.bank_waits(7)) if si == 0 else ())
            te_b = p.cnt['te']
            for ci in range(NT):
                p.op('ve', (lambda e, ci=ci: e.tensor_copy(bsc[:], p.bank(4 + ci)[:, 0:512])),
                     waits=([('te', te_b)] if ci == 0 else []) + [('se', p.cnt['se']),
                                                                  ('ve', p.cnt['ve'])])
                p.bank_release(4 + ci, 've', p.cnt['ve'])
                p.op('ve', (lambda e, ci=ci: e.tensor_tensor(
                    bsc[:], p.bank(ci)[:, 0:512], bsc[:], ALU.mult)),
                     waits=[('ve', p.cnt['ve'])])
                p.bank_release(ci, 've', p.cnt['ve'])
                p.op('se', (lambda e, ci=ci, out_bf=out_bf, gcol=gcol: e.activation(
                    out_bf[:, ci * 512:(ci + 1) * 512], bsc[:], AF.Identity,
                    bias=zerocol, scale=gcol)), waits=[('ve', p.cnt['ve'])])
            return p.cnt['se']

        dma_w = p.cnt['sq']
        for hp in range(NF):
            wq_rhs = [('se', p.cnt['se']), ('sq', dma_w)]
            tq = proj_pair(sq_wq, hp, wq_rhs)
            qk_done = qk_rms(q_bf, ctx.col('l0f_qn'), True, tq)
            tk = proj_pair(sq_wk, hp, [])
            qk_done = qk_rms(k_bf, ctx.col('l0f_kn'), False, tk)
            bi, tile, cast = sq_wv.get(hp * 128, 128)
            for ci in range(NT):
                w0 = [('po', cast)] + p.bank_waits(ci)
                for k in range(8):
                    p.op('te', (lambda e, k=k, ci=ci, tile=tile:
                                e.matmul(p.bank(ci)[0:128, 0:512], tile(k),
                                         x_bf[k][:, ci * 512:(ci + 1) * 512],
                                         start=(k == 0), stop=(k == 7))),
                         waits=w0 if k == 0 else ())
                p.op('ve', (lambda e, ci=ci: e.tensor_copy(
                    vfm[:, ci * 512:(ci + 1) * 512], p.bank(ci)[:, 0:512])),
                     waits=[('te', p.cnt['te'])])
                p.bank_release(ci, 've', p.cnt['ve'])
            sq_wv.done(bi, p.cnt['te'])
            ve_vfm = p.cnt['ve']
            for m in range(16):
                b = 4 + m // 8
                p.op('te', (lambda e, m=m, b=b: e.transpose(
                    p.bank_bf(b)[:, (m % 8) * 128:(m % 8) * 128 + 128],
                    vfm[:, m * 128:(m + 1) * 128], tt['ident_bf'][:])),
                     waits=([('ve', ve_vfm)] + p.bank_waits(4) + p.bank_waits(5))
                     if m == 0 else ())
                if m % 8 == 7:
                    p.op('ve', (lambda e, m=m, b=b: e.tensor_copy(
                        vtm[:, (m - 7) * 128:(m + 1) * 128], p.bank_bf(b)[:, 0:1024])),
                         waits=[('te', p.cnt['te'])])
                    p.bank_release(b, 've', p.cnt['ve'])
            if hp == 0:
                p.dump('q0', q_bf[:])
                p.dump('k0', k_bf[:])
                p.dump('vtm0', vtm[:])
            negmax = cols[:, 0:1]; sums = cols[:, 1:2]; rsum = cols[:, 2:3]
            rsum_read = 0
            otok_read = 0
            pt_read = 0
            for h2 in range(2):
                for qt in range(16):
                    kw = (qt + 1) * 128
                    nk = qt + 1
                    nkc = (kw + 511) // 512
                    w0 = [('se', p.cnt['se']), ('ve', p.cnt['ve'])]
                    for b2 in range(nkc):
                        w0 += p.bank_waits(b2)
                    for c in range(nkc):
                        c0, c1 = c * 512, min(kw, (c + 1) * 512)
                        p.op('te', (lambda e, h2=h2, qt=qt, c0=c0, c1=c1:
                                    e.matmul(p.psum.ap()[0:128, c0:c1],
                                             q_bf[h2 * 64:h2 * 64 + 64, qt * 128:(qt + 1) * 128],
                                             k_bf[h2 * 64:h2 * 64 + 64, c0:c1],
                                             start=True, stop=True)),
                             waits=w0 if c == 0 else ())
                    p.op('te', (lambda e, kw=kw: e.matmul(
                        p.psum.ap()[0:128, kw - 128:kw], tt['ident_bf'][:],
                        tt['mask_causal_bf'][:], start=False, stop=True,
                        skip_group_check=True)))
                    te_s = p.cnt['te']
                    p.op('ve', (lambda e, kw=kw: e.tensor_reduce(
                        negmax[:], p.psum.ap()[0:128, 0:kw], AX.X, ALU.max, negate=True)),
                         waits=[('te', te_s)])
                    ve_max = p.cnt['ve']
                    p.op('se', (lambda e, kw=kw: e.activation(
                        P_s[:, 0:kw], p.psum.ap()[0:128, 0:kw], AF.Exp,
                        bias=negmax[:, 0:1], accum_out=sums[:, 0:1])),
                         waits=[('ve', ve_max)]
                         + ([('ve', rsum_read)] if rsum_read else [])
                         + ([('te', pt_read)] if pt_read else []))
                    se_exp = p.cnt['se']
                    for b2 in range(nkc):
                        p.bank_release(b2, 'se', se_exp)
                    p.op('ve', (lambda e: e.reciprocal(rsum[:], sums[:, 0:1])),
                         waits=[('se', se_exp)])
                    rsum_done = p.cnt['ve']
                    rsum_read = rsum_done
                    for ki in range(nk):
                        b = 4 + ki // 8
                        p.op('te', (lambda e, ki=ki, b=b: e.transpose(
                            p.bank_bf(b)[:, (ki % 8) * 128:(ki % 8) * 128 + 128],
                            P_s[:, ki * 128:(ki + 1) * 128], tt['ident_bf'][:])),
                             waits=([('se', se_exp)] + p.bank_waits(4) + p.bank_waits(5))
                             if ki == 0 else ())
                    te_pt = p.cnt['te']
                    pt_read = te_pt
                    nspan = min(nk, 8)
                    p.op('ve', (lambda e, nspan=nspan: e.tensor_copy(
                        PT_s[:, 0:nspan * 128], p.bank_bf(4)[:, 0:nspan * 128])),
                         waits=[('te', te_pt)])
                    if nk > 8:
                        p.op('ve', (lambda e, nk=nk: e.tensor_copy(
                            PT_s[:, 1024:nk * 128], p.bank_bf(5)[:, 0:(nk - 8) * 128])))
                    ve_pt = p.cnt['ve']
                    p.bank_release(4, 've', ve_pt)
                    p.bank_release(5, 've', ve_pt)
                    for ki in range(nk):
                        p.op('te', (lambda e, ki=ki, h2=h2, last=(ki == nk - 1):
                                    e.matmul(p.bank(6)[0:128, 0:64],
                                             PT_s[:, ki * 128:(ki + 1) * 128],
                                             vtm[:, ki * 128 + h2 * 64:ki * 128 + h2 * 64 + 64],
                                             start=(ki == 0), stop=last)),
                             waits=([('ve', ve_pt)] + p.bank_waits(6)) if ki == 0 else ())
                    te_pv = p.cnt['te']
                    p.op('ve', (lambda e: e.tensor_scalar(
                        otok[:], p.bank(6)[0:128, 0:64], rsum[:, 0:1], None, ALU.mult)),
                         waits=[('te', te_pv), ('ve', rsum_done)]
                         + ([('te', otok_read)] if otok_read else []))
                    ve_o = p.cnt['ve']
                    p.bank_release(6, 've', ve_o)
                    p.op('te', (lambda e, qt=qt: e.transpose(
                        p.bank_bf(7)[0:64, (qt % 8) * 128:(qt % 8) * 128 + 128],
                        otok[:], tt['ident_bf'][:])),
                         waits=[('ve', ve_o)] + (p.bank_waits(7) if qt % 8 == 0 else []))
                    otok_read = p.cnt['te']
                    if qt % 8 == 7:
                        p.op('ve', (lambda e, h2=h2, hp=hp, qt=qt: e.tensor_copy(
                            o_all[hp][h2 * 64:h2 * 64 + 64, (qt - 7) * 128:(qt + 1) * 128],
                            p.bank_bf(7)[0:64, 0:1024])),
                             waits=[('te', otok_read)])
                        p.bank_release(7, 've', p.cnt['ve'])
    p.dump('o_all0', o_all[0][:])
    p.dump('o_all7', o_all[7][:])
    def evac_wo(gi, m0, mw, ni, c0, cw, bidx, pap, te_cnt):
        p.op('ve', (lambda e, gi=gi, c0=c0, cw=cw, pap=pap: e.tensor_copy(
            delta[gi][:, c0:c0 + cw], pap[0:128, 0:cw])),
             waits=[('te', te_cnt)])
        p.bank_release(bidx, 've', p.cnt['ve'])
    emit_linear(ctx, oes, 'l0f_wo', aps['w_l0f_woT'], D, D, o_all, evac_wo,
                m_groups=[(256 + i * 128, 128) for i in range(6)],
                wait_rhs=[('ve', p.cnt['ve'])])
    p.barrier()
    oes.close()

    p.dump('postwo_x2', x[2][:])
    emit_postln(ctx, es, 'l0f_n2', x, None, eps=EPS, dst_bf=x_bf, rms=True,
                gamma=[ctx.col('l0f_n2', f) for f in range(NF)], beta=[zerocol] * NF,
                wait_src=[('ve', p.cnt['ve'])],
                src2=[None, None] + [delta[i] for i in range(6)])

    wts_tm = sb(nc, es, 'l0f_wtstm', [128, 64], F32)
    with ExitStack() as ges:
        logits = sb(nc, ges, 'l0f_logits', [4, T], F32)
        pscr = sb(nc, ges, 'l0f_pscr', [128, 4], F32)
        pscr2 = sb(nc, ges, 'l0f_pscr2', [128, 4], F32)
        keepm = sb(nc, ges, 'l0f_keepm', [128, 4], F32)
        cmp8 = sb(nc, ges, 'l0f_cmp8', [128, 4], U8)
        gc = sb(nc, ges, 'l0f_gcols', [128, 8], F32)

        def evac_gate(gi, m0, mw, ni, c0, cw, bidx, pap, te_cnt):
            p.op('ve', (lambda e, c0=c0, cw=cw, pap=pap: e.tensor_copy(
                logits[:, c0:c0 + cw], pap[0:4, 0:cw])), waits=[('te', te_cnt)])
            p.bank_release(bidx, 've', p.cnt['ve'])
        emit_linear(ctx, ges, 'l0f_gate', aps['w_l0f_gateT'], D, 4, x_bf, evac_gate,
                    m_groups=[(0, 4)])
        ve_lg = p.cnt['ve']
        for ci in range(16):
            p.op('te', (lambda e, ci=ci: e.transpose(
                p.bank(4)[0:128, ci * 4:ci * 4 + 4],
                logits[0:4, ci * 128:(ci + 1) * 128], tt['ident_f32'][0:4, 0:4])),
                 waits=([('ve', ve_lg)] + p.bank_waits(4)) if ci == 0 else ())
        te_lt = p.cnt['te']
        nm = gc[:, 0:1]; ssum = gc[:, 1:2]; rs = gc[:, 2:3]; pm1 = gc[:, 3:4]
        pm2 = gc[:, 4:5]; dcol = gc[:, 5:6]; dc2 = gc[:, 6:7]; rdc = gc[:, 7:8]
        eqf = sb(nc, ges, 'l0f_eqf', [128, 4], F32)
        t1s = sb(nc, ges, 'l0f_t1', [128, 4], F32)
        t2s = sb(nc, ges, 'l0f_t2', [128, 4], F32)
        kfs = sb(nc, ges, 'l0f_kf', [128, 4], F32)
        cmp8b = sb(nc, ges, 'l0f_cmp8b', [128, 4], U8)

        def vstep(fn, extra=()):
            p.op('ve', fn, waits=[('ve', p.cnt['ve'])] + list(extra))
        for ci in range(16):
            sl = p.bank(4)[0:128, ci * 4:ci * 4 + 4]
            vstep((lambda e, sl=sl: e.tensor_reduce(nm[:], sl, AX.X, ALU.max,
                                                    negate=True)),
                  extra=([('te', te_lt)] if ci == 0 else [('se', p.cnt['se'])]))
            p.op('se', (lambda e, sl=sl: e.activation(pscr[:], sl, AF.Exp,
                                                      bias=nm[:, 0:1],
                                                      accum_out=ssum[:, 0:1])),
                 waits=[('ve', p.cnt['ve'])])
            vstep(lambda e: e.reciprocal(rs[:], ssum[:, 0:1]),
                  extra=[('se', p.cnt['se'])])
            vstep(lambda e: e.tensor_scalar(pscr2[:], pscr[:], rs[:, 0:1], None, ALU.mult))
            vstep(lambda e: e.tensor_reduce(pm1[:], pscr2[:], AX.X, ALU.max))
            vstep(lambda e: e.tensor_scalar(cmp8[:], pscr2[:], pm1[:, 0:1], None, ALU.is_ge))
            vstep(lambda e: e.tensor_copy(eqf[:], cmp8[:]))
            vstep(lambda e: e.tensor_tensor(t1s[:], pscr2[:], eqf[:], ALU.mult))
            vstep(lambda e: e.tensor_tensor(t2s[:], pscr2[:], t1s[:], ALU.subtract))
            vstep(lambda e: e.tensor_reduce(pm2[:], t2s[:], AX.X, ALU.max))
            vstep(lambda e: e.tensor_scalar(cmp8b[:], pscr2[:], pm2[:, 0:1], None, ALU.is_ge))
            vstep(lambda e: e.tensor_copy(keepm[:], cmp8b[:]))
            vstep(lambda e: e.tensor_tensor(dcol[:], pm1[:], pm2[:], ALU.add))
            vstep(lambda e: e.tensor_single_scalar(dc2[:], dcol[:], 1e-8, ALU.add))
            vstep(lambda e: e.reciprocal(rdc[:], dc2[:]))
            vstep(lambda e: e.tensor_tensor(kfs[:], keepm[:], pscr2[:], ALU.mult))
            vstep((lambda e, ci=ci: e.tensor_scalar(wts_tm[:, ci * 4:ci * 4 + 4],
                                                    kfs[:], rdc[:, 0:1], None, ALU.mult)))
        p.bank_release(4, 've', p.cnt['ve'])
        p.dump('xf0', x_bf[0][:])
        p.dump('logits', logits[:])
        p.dump('wts_tm', wts_tm[:])
        p.barrier()

    slabs_e = {nm2: Slabs(ctx, aps['w_l0f_' + nm2], kk) for nm2, kk in [
        ('e0upT', 1024), ('e0dnT', 2048), ('e1w1T', 1024), ('e1w2T', 1024),
        ('e1dnT', 2048), ('e2l1T', 1024), ('e2l2T', 2048), ('e2l3T', 2048),
        ('e2l4T', 2048), ('e3dnT', 1024), ('e3upT', 1024), ('e3outT', 2048)]}
    with ExitStack() as ees:
        hidA = sb(nc, ees, 'l0f_hidA', [128, 8192], BF16)
        hidB = sb(nc, ees, 'l0f_hidB', [128, 8192], BF16)
        wts_b = sb(nc, ees, 'l0f_wtsb', [128, T], F32)
        wrow = sb(nc, ees, 'l0f_wrow', [1, T], F32)
        hid_gate = {id(hidA): 0, id(hidB): 0}

        def hid_layer(slab, src_fn, KC, nm_t, act, out_t, tq, wait_extra=(), src_t=None):
            for m in range(nm_t):
                bi, tile, cast = slab.get(m * 128, 128)
                b = 6 + m % 2
                w0 = [('po', cast)] + p.bank_waits(b) + list(wait_extra)
                if m == 0 and hid_gate[id(out_t)]:
                    w0.append(('te', hid_gate[id(out_t)]))
                for k in range(KC):
                    p.op('te', (lambda e, k=k, b=b, tile=tile, src_fn=src_fn:
                                e.matmul(p.bank(b)[0:128, 0:512], tile(k), src_fn(k),
                                         start=(k == 0), stop=(k == KC - 1))),
                         waits=w0 if k == 0 else ())
                slab.done(bi, p.cnt['te'])
                hg = hid_gate[id(out_t)]
                p.op('se', (lambda e, m=m, b=b, out_t=out_t, act=act: e.activation(
                    out_t[:, m * 512:(m + 1) * 512], p.bank(b)[0:128, 0:512], act)),
                     waits=[('te', p.cnt['te'])] + ([('te', hg)] if hg else []))
                p.bank_release(b, 'se', p.cnt['se'])
            if src_t is not None:
                hid_gate[id(src_t)] = p.cnt['te']

        def weight_hid(out_t, nm_t, ew):
            for m in range(nm_t):
                p.op('ve', (lambda e, m=m, out_t=out_t, ew=ew: e.tensor_tensor(
                    out_t[:, m * 512:(m + 1) * 512], out_t[:, m * 512:(m + 1) * 512],
                    wts_b[:, ew * 512:(ew + 1) * 512], ALU.mult)),
                     waits=[('se', p.cnt['se']), ('ve', p.cnt['ve'])])

        def down_proj(slab, hid_t, KC, first, last):
            for fi in range(6):
                bi, tile, cast = slab.get(256 + fi * 128, 128)
                w0 = [('po', cast), ('ve', p.cnt['ve'])] + (p.bank_waits(fi) if first else [])
                for k in range(KC):
                    p.op('te', (lambda e, k=k, fi=fi, tile=tile, hid_t=hid_t:
                                e.matmul(p.bank(fi)[0:128, 0:512], tile(k),
                                         hid_t[:, k * 512:(k + 1) * 512],
                                         start=(first and k == 0),
                                         stop=(last and k == KC - 1),
                                         skip_group_check=True)),
                         waits=w0 if k == 0 else ())
                slab.done(bi, p.cnt['te'])
            hid_gate[id(hid_t)] = p.cnt['te']

        for tq in range(NT):
            tc0 = tq * 512
            xfq = lambda k, tc0=tc0: x_bf[k][:, tc0:tc0 + 512]
            vw = p.cnt['ve']
            for ew in range(4):
                for c4 in range(4):
                    ti = tq * 4 + c4
                    p.op('te', (lambda e, ew=ew, c4=c4, ti=ti: e.transpose(
                        p.bank(4 + ew)[0:1, c4 * 128:(c4 + 1) * 128],
                        wts_tm[0:128, ti * 4 + ew:ti * 4 + ew + 1], tt['ident_f32'][:])),
                         waits=([('ve', vw)] + p.bank_waits(4 + ew)) if c4 == 0 else ())
            te_w = p.cnt['te']
            for ew in range(4):
                p.op('ve', (lambda e, ew=ew: e.tensor_copy(
                    wrow[0:1, ew * 512:(ew + 1) * 512], p.bank(4 + ew)[0:1, 0:512])),
                     waits=[('te', te_w)])
                p.bank_release(4 + ew, 've', p.cnt['ve'])
            ve_wr = p.cnt['ve']
            for ew in range(4):
                p.op('te', (lambda e, ew=ew: e.matmul(
                    p.bank(4 + ew)[0:128, 0:512], tt['ones_row_f32'][:],
                    wrow[0:1, ew * 512:(ew + 1) * 512], start=True, stop=True)),
                     waits=[('ve', ve_wr)] + p.bank_waits(4 + ew))
            te_wb = p.cnt['te']
            for ew in range(4):
                p.op('ve', (lambda e, ew=ew: e.tensor_copy(
                    wts_b[:, ew * 512:(ew + 1) * 512], p.bank(4 + ew)[0:128, 0:512])),
                     waits=[('te', te_wb)])
                p.bank_release(4 + ew, 've', p.cnt['ve'])

            hid_layer(slabs_e['e0upT'], xfq, 8, 16, AF.Gelu, hidA, tq)
            if tq == 0:
                p.dump('hid0', hidA[:])
                p.dump('wtsb0', wts_b[:])
            weight_hid(hidA, 16, 0)
            down_proj(slabs_e['e0dnT'], hidA, 16, True, False)
            hid_layer(slabs_e['e1w1T'], xfq, 8, 16, AF.Silu, hidB, tq)
            for m in range(16):
                bi, tile, cast = slabs_e['e1w2T'].get(m * 128, 128)
                b = 6 + m % 2
                w0 = [('po', cast)] + p.bank_waits(b)
                for k in range(8):
                    p.op('te', (lambda e, k=k, b=b, tile=tile, xfq=xfq:
                                e.matmul(p.bank(b)[0:128, 0:512], tile(k), xfq(k),
                                         start=(k == 0), stop=(k == 7))),
                         waits=w0 if k == 0 else ())
                slabs_e['e1w2T'].done(bi, p.cnt['te'])
                hg = hid_gate[id(hidB)]
                p.op('ve', (lambda e, m=m, b=b: e.tensor_tensor(
                    hidB[:, m * 512:(m + 1) * 512], hidB[:, m * 512:(m + 1) * 512],
                    p.bank(b)[0:128, 0:512], ALU.mult)),
                     waits=[('te', p.cnt['te']), ('se', p.cnt['se'])])
                p.bank_release(b, 've', p.cnt['ve'])
            weight_hid(hidB, 16, 1)
            down_proj(slabs_e['e1dnT'], hidB, 16, False, False)
            hid_layer(slabs_e['e2l1T'], xfq, 8, 16, AF.Gelu, hidA, tq)
            hid_layer(slabs_e['e2l2T'],
                      (lambda k: hidA[:, k * 512:(k + 1) * 512]), 16, 16, AF.Gelu, hidB, tq,
                      wait_extra=[('se', p.cnt['se'])], src_t=hidA)
            hid_layer(slabs_e['e2l3T'],
                      (lambda k: hidB[:, k * 512:(k + 1) * 512]), 16, 16, AF.Gelu, hidA, tq,
                      wait_extra=[('se', p.cnt['se'])], src_t=hidB)
            weight_hid(hidA, 16, 2)
            down_proj(slabs_e['e2l4T'], hidA, 16, False, False)
            hid_layer(slabs_e['e3dnT'], xfq, 8, 8, AF.Gelu, hidB, tq)
            hid_layer(slabs_e['e3upT'],
                      (lambda k: hidB[:, k * 512:(k + 1) * 512]), 8, 16, AF.Gelu, hidA, tq,
                      wait_extra=[('se', p.cnt['se'])], src_t=hidB)
            weight_hid(hidA, 16, 3)
            down_proj(slabs_e['e3outT'], hidA, 16, False, True)
            for fi in range(6):
                p.op('ve', (lambda e, fi=fi, tc0=tc0: e.tensor_tensor(
                    delta[fi][:, tc0:tc0 + 512], delta[fi][:, tc0:tc0 + 512],
                    p.bank(fi)[0:128, 0:512], ALU.add)),
                     waits=[('te', p.cnt['te'])])
                p.bank_release(fi, 've', p.cnt['ve'])
            p.flush()
        p.barrier()

    for ff in range(NF):
        p.dump(f'premoe_x{ff}', x[ff][:])
    emit_postln(ctx, es, 'l0f_ln', x, x, eps=1e-5, dst_bf=x_bf,
                wait_src=[('ve', p.cnt['ve'])],
                src2=[None, None] + [delta[i] for i in range(6)], src_scale=2.0)


def emit_l1_ffn(ctx, es, x, x_bf):
    nc, p = ctx.nc, ctx.p
    tt = ctx.t
    aps = ctx.aps
    zerocol = tt['fconsts'][:, 5:6]
    delta = [sb(nc, es, f'l1f_delta{i}', [128, T], BF16) for i in range(NF)]
    SLOTS = [(4, 0), (4, 32), (4, 64), (5, 0), (5, 32), (5, 64), (6, 0), (6, 32)]
    sh_gate = [0, 0]

    def build_sh(l, dcur):
        for f in range(NF):
            w = [('ve', p.cnt['ve']), ('te', p.cnt['te']), ('se', p.cnt['se']),
                 ('po', p.cnt['po'])]
            if f < 4:
                if dcur is None:
                    continue
                p.op('ve', (lambda e, f=f, dcur=dcur: e.tensor_tensor(
                    x_bf[f][:], x[f][:], dcur[f][:], ALU.add)), waits=w)
            else:
                p.op('ve', (lambda e, f=f: e.memset(x_bf[f][:, 0:1], 0.0)), waits=w)
                if dcur is None:
                    p.op('po', (lambda e, f=f: e.tensor_copy(
                        x_bf[f][:, 1:T], x[f][:, 0:T - 1])), waits=[('ve', p.cnt['ve'])])
                else:
                    p.op('ve', (lambda e, f=f, dcur=dcur: e.tensor_tensor(
                        x_bf[f][:, 1:T], x[f][:, 0:T - 1], dcur[f][:, 0:T - 1],
                        ALU.add)))

    def qk_l2rms(out_t, gcol, te_proj, vstage, rstd_scr, bsc):
        p.op('se', (lambda e: e.activation(vstage[:], p.bank_span(0, 4), AF.Square)),
             waits=[('te', te_proj)] + p.bank_waits(4) + p.bank_waits(5) + p.bank_waits(6))
        sq_done = p.cnt['se']
        for si, (h2, ci) in enumerate([(a, b) for a in range(2) for b in range(4)]):
            b, r = SLOTS[si]
            p.op('te', (lambda e, h2=h2, ci=ci, b=b, r=r:
                        e.matmul(p.bank(b)[r:r + 1, 0:512],
                                 tt['ones_col_bf'][h2 * 64:(h2 + 1) * 64, 0:1],
                                 vstage[h2 * 64:(h2 + 1) * 64, ci * 512:(ci + 1) * 512],
                                 start=True, stop=True)),
                 waits=[('se', sq_done)] if si == 0 else ())
        te_norm = p.cnt['te']
        for si in range(8):
            b, r = SLOTS[si]
            p.op('se', (lambda e, b=b, r=r: e.activation(
                p.bank(b)[r:r + 1, 0:512], p.bank(b)[r:r + 1, 0:512], AF.Ln,
                bias=1e-24)),
                 waits=([('te', te_norm)] if si == 0 else [('se', p.cnt['se'])]))
            p.op('se', (lambda e, b=b, r=r: e.activation(
                p.bank(b)[r:r + 1, 0:512], p.bank(b)[r:r + 1, 0:512], AF.Exp,
                scale=-0.5)), waits=[('se', p.cnt['se'])])
        se_rows = p.cnt['se']

        def slot_row(b, r):
            if b == 6:
                return bsc[r:r + 1, 0:512]
            j = (4, 5).index(b)
            return rstd_scr[r:r + 1, j * 512:(j + 1) * 512]
        for si in range(8):
            b, r = SLOTS[si]
            p.op('ve', (lambda e, b=b, r=r: e.tensor_copy(
                slot_row(b, r), p.bank(b)[r:r + 1, 0:512])),
                 waits=[('se', se_rows)] if si == 0 else ())
        ve_rows = p.cnt['ve']
        for b in (4, 5):
            p.bank_release(b, 've', ve_rows)
        for si, (h2, ci) in enumerate([(a, b) for a in range(2) for b in range(4)]):
            b, r = SLOTS[si]
            p.op('te', (lambda e, h2=h2, ci=ci, b=b, r=r:
                        e.matmul(p.bank(4 + ci)[h2 * 64:h2 * 64 + 64, 0:512],
                                 tt['ones128'][r:r + 1, 0:64],
                                 slot_row(b, r),
                                 start=True, stop=True)),
                 waits=([('ve', ve_rows)] + p.bank_waits(4) + p.bank_waits(5)
                        + p.bank_waits(6) + p.bank_waits(7)) if si == 0 else ())
        te_b = p.cnt['te']
        for ci in range(NT):
            p.op('ve', (lambda e, ci=ci: e.tensor_copy(bsc[:], p.bank(4 + ci)[:, 0:512])),
                 waits=([('te', te_b)] if ci == 0 else []) + [('se', p.cnt['se']),
                                                              ('ve', p.cnt['ve'])])
            p.bank_release(4 + ci, 've', p.cnt['ve'])
            p.op('ve', (lambda e, ci=ci: e.tensor_tensor(
                bsc[:], p.bank(ci)[:, 0:512], bsc[:], ALU.mult)),
                 waits=[('ve', p.cnt['ve'])])
            p.bank_release(ci, 've', p.cnt['ve'])
            p.op('se', (lambda e, ci=ci, out_t=out_t, gcol=gcol: e.activation(
                out_t[:, ci * 512:(ci + 1) * 512], bsc[:], AF.Identity,
                bias=zerocol, scale=gcol)), waits=[('ve', p.cnt['ve'])])
        return p.cnt['se']

    for l in range(2):
        les1 = ExitStack()
        q_t = [sb(nc, les1, f'l1f_q{i}_{l}', [128, T], BF16) for i in range(4)]
        k_t = [sb(nc, les1, f'l1f_k{i}_{l}', [128, T], BF16) for i in range(4)]
        vtm = sb(nc, les1, f'l1f_vtm_{l}', [128, 16 * 512], BF16)
        vstage = sb(nc, les1, f'l1f_vstage_{l}', [128, T], BF16)
        tabb = sb(nc, les1, f'l1f_tab_{l}', [128, 1024], BF16)
        rstd_scr = sb(nc, les1, f'l1f_rstd_{l}', [128, 1024], F32)
        bsc = sb(nc, les1, f'l1f_bsc_{l}', [128, 512], F32)
        cols = sb(nc, les1, f'l1f_cols_{l}', [128, 8], F32)
        otok = sb(nc, les1, f'l1f_otok_{l}', [128, 64], BF16)
        sq_qkv = Slabs(ctx, aps[f'w_l1f_qkv{l}'], D)
        sq_wo = Slabs(ctx, aps[f'w_l1f_wo{l}'], 512)
        sq_w1 = Slabs(ctx, aps[f'w_l1f_w1{l}'], D)
        sq_w2a = Slabs(ctx, aps[f'w_l1f_w2{l}'], 1408)
        sq_w2b = Slabs(ctx, aps[f'w_l1f_w2{l}'], 1408, k0=1408)
        build_sh(l, None if l == 0 else delta)
        sh_done = [('po', p.cnt['po']), ('ve', p.cnt['ve'])]
        for gi in range(12):
            bi, tile, cast = sq_qkv.get(gi * 128, 128)
            for ci in range(NT):
                w0 = [('po', cast)] + p.bank_waits(ci) + (sh_done if gi == 0 else [])
                for k in range(8):
                    p.op('te', (lambda e, k=k, ci=ci, tile=tile:
                                e.matmul(p.bank(ci)[0:128, 0:512], tile(k),
                                         x_bf[k][:, ci * 512:(ci + 1) * 512],
                                         start=(k == 0), stop=(k == 7))),
                         waits=w0 if k == 0 else ())
            sq_qkv.done(bi, p.cnt['te'])
            if gi < 4:
                qk_l2rms(q_t[gi], ctx.col(f'l1f_qs{l}'), p.cnt['te'], vstage, rstd_scr, bsc)
            elif gi < 8:
                qk_l2rms(k_t[gi - 4], ctx.col(f'l1f_ks{l}'), p.cnt['te'], vstage, rstd_scr, bsc)
            else:
                vt = gi - 8
                for ci in range(NT):
                    p.op('ve', (lambda e, ci=ci: e.tensor_copy(
                        vstage[:, ci * 512:(ci + 1) * 512], p.bank(ci)[:, 0:512])),
                         waits=[('te', p.cnt['te'])] if ci == 0 else ())
                    p.bank_release(ci, 've', p.cnt['ve'])
                ve_v = p.cnt['ve']
                for m in range(16):
                    b = 4 + m // 8
                    p.op('te', (lambda e, m=m, b=b: e.transpose(
                        p.bank_bf(b)[:, (m % 8) * 128:(m % 8) * 128 + 128],
                        vstage[:, m * 128:(m + 1) * 128], tt['ident_bf'][:])),
                         waits=([('ve', ve_v)] + p.bank_waits(4) + p.bank_waits(5))
                         if m == 0 else ())
                    if m % 8 == 7:
                        p.op('ve', (lambda e, m=m, b=b, vt=vt: e.tensor_copy(
                            vtm[:].rearrange('p (tt c) -> p tt c', c=512)
                            [:, m - 7:m + 1, vt * 128:(vt + 1) * 128],
                            p.bank_bf(b)[:, 0:1024]
                            .rearrange('p (tt c) -> p tt c', c=128))),
                             waits=[('te', p.cnt['te'])])
                        p.bank_release(b, 've', p.cnt['ve'])
        for ci in range(NT):
            for side in range(2):
                p.dma('po', tabb[:],
                      aps['c_l1f_tab'][:, ci * 2048 + side * 1024:
                                       ci * 2048 + side * 1024 + 1024],
                      waits=[('te', p.cnt['te']), ('ve', p.cnt['ve'])])
                tl = p.cnt['gq']
                for f in range(4):
                    qt_ = q_t[f] if side == 0 else k_t[f]
                    p.op('te', (lambda e, qt_=qt_, ci=ci: e.matmul(
                        p.bank(6)[:, 0:512],
                        tt['permrot_bf'][:], qt_[:, ci * 512:(ci + 1) * 512],
                        start=True, stop=True)),
                         waits=[('gq', tl), ('se', p.cnt['se']),
                                ('ve', p.cnt['ve'])] + p.bank_waits(6))
                    te_pm = p.cnt['te']
                    p.op('ve', (lambda e, qt_=qt_, ci=ci: e.tensor_tensor(
                        bsc[:], qt_[:, ci * 512:(ci + 1) * 512], tabb[:, 0:512],
                        ALU.mult)),
                         waits=[('gq', tl), ('ve', p.cnt['ve']), ('se', p.cnt['se'])])
                    p.op('ve', (lambda e, qt_=qt_, ci=ci: e.tensor_tensor(
                        qt_[:, ci * 512:(ci + 1) * 512], p.bank(6)[:, 0:512],
                        tabb[:, 512:1024], ALU.mult)),
                         waits=[('te', te_pm), ('ve', p.cnt['ve'])])
                    p.op('ve', (lambda e, qt_=qt_, ci=ci: e.tensor_tensor(
                        qt_[:, ci * 512:(ci + 1) * 512],
                        qt_[:, ci * 512:(ci + 1) * 512], bsc[:], ALU.add)),
                         waits=[('ve', p.cnt['ve'])])
                    p.bank_release(6, 've', p.cnt['ve'])
        negmax = cols[:, 0:1]; sums = cols[:, 1:2]; rsum = cols[:, 2:3]
        rsum_read = 0
        otok_read = 0
        pt_read = 0
        for h2x in range(8):
            ft, h2 = h2x // 2, h2x % 2
            qtile = q_t[ft]; ktile = k_t[ft]
            for qt in range(16):
                nk = min(qt + 1, 5)
                kc0 = qt + 1 - nk
                kw = nk * 128
                w0 = [('se', p.cnt['se']), ('ve', p.cnt['ve'])]
                w0 += p.bank_waits(0) + p.bank_waits(1)
                nkc = (kw + 511) // 512
                for c in range(nkc):
                    c0, c1 = c * 512, min(kw, (c + 1) * 512)
                    p.op('te', (lambda e, h2=h2, qt=qt, c0=c0, c1=c1, kc0=kc0,
                                qtile=qtile, ktile=ktile:
                                e.matmul(p.psum.ap()[0:128, c0:c1],
                                         qtile[h2 * 64:h2 * 64 + 64, qt * 128:(qt + 1) * 128],
                                         ktile[h2 * 64:h2 * 64 + 64,
                                               kc0 * 128 + c0:kc0 * 128 + c1],
                                         start=True, stop=True)),
                         waits=w0 if c == 0 else ())
                p.op('te', (lambda e, kw=kw: e.matmul(
                    p.psum.ap()[0:128, kw - 128:kw], tt['ident_bf'][:],
                    tt['mask_causal_bf'][:], start=False, stop=True,
                    skip_group_check=True)))
                if nk == 5:
                    p.op('te', (lambda e: e.matmul(
                        p.psum.ap()[0:128, 0:128], tt['ident_bf'][:],
                        tt['mask_keep_ge_bf'][:], start=False, stop=True,
                        skip_group_check=True)))
                te_s = p.cnt['te']
                p.op('ve', (lambda e, kw=kw: e.tensor_reduce(
                    negmax[:], p.psum.ap()[0:128, 0:kw], AX.X, ALU.max, negate=True)),
                     waits=[('te', te_s)])
                ve_max = p.cnt['ve']
                p.op('se', (lambda e, kw=kw: e.activation(
                    vstage[:, 0:kw], p.psum.ap()[0:128, 0:kw], AF.Exp,
                    bias=negmax[:, 0:1], accum_out=sums[:, 0:1])),
                     waits=[('ve', ve_max)]
                     + ([('ve', rsum_read)] if rsum_read else [])
                     + ([('te', pt_read)] if pt_read else []))
                se_exp = p.cnt['se']
                for b2 in range(nkc):
                    p.bank_release(b2, 'se', se_exp)
                p.op('ve', (lambda e: e.reciprocal(rsum[:], sums[:, 0:1])),
                     waits=[('se', se_exp)])
                rsum_done = p.cnt['ve']
                rsum_read = rsum_done
                for ki in range(nk):
                    p.op('te', (lambda e, ki=ki: e.transpose(
                        p.bank_bf(4)[:, ki * 128:ki * 128 + 128],
                        vstage[:, ki * 128:(ki + 1) * 128], tt['ident_bf'][:])),
                         waits=([('se', se_exp)] + p.bank_waits(4)) if ki == 0 else ())
                te_pt = p.cnt['te']
                pt_read = te_pt
                p.op('ve', (lambda e, kw=kw: e.tensor_copy(
                    vstage[:, 1024:1024 + kw], p.bank_bf(4)[:, 0:kw])), waits=[('te', te_pt)])
                ve_pt = p.cnt['ve']
                p.bank_release(4, 've', ve_pt)
                for ki in range(nk):
                    p.op('te', (lambda e, ki=ki, kc0=kc0, h2x=h2x, last=(ki == nk - 1):
                                e.matmul(p.bank(6)[0:128, 0:64],
                                         vstage[:, 1024 + ki * 128:1024 + (ki + 1) * 128],
                                         vtm[:, (kc0 + ki) * 512 + h2x * 64:
                                             (kc0 + ki) * 512 + h2x * 64 + 64],
                                         start=(ki == 0), stop=last)),
                         waits=([('ve', ve_pt)] + p.bank_waits(6)) if ki == 0 else ())
                te_pv = p.cnt['te']
                p.op('ve', (lambda e: e.tensor_scalar(
                    otok[:], p.bank(6)[0:128, 0:64], rsum[:, 0:1], None, ALU.mult)),
                     waits=[('te', te_pv), ('ve', rsum_done)]
                     + ([('te', otok_read)] if otok_read else []))
                ve_o = p.cnt['ve']
                p.bank_release(6, 've', ve_o)
                p.op('te', (lambda e, qt=qt: e.transpose(
                    p.bank_bf(7)[0:64, (qt % 8) * 128:(qt % 8) * 128 + 128],
                    otok[:], tt['ident_bf'][:])),
                     waits=[('ve', ve_o)] + (p.bank_waits(7) if qt % 8 == 0 else []))
                otok_read = p.cnt['te']
                if qt % 8 == 7:
                    p.op('ve', (lambda e, h2=h2, ft=ft, qt=qt: e.tensor_copy(
                        x_bf[ft][h2 * 64:h2 * 64 + 64, (qt - 7) * 128:(qt + 1) * 128],
                        p.bank_bf(7)[0:64, 0:1024])),
                         waits=[('te', otok_read)])
                    p.bank_release(7, 've', p.cnt['ve'])
        def evac_wo(gi, m0, mw, ni, c0, cw, bidx, pap, te_cnt, l=l):
            if l == 0:
                p.op('ve', (lambda e, gi=gi, c0=c0, cw=cw, pap=pap: e.tensor_copy(
                    delta[gi][:, c0:c0 + cw], pap[0:128, 0:cw])),
                     waits=[('te', te_cnt)])
            else:
                p.op('ve', (lambda e, gi=gi, c0=c0, cw=cw, pap=pap: e.tensor_tensor(
                    delta[gi][:, c0:c0 + cw], delta[gi][:, c0:c0 + cw],
                    pap[0:128, 0:cw], ALU.add)),
                     waits=[('te', te_cnt)])
            p.bank_release(bidx, 've', p.cnt['ve'])
        emit_linear(ctx, es, f'l1f_wo{l}', aps[f'w_l1f_wo{l}'], 512, D,
                    x_bf[0:4], evac_wo, wait_rhs=[('ve', p.cnt['ve'])])
        p.barrier()
        les1.close()

        def shift_builder(f, rt, waits, l=l):
            sc = delta[f]
            if f < 4:
                p.op('ve', (lambda e, f=f, rt=rt: e.tensor_tensor(
                    rt[:], x[f][:], delta[f][:], ALU.add)), waits=waits)
            else:
                p.op('ve', (lambda e, f=f, rt=rt: e.tensor_tensor(
                    rt[:, 1:T], x[f][:, 0:T - 1], delta[f][:, 0:T - 1], ALU.add)),
                     waits=waits)
                p.op('ve', (lambda e, rt=rt: e.memset(rt[:, 0:1], 0.0)),
                     waits=[('ve', p.cnt['ve'])])
            return p.cnt['ve']
        emit_postln(ctx, es, f'l1f_ln{l}', x, None, eps=1e-5, dst_bf=x_bf,
                    gamma=[ctx.col(f'l1f_lng{l}', f) for f in range(NF)],
                    beta=[ctx.col(f'l1f_lnb{l}', f) for f in range(NF)],
                    wait_src=[('ve', p.cnt['ve']), ('se', p.cnt['se'])],
                    src_builder=shift_builder)

        with ExitStack() as fes:
            agA = sb(nc, fes, f'l1f_agA{l}', [128, 22 * 512], BF16)
            gg = sb(nc, fes, f'l1f_gg{l}', [128, 22 * 512], BF16)
            hid_gate = {id(agA): None, id(gg): None}
            for tq in range(NT):
                tc0 = tq * 512
                for part, out_t, act, bcol in ((0, agA, AF.Identity, f'l1f_b1a{l}'),
                                               (1, gg, AF.Gelu, f'l1f_b1g{l}')):
                    for m in range(22):
                        bi, tile, cast = sq_w1.get(part * 2816 + m * 128, 128)
                        b = 6 + m % 2
                        w0 = [('po', cast)] + p.bank_waits(b)
                        if m == 0 and hid_gate[id(out_t)]:
                            w0.append(hid_gate[id(out_t)])
                        for k in range(8):
                            p.op('te', (lambda e, k=k, b=b, tile=tile, tc0=tc0:
                                        e.matmul(p.bank(b)[0:128, 0:512], tile(k),
                                                 x_bf[k][:, tc0:tc0 + 512],
                                                 start=(k == 0), stop=(k == 7))),
                                 waits=w0 if k == 0 else ())
                        sq_w1.done(bi, p.cnt['te'])
                        p.op('se', (lambda e, m=m, b=b, out_t=out_t, act=act, bcol=bcol:
                                    e.activation(out_t[:, m * 512:(m + 1) * 512],
                                                 p.bank(b)[0:128, 0:512], act,
                                                 bias=ctx.col(bcol, m))),
                             waits=[('te', p.cnt['te'])])
                        p.bank_release(b, 'se', p.cnt['se'])
                for m in range(22):
                    p.op('ve', (lambda e, m=m: e.tensor_tensor(
                        agA[:, m * 512:(m + 1) * 512], agA[:, m * 512:(m + 1) * 512],
                        gg[:, m * 512:(m + 1) * 512], ALU.mult)),
                         waits=[('se', p.cnt['se'])] if m == 0 else ())
                ve_h2 = p.cnt['ve']
                for fi in range(NF):
                    bi1, tile1, cast1 = sq_w2a.get(fi * 128, 128)
                    b = fi % 6
                    w0 = [('po', cast1), ('ve', ve_h2)] + p.bank_waits(b)
                    for k in range(11):
                        p.op('te', (lambda e, k=k, b=b, tile=tile1:
                                    e.matmul(p.bank(b)[0:128, 0:512], tile(k),
                                             agA[:, k * 512:(k + 1) * 512],
                                             start=(k == 0), stop=False)),
                             waits=w0 if k == 0 else ())
                    sq_w2a.done(bi1, p.cnt['te'])
                    bi2, tile2, cast2 = sq_w2b.get(fi * 128, 128)
                    for k in range(11):
                        p.op('te', (lambda e, k=k, b=b, tile=tile2:
                                    e.matmul(p.bank(b)[0:128, 0:512], tile(k),
                                             agA[:, (11 + k) * 512:(12 + k) * 512],
                                             start=False, stop=(k == 10))),
                             waits=[('po', cast2)] if k == 0 else ())
                    sq_w2b.done(bi2, p.cnt['te'])
                    p.op('ve', (lambda e, fi=fi, b=b, tc0=tc0, l=l:
                                e.scalar_tensor_tensor(
                                    delta[fi][:, tc0:tc0 + 512], p.bank(b)[0:128, 0:512],
                                    ctx.col(f'l1f_b2{l}', fi), delta[fi][:, tc0:tc0 + 512],
                                    ALU.add, ALU.add)),
                         waits=[('te', p.cnt['te'])])
                    p.bank_release(b, 've', p.cnt['ve'])
                hid_gate[id(agA)] = ('te', p.cnt['te'])
                hid_gate[id(gg)] = ('ve', ve_h2)
                p.flush()
        p.barrier()
    emit_postln(ctx, es, 'l1f_lnout', x, x, eps=1e-5, dst_bf=x_bf,
                wait_src=[('ve', p.cnt['ve'])],
                src2=[delta[i] for i in range(NF)], src_scale=2.0)


def emit_l0_attn(ctx, es, x, x_bf):
    nc, p = ctx.nc, ctx.p
    tt = ctx.t
    aps = ctx.aps
    TWO_PI = 6.283185307179586

    SCAL = sb(nc, es, 'l0a_scal', [128, T], F32)
    ctx.p.op('ve', (lambda e: e.memset(SCAL[:], 0.0)),
             waits=[('ve', ctx.p.cnt['ve'])])
    B_bf = sb(nc, es, 'l0a_Bbf', [64, T], BF16)
    C_bf = sb(nc, es, 'l0a_Cbf', [64, T], BF16)
    theta = sb(nc, es, 'l0a_theta', [32, T], F32)

    def evac_theta(gi, m0, mw, ni, c0, cw, bidx, pap, te_cnt):
        p.op('ve', (lambda e, c0=c0, cw=cw, pap=pap: e.tensor_copy(
            theta[:, c0:c0 + cw], pap[0:32, 0:cw])), waits=[('te', te_cnt)])
        p.bank_release(bidx, 've', p.cnt['ve'])
    emit_linear(ctx, es, 'l0a_th', aps['w_l0a_theta'], D, 32, x_bf, evac_theta,
                m_groups=[(0, 32)], wait_rhs=[('po', p.cnt['po'])])

    emit_postln(ctx, es, 'l0a_ln', x, None, eps=1e-6, dst_bf=x_bf,
                gamma=[ctx.col('l0a_lng', f) for f in range(NF)],
                beta=[ctx.col('l0a_lnb', f) for f in range(NF)],
                wait_src=[('ve', p.cnt['ve']), ('te', p.cnt['te'])])

    sq_proj = Slabs(ctx, aps['w_l0a_proj'], D)
    sq_out = Slabs(ctx, aps['w_l0a_out'], 2048)
    SLOTS = [(4, 0), (4, 32), (4, 64), (5, 0), (5, 32), (5, 64), (6, 0), (6, 32)]
    with ExitStack() as tes:
        BC_raw = sb(nc, tes, 'l0a_bcraw', [128, T], F32)
        TSCAL = sb(nc, tes, 'l0a_tscal', [16, T], F32)
        LAM16 = sb(nc, tes, 'l0a_lam16', [16, T], F32)
        ONES32 = sb(nc, tes, 'l0a_ones32', [32, T], F32)
        phib = sb(nc, tes, 'l0a_phi', [32, T], F32)
        sinb = sb(nc, tes, 'l0a_sin', [32, T], F32)
        ni32 = sb(nc, tes, 'l0a_ni32', [32, T], I32)
        rr32 = ONES32
        cs2 = sb(nc, tes, 'l0a_cs2', [128, T], BF16)
        ss2 = sb(nc, tes, 'l0a_ss2', [128, T], BF16)
        colscr = sb(nc, tes, 'l0a_colscr', [128, 8], F32)
        rowscr = phib[0:1, 0:T]
        ni32f = ni32[:].bitcast(F32)
        bcsc = sb(nc, tes, 'l0a_bcsc', [128, 512], F32)
        bcnbf = sb(nc, tes, 'l0a_bcnbf', [128, 512], BF16)
        perm32 = sb(nc, tes, 'l0a_perm32', [128, 128], BF16)
        p.dma('po', perm32[:], aps['c_perm32_bf'][:, :])
        perm_gq = p.cnt['gq']
        p.op('ve', lambda e: e.memset(ONES32[:], 1.0))

        def evac_bc(gi, m0, mw, ni, c0, cw, bidx, pap, te_cnt):
            p.op('ve', (lambda e, c0=c0, cw=cw, pap=pap: e.tensor_copy(
                BC_raw[:, c0:c0 + cw], pap[0:128, 0:cw])), waits=[('te', te_cnt)])
            p.bank_release(bidx, 've', p.cnt['ve'])
        emit_linear(ctx, tes, 'l0a_bc', aps['w_l0a_proj'], D, 128, x_bf, evac_bc,
                    m_groups=[(4096, 128)], wait_rhs=[('se', p.cnt['se'])])

        def evac_dt(gi, m0, mw, ni, c0, cw, bidx, pap, te_cnt):
            p.op('ve', (lambda e, c0=c0, cw=cw, pap=pap: e.tensor_copy(
                TSCAL[0:16, c0:c0 + cw], pap[0:16, 0:cw])), waits=[('te', te_cnt)])
            p.bank_release(bidx, 've', p.cnt['ve'])
        emit_linear(ctx, tes, 'l0a_dtm', aps['w_l0a_proj'], D, 16, x_bf, evac_dt,
                    m_groups=[(4224, 16)])

        def evac_lam(gi, m0, mw, ni, c0, cw, bidx, pap, te_cnt):
            p.op('se', (lambda e, c0=c0, cw=cw, pap=pap: e.activation(
                LAM16[0:16, c0:c0 + cw], pap[0:16, 0:cw], AF.Sigmoid)),
                 waits=[('te', te_cnt)])
            p.bank_release(bidx, 'se', p.cnt['se'])
        emit_linear(ctx, tes, 'l0a_lamm', aps['w_l0a_proj'], D, 16, x_bf, evac_lam,
                    m_groups=[(4240, 16)])

        dt = TSCAL[0:16, 0:T]
        lam = LAM16[0:16, 0:T]
        scr16 = ni32[0:16, 0:T].bitcast(F32)

        def vst(fn, w=()):
            p.op('ve', fn, waits=[('ve', p.cnt['ve'])] + list(w))
        vst(lambda e: e.tensor_scalar(dt, dt, ctx.col('l0a_dtb')[0:16, 0:1], None, ALU.add),
            w=[('se', p.cnt['se'])])
        vst(lambda e: e.tensor_single_scalar(scr16, dt, -1.0, ALU.mult))
        vst(lambda e: e.tensor_tensor(scr16, dt, scr16, ALU.min))
        p.op('se', (lambda e: e.activation(scr16, scr16, AF.Exp)),
             waits=[('ve', p.cnt['ve'])])
        p.op('se', (lambda e: e.activation(scr16, scr16, AF.Ln, bias=1.0)),
             waits=[('se', p.cnt['se'])])
        vst(lambda e: e.tensor_single_scalar(dt, dt, 0.0, ALU.max),
            w=[('se', p.cnt['se'])])
        vst(lambda e: e.tensor_tensor(dt, dt, scr16, ALU.add))
        dt_done = p.cnt['ve']
        p.op('se', (lambda e: e.activation(colscr[:, 0:1], ctx.col('l0a_Alog'),
                                           AF.Exp)), waits=[('ve', dt_done)])
        vst(lambda e: e.tensor_single_scalar(colscr[:, 0:1], colscr[:, 0:1],
                                             -1.0, ALU.mult), w=[('se', p.cnt['se'])])
        vst(lambda e: e.tensor_scalar(scr16, dt, colscr[0:16, 0:1], None, ALU.mult))
        vst(lambda e: e.tensor_tensor_scan(SCAL[0:16, 0:T], ONES32[0:16, 0:T], scr16,
                                           0.0, ALU.mult, ALU.add))
        for ci in range(NT):
            p.op('te', (lambda e, ci=ci: e.matmul(
                p.bank(0)[0:1, 0:512], tt['ones_col_f32'][0:16, 0:1],
                dt[:, ci * 512:(ci + 1) * 512], start=True, stop=True)),
                 waits=[('ve', p.cnt['ve'])] + p.bank_waits(0))
            p.op('ve', (lambda e, ci=ci: e.tensor_single_scalar(
                rowscr[:, ci * 512:(ci + 1) * 512], p.bank(0)[0:1, 0:512],
                1.0 / 16.0, ALU.mult)), waits=[('te', p.cnt['te'])])
            p.bank_release(0, 've', p.cnt['ve'])
        for ci in range(NT):
            p.op('te', (lambda e, ci=ci: e.matmul(
                p.bank(1)[0:32, 0:512], tt['ones_row_f32'][:, 0:32],
                rowscr[0:1, ci * 512:(ci + 1) * 512], start=True, stop=True)),
                 waits=[('ve', p.cnt['ve'])] + p.bank_waits(1))
            p.op('ve', (lambda e, ci=ci: e.tensor_tensor(
                theta[:, ci * 512:(ci + 1) * 512], theta[:, ci * 512:(ci + 1) * 512],
                p.bank(1)[0:32, 0:512], ALU.mult)),
                 waits=[('te', p.cnt['te']), ('ve', p.cnt['ve'])])
            p.bank_release(1, 've', p.cnt['ve'])
        vst(lambda e: e.tensor_tensor_scan(phib[:], ONES32[:], theta[:],
                                           0.0, ALU.mult, ALU.add))
        for dst, shift in ((sinb, 0.0), (theta, 0.25)):
            vst(lambda e, dst=dst, shift=shift: e.tensor_scalar(
                dst[:], phib[:], 1.0 / TWO_PI, shift + 0.5 + 16384.0,
                ALU.mult, ALU.add))
            vst(lambda e, dst=dst: e.tensor_copy(ni32[:], dst[:]))
            vst(lambda e: e.tensor_copy(rr32[:], ni32[:]))
            vst(lambda e, dst=dst: e.tensor_tensor(dst[:], dst[:], rr32[:],
                                                   ALU.subtract))
            vst(lambda e, dst=dst: e.tensor_single_scalar(ni32[:], dst[:], 0.0,
                                                          ALU.is_lt))
            vst(lambda e: e.tensor_copy(rr32[:], ni32[:]))
            vst(lambda e, dst=dst: e.tensor_tensor(dst[:], dst[:], rr32[:], ALU.add))
            vst(lambda e, dst=dst: e.tensor_scalar(dst[:], dst[:], TWO_PI,
                                                   -3.141592653589793,
                                                   ALU.mult, ALU.add))
            p.op('se', (lambda e, dst=dst: e.activation(dst[:], dst[:], AF.Sin)),
                 waits=[('ve', p.cnt['ve'])])
        trig_se = p.cnt['se']
        p.op('ve', (lambda e: e.tensor_copy(cs2[0:32, 0:T], theta[:])),
             waits=[('se', trig_se), ('ve', p.cnt['ve'])])
        p.op('ve', (lambda e: e.tensor_copy(cs2[32:64, 0:T], cs2[0:32, 0:T])),
             waits=[('ve', p.cnt['ve'])])
        p.op('ve', (lambda e: e.tensor_copy(cs2[64:96, 0:T], cs2[0:32, 0:T])))
        p.op('ve', (lambda e: e.tensor_single_scalar(ss2[0:32, 0:T], sinb[:], -1.0,
                                                     ALU.mult)))
        p.op('ve', (lambda e: e.tensor_copy(ss2[32:64, 0:T], sinb[:])))
        p.op('ve', (lambda e: e.tensor_single_scalar(ss2[64:96, 0:T], sinb[:], -1.0,
                                                     ALU.mult)))
        vrep = p.cnt['ve']
        p.dma('po', cs2[96:128, 0:T], cs2[0:32, 0:T], waits=[('ve', vrep)])
        p.dma('po', ss2[96:128, 0:T], ss2[32:64, 0:T])
        rep_gq = p.cnt['gq']

        for ci in range(NT):
            p.op('se', (lambda e, ci=ci: e.activation(
                bcnbf[:], BC_raw[:, ci * 512:(ci + 1) * 512], AF.Square)),
                 waits=[('ve', p.cnt['ve']), ('te', p.cnt['te']),
                        ('se', p.cnt['se'])])
            sq_se = p.cnt['se']
            for h2 in range(2):
                b, r = SLOTS[h2 * 4 + ci]
                p.op('te', (lambda e, h2=h2, b=b, r=r: e.matmul(
                    p.bank(b)[r:r + 1, 0:512],
                    tt['ones_col_bf'][h2 * 64:(h2 + 1) * 64, 0:1],
                    bcnbf[h2 * 64:(h2 + 1) * 64, 0:512], start=True, stop=True)),
                     waits=[('se', sq_se)] + (p.bank_waits(b) if ci == 0 else []))
        te_norm = p.cnt['te']
        for si in range(8):
            b, r = SLOTS[si]
            p.op('se', (lambda e, b=b, r=r: e.activation(
                p.bank(b)[r:r + 1, 0:512], p.bank(b)[r:r + 1, 0:512], AF.Ln,
                scale=1.0 / 64.0, bias=1e-06)),
                 waits=([('te', te_norm)] if si == 0 else [('se', p.cnt['se'])]))
            p.op('se', (lambda e, b=b, r=r: e.activation(
                p.bank(b)[r:r + 1, 0:512], p.bank(b)[r:r + 1, 0:512], AF.Exp,
                scale=-0.5)), waits=[('se', p.cnt['se'])])
        se_rows = p.cnt['se']

        def slot_row(b, r):
            i = SLOTS.index((b, r))
            if i < 4:
                return phib[0:1, i * 512:(i + 1) * 512]
            return ni32f[0:1, (i - 4) * 512:(i - 3) * 512]
        for si in range(8):
            b, r = SLOTS[si]
            p.op('ve', (lambda e, b=b, r=r: e.tensor_copy(
                slot_row(b, r), p.bank(b)[r:r + 1, 0:512])),
                 waits=[('se', se_rows)] if si == 0 else ())
        ve_rows = p.cnt['ve']
        for b in (4, 5, 6):
            p.bank_release(b, 've', ve_rows)
        for si in range(8):
            h2, ci = si // 4, si % 4
            b, r = SLOTS[si]
            p.op('te', (lambda e, h2=h2, ci=ci, b=b, r=r: e.matmul(
                p.bank(4 + ci)[h2 * 64:h2 * 64 + 64, 0:512],
                tt['ones128'][0:1, 0:64], slot_row(b, r), start=True, stop=True)),
                 waits=([('ve', ve_rows)] + p.bank_waits(4) + p.bank_waits(5)
                        + p.bank_waits(6) + p.bank_waits(7)) if si == 0 else ())
        te_b = p.cnt['te']
        for ci in range(NT):
            p.op('ve', (lambda e, ci=ci: e.tensor_tensor(
                bcsc[:], BC_raw[:, ci * 512:(ci + 1) * 512], p.bank(4 + ci)[:, 0:512],
                ALU.mult)),
                 waits=([('te', te_b)] if ci == 0 else []) + [('se', p.cnt['se']),
                                                              ('ve', p.cnt['ve'])])
            p.bank_release(4 + ci, 've', p.cnt['ve'])
            p.op('se', (lambda e, ci=ci: e.activation(
                BC_raw[:, ci * 512:(ci + 1) * 512], bcsc[:], AF.Identity,
                bias=ctx.col('l0a_bcb'), scale=ctx.col('l0a_bcs'))),
                 waits=[('ve', p.cnt['ve'])])
        bcn_se = p.cnt['se']

        for ci in range(NT):
            c0 = ci * 512
            p.op('ve', (lambda e, c0=c0: e.tensor_copy(
                bcnbf[:], BC_raw[:, c0:c0 + 512])),
                 waits=[('se', bcn_se), ('ve', p.cnt['ve'])])
            cast_ve = p.cnt['ve']
            p.op('te', (lambda e: e.matmul(p.bank(1)[:, 0:512], perm32[:],
                                           bcnbf[:], start=True, stop=True)),
                 waits=[('ve', cast_ve), ('gq', max(rep_gq, perm_gq))]
                 + p.bank_waits(1))
            te_pm = p.cnt['te']
            for half, out_t in ((0, B_bf), (1, C_bf)):
                hb = half * 64
                p.op('ve', (lambda e, c0=c0, hb=hb, out_t=out_t: e.tensor_tensor(
                    out_t[0:64, c0:c0 + 512], BC_raw[hb:hb + 64, c0:c0 + 512],
                    cs2[hb:hb + 64, c0:c0 + 512], ALU.mult)),
                     waits=[('ve', p.cnt['ve'])])
                p.op('ve', (lambda e, c0=c0, hb=hb: e.tensor_tensor(
                    bcsc[0:64, 0:512], p.bank(1)[hb:hb + 64, 0:512],
                    ss2[hb:hb + 64, c0:c0 + 512], ALU.mult)),
                     waits=[('te', te_pm), ('ve', p.cnt['ve'])])
                p.op('ve', (lambda e, c0=c0, out_t=out_t: e.tensor_tensor(
                    out_t[0:64, c0:c0 + 512], out_t[0:64, c0:c0 + 512],
                    bcsc[0:64, 0:512], ALU.add)), waits=[('ve', p.cnt['ve'])])
            p.bank_release(1, 've', p.cnt['ve'])

        vst(lambda e: e.tensor_tensor(scr16, dt, lam, ALU.mult),
            w=[('se', p.cnt['se']), ('te', p.cnt['te'])])
        vst(lambda e: e.tensor_tensor(scr16, dt, scr16, ALU.subtract))
        vst(lambda e: e.tensor_tensor(dt, dt, lam, ALU.mult))
        dtlam = dt
        vst(lambda e: e.tensor_copy(lam, dtlam))
        woff = lam
        vst(lambda e: e.tensor_tensor(LAM16[0:16, 0:T - 1],
                                      LAM16[0:16, 0:T - 1],
                                      scr16[:, 1:T], ALU.add))
        vst(lambda e: e.tensor_tensor(scr16, dtlam, woff, ALU.subtract))
        p.op('se', (lambda e: e.activation(LAM16[0:16, 0:T], LAM16[0:16, 0:T],
                                           AF.Ln)), waits=[('ve', p.cnt['ve'])])
        p.op('ve', (lambda e: e.tensor_tensor(sinb[0:16, 0:T], LAM16[0:16, 0:T],
                                              SCAL[0:16, 0:T], ALU.subtract)),
             waits=[('se', p.cnt['se']), ('ve', p.cnt['ve'])])
        p.op('ve', (lambda e: e.tensor_copy(SCAL[32:48, 0:T], sinb[0:16, 0:T])),
             waits=[('ve', p.cnt['ve'])])
        for ci in range(NT):
            c0 = ci * 512
            p.op('ve', (lambda e, c0=c0: e.tensor_tensor(
                bcnbf[0:64, 0:512], B_bf[0:64, c0:c0 + 512], C_bf[0:64, c0:c0 + 512],
                ALU.mult)), waits=[('ve', p.cnt['ve'])])
            p.op('te', (lambda e: e.matmul(p.bank(2)[0:1, 0:512],
                                           tt['ones_col_bf'][0:64, 0:1],
                                           bcnbf[0:64, 0:512], start=True, stop=True)),
                 waits=[('ve', p.cnt['ve'])] + (p.bank_waits(2) if ci == 0 else []))
            p.op('ve', (lambda e, c0=c0: e.tensor_copy(
                rowscr[0:1, c0:c0 + 512], p.bank(2)[0:1, 0:512])),
                 waits=[('te', p.cnt['te'])])
            p.bank_release(2, 've', p.cnt['ve'])
        for ci in range(NT):
            c0 = ci * 512
            p.op('te', (lambda e, c0=c0: e.matmul(
                p.bank(3)[0:16, 0:512], tt['ones_row_f32'][:, 0:16],
                rowscr[0:1, c0:c0 + 512], start=True, stop=True)),
                 waits=[('ve', p.cnt['ve'])] + (p.bank_waits(3) if ci == 0 else []))
            p.op('ve', (lambda e, c0=c0: e.tensor_tensor(
                bcsc[0:16, 0:512], scr16[:, c0:c0 + 512],
                p.bank(3)[0:16, 0:512], ALU.mult)),
                 waits=[('te', p.cnt['te']), ('ve', p.cnt['ve'])])
            p.op('ve', (lambda e: e.tensor_scalar(
                bcsc[0:16, 0:512], bcsc[0:16, 0:512],
                ctx.col('l0a_D')[0:16, 0:1], None, ALU.add)),
                 waits=[('ve', p.cnt['ve'])])
            p.op('ve', (lambda e, c0=c0: e.tensor_copy(
                SCAL[64:80, c0:c0 + 512], bcsc[0:16, 0:512])),
                 waits=[('ve', p.cnt['ve'])])
            p.bank_release(3, 've', p.cnt['ve'])
        p.dump('l0a_cumA', SCAL[0:16, 0:T])
        p.dump('l0a_LWMR', SCAL[32:48, 0:T])
        p.dump('l0a_G2', SCAL[64:80, 0:T])
        p.dump('l0a_B', B_bf[:])
        p.dump('l0a_C', C_bf[:])
        p.dump('l0a_dtlam', TSCAL[0:16, 0:T])
        p.barrier()
    with ExitStack() as ses:
        xs_q = sb(nc, ses, 'l0a_xsq', [128, 16 * 512], BF16)
        yq = sb(nc, ses, 'l0a_yq', [128, 16 * 512], BF16)
        Hst = sb(nc, ses, 'l0a_H', [64, 16 * 128], F32)
        BTq = sb(nc, ses, 'l0a_BTq', [128, 256], BF16)
        SCTq = sb(nc, ses, 'l0a_SCTq', [128, 512], F32)
        W1scr = sb(nc, ses, 'l0a_W1', [128, 128], F32)
        Pbf = sb(nc, ses, 'l0a_Pbf', [128, 128], BF16)
        xTbf = sb(nc, ses, 'l0a_xT', [128, 128], BF16)
        btwbf = sb(nc, ses, 'l0a_btw', [128, 64], BF16)
        hbf = sb(nc, ses, 'l0a_hbf', [64, 128], BF16)
        ctil = sb(nc, ses, 'l0a_ctil', [64, 128], BF16)
        combf = sb(nc, ses, 'l0a_comb', [128, 512], F32)
        yacc = sb(nc, ses, 'l0a_yacc', [128, 512], F32)
        zsil = sb(nc, ses, 'l0a_zsil', [128, 512], BF16)
        maskWb = [sb(nc, ses, f'l0a_mw{i}', [128, 128], F32) for i in range(2)]
        colscr2 = sb(nc, ses, 'l0a_colscr2', [128, 8], F32)
        g2row = sb(nc, ses, 'l0a_g2row', [1, 512], F32)
        rows4 = sb(nc, ses, 'l0a_rows4', [1, 1024], F32)
        aend_b = sb(nc, ses, 'l0a_aendb', [64, 256], F32)
        pre_b = sb(nc, ses, 'l0a_preb', [128, 256], F32)
        erow = sb(nc, ses, 'l0a_erow', [1, 128], F32)
        bcmask = sb(nc, ses, 'l0a_bcmask', [128, 128], F32)
        maskW0t = sb(nc, ses, 'l0a_maskW0', [128, 128], F32)
        p.dma('po', bcmask[:], aps['c_bcmask_lhsT'][:, :])
        p.dma('po', maskW0t[:], aps['c_maskW0'][:, :])
        mload_gq = p.cnt['gq']

        p.op('po', lambda e: e.memset(Hst[:], 0.0))
        for i in range(2):
            p.op('ve', (lambda e, i=i: e.tensor_copy(maskWb[i][:], maskW0t[:])),
                 waits=[('ve', p.cnt['ve']), ('gq', mload_gq)])
        h_memset = p.cnt['po']
        p.dma('po', rows4[0:1, 0:256], SCAL[0:16, 127::128],
              waits=[('ve', p.cnt['ve'])])
        rr_gq = p.cnt['gq']
        p.wait_only('ve', [('gq', rr_gq)])
        vst2 = lambda fn, w=(): p.op('ve', fn, waits=[('ve', p.cnt['ve'])] + list(w))
        vst2(lambda e: e.tensor_tensor(rows4[0:1, 257:512], rows4[0:1, 1:256],
                                       rows4[0:1, 0:255], ALU.subtract),
             w=[('gq', rr_gq)])
        vst2(lambda e: e.memset(rows4[0:1, 256:512:16], 0.0))
        p.op('se', (lambda e: e.activation(rows4[0:1, 256:512], rows4[0:1, 256:512],
                                           AF.Exp)), waits=[('ve', p.cnt['ve'])])
        vst2(lambda e: e.tensor_single_scalar(rows4[0:1, 512:768], rows4[0:1, 0:256],
                                              -1.0, ALU.mult),
             w=[('se', p.cnt['se'])])
        negrr = rows4[0:1, 512:768]
        p.op('te', (lambda e: e.matmul(p.bank(5)[0:64, 0:256],
                                       tt['ones_row_f32'][:, 0:64],
                                       rows4[0:1, 256:512], start=True, stop=True)),
             waits=[('ve', p.cnt['ve']), ('se', p.cnt['se'])] + p.bank_waits(5))
        p.op('ve', (lambda e: e.tensor_copy(aend_b[:], p.bank(5)[0:64, 0:256])),
             waits=[('te', p.cnt['te'])])
        p.bank_release(5, 've', p.cnt['ve'])
        p.op('te', (lambda e: e.matmul(p.bank(5)[0:128, 0:256],
                                       tt['ones_row_f32'][:],
                                       rows4[0:1, 0:256], start=True, stop=True)),
             waits=[('ve', p.cnt['ve'])] + p.bank_waits(5))
        p.op('ve', (lambda e: e.tensor_copy(pre_b[:], p.bank(5)[0:128, 0:256])),
             waits=[('te', p.cnt['te'])])
        p.bank_release(5, 've', p.cnt['ve'])

        mwi = [0]
        for tq in range(NT):
            tc0 = tq * 512
            for m in range(16):
                bi, tile, cast = sq_proj.get(m * 128, 128)
                b = 6 + m % 2
                w0 = [('po', cast)] + p.bank_waits(b)
                for k in range(8):
                    p.op('te', (lambda e, k=k, b=b, tile=tile, tc0=tc0:
                                e.matmul(p.bank(b)[0:128, 0:512], tile(k),
                                         x_bf[k][:, tc0:tc0 + 512],
                                         start=(k == 0), stop=(k == 7))),
                         waits=w0 if k == 0 else ())
                sq_proj.done(bi, p.cnt['te'])
                p.op('se', (lambda e, m=m, b=b: e.activation(
                    xs_q[:, m * 512:(m + 1) * 512], p.bank(b)[0:128, 0:512],
                    AF.Silu)), waits=[('te', p.cnt['te'])])
                p.bank_release(b, 'se', p.cnt['se'])
            xs_se = p.cnt['se']
            for crel in range(4):
                c0 = tc0 + crel * 128
                p.op('te', (lambda e, c0=c0, crel=crel: e.transpose(
                    p.bank_bf(4)[:, crel * 64:crel * 64 + 64],
                    B_bf[0:64, c0:c0 + 128], tt['ident_bf'][0:64, 0:64])),
                     waits=([('ve', p.cnt['ve'])] + p.bank_waits(4))
                     if crel == 0 else ())
            p.op('ve', (lambda e: e.tensor_copy(BTq[:], p.bank_bf(4)[:, 0:256])),
                 waits=[('te', p.cnt['te'])])
            p.bank_release(4, 've', p.cnt['ve'])
            for crel in range(4):
                c0 = tc0 + crel * 128
                p.op('te', (lambda e, c0=c0: e.transpose(
                    p.bank(5)[:, 0:128], SCAL[:, c0:c0 + 128], tt['ident_f32'][:])),
                     waits=[('ve', p.cnt['ve']), ('se', p.cnt['se'])] + p.bank_waits(5))
                p.op('ve', (lambda e, crel=crel: e.tensor_copy(
                    SCTq[:, crel * 128:(crel + 1) * 128], p.bank(5)[:, 0:128])),
                     waits=[('te', p.cnt['te'])])
                p.bank_release(5, 've', p.cnt['ve'])
            sct_ve = p.cnt['ve']
            for h in range(16):
                hc_base = h * 16 + tq * 4
                for crel in range(4):
                    c = tq * 4 + crel
                    cg0 = c * 128
                    hc = h * 16 + c
                    mi = mwi[0] % 2
                    mwi[0] += 1
                    p.dma('po', maskWb[mi][0:1, 0:128], SCAL[h:h + 1, cg0:cg0 + 128],
                          waits=[('te', p.cnt['te'])])
                    mw_gq = p.cnt['gq']
                    p.op('te', (lambda e, mi=mi: e.matmul(
                        p.bank(1)[0:128, 0:128], bcmask[:], maskWb[mi][:],
                        start=True, stop=True)),
                         waits=[('gq', mw_gq)] + p.bank_waits(1))
                    te_w = p.cnt['te']
                    p.op('te', (lambda e, cg0=cg0: e.matmul(
                        p.bank(2)[0:128, 0:128], B_bf[0:64, cg0:cg0 + 128],
                        C_bf[0:64, cg0:cg0 + 128], start=True, stop=True)),
                         waits=p.bank_waits(2))
                    te_cbt = p.cnt['te']
                    p.op('se', (lambda e, crel=crel, h=h: e.activation(
                        W1scr[:], p.bank(1)[0:128, 0:128], AF.Exp,
                        bias=SCTq[:, crel * 128 + 32 + h:crel * 128 + 33 + h])),
                         waits=[('te', te_w), ('ve', p.cnt['ve'])])
                    se_w1 = p.cnt['se']
                    if c > 0:
                        p.op('se', (lambda e, hc=hc: e.activation(
                            erow[:], p.bank(1)[0:1, 0:128], AF.Exp,
                            bias=negrr[0:1, hc - 1:hc])),
                             waits=[('se', p.cnt['se']), ('te', p.cnt['te'])])
                    se_er = p.cnt['se']
                    p.bank_release(1, 'se', se_er)
                    p.op('ve', (lambda e: e.tensor_tensor(Pbf[:], W1scr[:],
                                                          p.bank(2)[0:128, 0:128],
                                                          ALU.mult)),
                         waits=[('se', se_w1), ('te', te_cbt), ('ve', p.cnt['ve'])])
                    p.bank_release(2, 've', p.cnt['ve'])
                    p.op('te', (lambda e, h=h, crel=crel: e.transpose(
                        p.bank_bf(4)[:, 0:128],
                        xs_q[:, h * 512 + crel * 128:h * 512 + (crel + 1) * 128],
                        tt['ident_bf'][:])),
                         waits=[('se', xs_se)] + p.bank_waits(4))
                    p.op('ve', (lambda e: e.tensor_copy(xTbf[:], p.bank_bf(4)[:, 0:128])),
                         waits=[('te', p.cnt['te']), ('ve', p.cnt['ve'])])
                    xt_ve = p.cnt['ve']
                    p.bank_release(4, 've', xt_ve)
                    if c > 0:
                        p.op('ve', (lambda e, h=h: e.tensor_copy(
                            hbf[:], Hst[:, h * 128:(h + 1) * 128])),
                             waits=[('ve', p.cnt['ve']), ('te', p.cnt['te'])])
                        p.op('te', (lambda e: e.matmul(
                            p.bank(3)[0:64, 0:128], tt['ones_row_f32'][:, 0:64],
                            erow[0:1, 0:128], start=True, stop=True)),
                             waits=[('se', se_er)] + p.bank_waits(3))
                        p.op('ve', (lambda e, cg0=cg0: e.tensor_tensor(
                            ctil[:], C_bf[0:64, cg0:cg0 + 128], p.bank(3)[0:64, 0:128],
                            ALU.mult)), waits=[('te', p.cnt['te']), ('ve', p.cnt['ve'])])
                        p.bank_release(3, 've', p.cnt['ve'])
                    p.op('te', (lambda e: e.matmul(
                        p.bank(0)[0:128, 0:128], xTbf[:], Pbf[:],
                        start=True, stop=True)),
                         waits=[('ve', p.cnt['ve'])] + p.bank_waits(0))
                    if c > 0:
                        p.op('te', (lambda e: e.matmul(
                            p.bank(0)[0:128, 0:128], hbf[:], ctil[:],
                            start=False, stop=True, skip_group_check=True)))
                    te_y = p.cnt['te']
                    p.op('ve', (lambda e, crel=crel: e.tensor_copy(
                        yacc[:, crel * 128:(crel + 1) * 128], p.bank(0)[0:128, 0:128])),
                         waits=[('te', te_y)])
                    p.bank_release(0, 've', p.cnt['ve'])
                    p.op('se', (lambda e, crel=crel, h=h, hc=hc: e.activation(
                        colscr2[:, 0:1], SCTq[:, crel * 128 + 32 + h:crel * 128 + 33 + h],
                        AF.Exp, bias=pre_b[:, hc:hc + 1])),
                         waits=[('ve', p.cnt['ve']), ('se', p.cnt['se'])])
                    p.op('ve', (lambda e, crel=crel: e.tensor_scalar(
                        btwbf[:], BTq[:, crel * 64:(crel + 1) * 64],
                        colscr2[:, 0:1], None, ALU.mult)),
                         waits=[('se', p.cnt['se']), ('ve', p.cnt['ve']),
                                ('te', p.cnt['te'])])
                    p.op('te', (lambda e: e.matmul(
                        p.bank(3)[0:64, 0:128], btwbf[:], xTbf[:],
                        start=True, stop=True)),
                         waits=[('ve', p.cnt['ve'])] + p.bank_waits(3))
                    p.op('ve', (lambda e, h=h, hc=hc: e.scalar_tensor_tensor(
                        Hst[:, h * 128:(h + 1) * 128], Hst[:, h * 128:(h + 1) * 128],
                        aend_b[:, hc:hc + 1], p.bank(3)[0:64, 0:128],
                        ALU.mult, ALU.add)),
                         waits=[('te', p.cnt['te']), ('ve', p.cnt['ve']),
                                ('po', h_memset)])
                    p.bank_release(3, 've', p.cnt['ve'])
                bi, tile, cast = sq_proj.get(2048 + h * 128, 128)
                bz = 6 + h % 2
                w0 = [('po', cast)] + p.bank_waits(bz)
                for k in range(8):
                    p.op('te', (lambda e, k=k, bz=bz, tile=tile, tc0=tc0:
                                e.matmul(p.bank(bz)[0:128, 0:512], tile(k),
                                         x_bf[k][:, tc0:tc0 + 512],
                                         start=(k == 0), stop=(k == 7))),
                         waits=w0 if k == 0 else ())
                sq_proj.done(bi, p.cnt['te'])
                te_z = p.cnt['te']
                p.dma('po', g2row[0:1, 0:512],
                      SCAL[64 + h:65 + h, tc0:tc0 + 512],
                      waits=[('te', p.cnt['te'])])
                g2_gq = p.cnt['gq']
                p.op('te', (lambda e: e.matmul(
                    p.bank(5)[0:128, 0:512], tt['ones_row_f32'][:],
                    g2row[0:1, 0:512], start=True, stop=True)),
                     waits=[('gq', g2_gq)] + p.bank_waits(5))
                te_g2 = p.cnt['te']
                p.op('ve', (lambda e, h=h: e.tensor_tensor(
                    combf[:], xs_q[:, h * 512:(h + 1) * 512], p.bank(5)[:, 0:512],
                    ALU.mult)), waits=[('te', te_g2), ('ve', p.cnt['ve'])])
                p.bank_release(5, 've', p.cnt['ve'])
                p.op('ve', (lambda e: e.tensor_tensor(
                    combf[:], combf[:], yacc[:], ALU.add)),
                     waits=[('ve', p.cnt['ve'])])
                p.op('se', (lambda e, bz=bz: e.activation(
                    zsil[:], p.bank(bz)[0:128, 0:512], AF.Silu)),
                     waits=[('te', te_z), ('ve', p.cnt['ve'])])
                p.bank_release(bz, 'se', p.cnt['se'])
                p.op('ve', (lambda e, h=h: e.tensor_tensor(
                    yq[:, h * 512:(h + 1) * 512], combf[:], zsil[:], ALU.mult)),
                     waits=[('se', p.cnt['se']), ('ve', p.cnt['ve'])])
            yq_ve = p.cnt['ve']
            for fi in range(NF):
                bi, tile, cast = sq_out.get(fi * 128, 128)
                b = fi % 6
                w0 = [('po', cast), ('ve', yq_ve)] + p.bank_waits(b)
                for k in range(16):
                    p.op('te', (lambda e, k=k, b=b, tile=tile:
                                e.matmul(p.bank(b)[0:128, 0:512], tile(k),
                                         yq[:, k * 512:(k + 1) * 512],
                                         start=(k == 0), stop=(k == 15))),
                         waits=w0 if k == 0 else ())
                sq_out.done(bi, p.cnt['te'])
                p.op('ve', (lambda e, fi=fi, b=b, tc0=tc0: e.scalar_tensor_tensor(
                    x[fi][:, tc0:tc0 + 512], x[fi][:, tc0:tc0 + 512],
                    tt['fconsts'][:, 8:9], p.bank(b)[0:128, 0:512],
                    ALU.mult, ALU.add)),
                     waits=[('te', p.cnt['te']), ('ve', p.cnt['ve'])])
                p.bank_release(b, 've', p.cnt['ve'])
            if tq == 0:
                p.dump('l0a_yq0', yq[:])
                p.dump('l0a_H0', Hst[:])
                p.dump('l0a_x2', x[2][:])
            p.flush()
        p.barrier()
    emit_postln(ctx, es, 'l0a_lnout', x, x, eps=1e-5, dst_bf=x_bf,
                wait_src=[('ve', p.cnt['ve'])])


def build_block_program(block_names, const_arrays):
    nc = bass.Bass("TRN2", target_bir_lowering=False)
    aps = {}
    import ml_dtypes
    for name, arr in const_arrays.items():
        dt = BF16 if arr.dtype == ml_dtypes.bfloat16 else F32
        aps[name] = nc.dram_tensor(name, list(arr.shape), dt, kind="ExternalInput").ap()
    x_in = nc.dram_tensor("x_in", [D, T], F32, kind="ExternalInput").ap()
    x_out = nc.dram_tensor("x_out", [D, T], F32, kind="ExternalOutput").ap()
    es = ExitStack()
    with es:
        p = Prog(nc, es)
        ctx = Ctx(nc, p, aps)
        emit_load_consts(ctx, es)
        x = [sb(nc, es, f'x{i}', [128, T], F32) for i in range(NF)]
        x_bf = [sb(nc, es, f'xbf{i}', [128, T], BF16) for i in range(NF)]
        for f in range(NF):
            p.dma('po', x[f][:], x_in[f * 128:(f + 1) * 128, :])
        ld = p.cnt['gq']
        for f in range(NF):
            p.op('po', (lambda e, f=f: e.tensor_copy(x_bf[f][:], x[f][:])),
                 waits=[('gq', ld)])
        for e in ('te', 've', 'se'):
            p.wait_only(e, [('gq', ld)])
        for bn in block_names:
            with ExitStack() as bes:
                if bn == 'l1_attn':
                    emit_l1_attn(ctx, bes, x, x_bf)
                elif bn == 'l0_attn':
                    emit_l0_attn(ctx, bes, x, x_bf)
                elif bn == 'l0_ffn':
                    emit_l0_ffn(ctx, bes, x, x_bf)
                elif bn == 'l1_ffn':
                    emit_l1_ffn(ctx, bes, x, x_bf)
                else:
                    raise ValueError(bn)
                p.barrier(prods=('te', 've', 'se', 'po', 'sq', 'gq', 'cq',
                                 'w0', 'w1'))
                p.flush()
        for f in range(NF):
            p.dma('po', x_out[f * 128:(f + 1) * 128, :], x[f][:],
                  waits=[('ve', p.cnt['ve']), ('se', p.cnt['se']), ('po', p.cnt['po'])])
        p.wait_only('po', [('gq', p.cnt['gq'])])
        p.flush()
    return nc


BLOCK_SEQ = ['l0_attn', 'l0_ffn', 'l1_attn', 'l1_ffn']
FUSED = True


def kernel(**inputs):
    consts = prep_host(inputs)
    xs = [np.ascontiguousarray(inputs['L0_attn_u'][i].T) for i in range(8)]
    groups = [BLOCK_SEQ] if FUSED else [[bn] for bn in BLOCK_SEQ]
    for grp in groups:
        nc = build_block_program(grp, consts)
        in_maps = []
        for i in range(8):
            m = dict(consts)
            m['x_in'] = xs[i]
            in_maps.append(m)
        res = run_bass_kernel_spmd(nc, in_maps, core_ids=list(range(8)))
        xs = [np.asarray(res.results[i]['x_out']) for i in range(8)]
    return np.ascontiguousarray(np.stack([x.T for x in xs], 0))
